# version 07
# baseline (speedup 1.0000x reference)
import numpy as np
import concourse.bass as bass
import concourse.mybir as mybir
from concourse.bass_utils import run_bass_kernel_spmd

F32 = mybir.dt.float32
BF16 = mybir.dt.bfloat16
AF = mybir.ActivationFunctionType
ALU = mybir.AluOpType
AX = mybir.AxisListType

NCORES = 8
D = 2048
SEQ = 4096
TT = 128
NT = SEQ // TT
KC = 16
HALF = 1024
NH = 16
NE = 8
FE = 1024
FC = 8
GT = 512
NG = SEQ // GT
JT = GT // TT

LN_EPS = 1e-5
RMS_EPS = 1e-6
MOE_SCALE = 4.0

DEBUG_SERIAL = False

SB_RIGHT_END = 229344
ARENA_BYTES = 178 * 1024 + 256
ARENA_RESERVE = 16 * 1024


class Planner:
    ENGINES = ("sync", "gpsimd", "vector", "scalar", "tensor")

    def __init__(self, nc):
        self.nc = nc
        self.items = {e: [] for e in self.ENGINES}
        self.sems = {}
        self.cnt = {}

    def sem(self, name):
        self.sems[name] = self.nc.alloc_semaphore(name)
        self.cnt[name] = 0

    SERIAL_SEM = {"vector": "vE", "scalar": "sE", "gpsimd": "pE"}

    def step(self, engine, fn, wait=(), inc=(), _dma=False):
        wait = list(wait)
        inc = list(inc)
        own = self.SERIAL_SEM.get(engine)
        if own is not None and not _dma:
            wait.append((own, self.cnt[own]))
            if not any(s == own for s, _ in inc):
                inc.append((own, 1))
        wait = [(s, v) for (s, v) in wait if v > 0]
        if DEBUG_SERIAL:
            wait = [(s, self.cnt[s]) for s in self.cnt if self.cnt[s] > 0]
        self.items[engine].append((fn, tuple(wait), tuple(inc)))
        for s, a in inc:
            self.cnt[s] += a

    DMA_LANES = {"sync": 8, "scalar": 4, "gpsimd": 6}

    def init_lanes(self):
        self.lanes = {}
        self.lane_rr = {}
        for eng, n in self.DMA_LANES.items():
            pfx = {"sync": "dS", "scalar": "dA", "gpsimd": "dG"}[eng]
            self.lanes[eng] = [f"{pfx}{i}" for i in range(n)]
            for s in self.lanes[eng]:
                self.sem(s)
            self.lane_rr[eng] = 0

    def dma(self, engine, out, in_, wait=(), **kw):
        lane = self.lanes[engine][self.lane_rr[engine] % len(self.lanes[engine])]
        self.lane_rr[engine] += 1
        wait = list(wait) + [(lane, self.cnt[lane])]
        self.step(engine, lambda e: e.dma_start(out=out, in_=in_, **kw),
                  wait=wait, inc=[(lane, 16)], _dma=True)
        return (lane, self.cnt[lane])

    def all_lanes(self, engine):
        return [(s, self.cnt[s]) for s in self.lanes[engine]]

    def all_dma_events(self):
        out = []
        for eng in self.lanes:
            out += self.all_lanes(eng)
        return out

    def barrier(self):
        eng_sem = {"sync": "bS", "gpsimd": "bG", "vector": "bV",
                   "scalar": "bA", "tensor": "bT"}
        for e in self.ENGINES:
            w = self.all_lanes(e) if (e in self.lanes and e != "gpsimd") else []
            if e in self.SERIAL_SEM:
                w.append((self.SERIAL_SEM[e], self.cnt[self.SERIAL_SEM[e]]))
            self.step(e, lambda eng: eng.nop(), wait=w, inc=[(eng_sem[e], 1)],
                      _dma=True)
        targets = [(s, self.cnt[s]) for s in eng_sem.values()]
        for e in self.ENGINES:
            self.step(e, lambda eng: eng.nop(), wait=targets, _dma=True)

    def mark(self):
        return {e: len(self.items[e]) for e in self.ENGINES}

    def emit(self, block, upto=None):
        def make(engine):
            def run(eng):
                last = {}
                items = self.items[engine]
                if upto is not None:
                    items = items[:upto[engine]]
                for fn, waits, incs in items:
                    for s, v in waits:
                        if last.get(s, 0) >= v:
                            continue
                        eng.wait_ge(self.sems[s], v)
                        last[s] = v
                    inst = fn(eng)
                    for s, a in incs:
                        inst = inst.then_inc(self.sems[s], a)
            return run
        block.sync(make("sync"))
        block.gpsimd(make("gpsimd"))
        block.vector(make("vector"))
        block.scalar(make("scalar"))
        block.tensor(make("tensor"))


class Arena:
    def __init__(self, nc, base, size):
        self.nc, self.base, self.off, self.n = nc, base, base, 0
        self.limit = base + size

    def alloc(self, name, shape, dtype):
        esz = 2 if dtype == BF16 else 4
        nbytes = shape[1] * esz
        off = (self.off + 127) // 128 * 128
        assert off + nbytes <= self.limit, (
            f"arena overflow: {name} needs {nbytes}B at +{off - self.base}")
        self.n += 1
        t = self.nc.alloc_sbuf_tensor_at(f"{name}_{self.n}", shape, dtype, offset=off)
        self.off = off + nbytes
        return t.ap()

    def reset(self):
        self.off = self.base


def build(upto=None, dbg=False):
    nc = bass.Bass("TRN2", target_bir_lowering=False, debug=False)

    x_d = nc.dram_tensor("x", [SEQ, D], F32, kind="ExternalInput").ap()
    wqk_d = nc.dram_tensor("wqk", [D, D], F32, kind="ExternalInput").ap()
    bqk_d = nc.dram_tensor("bqk", [D], F32, kind="ExternalInput").ap()
    wv_d = nc.dram_tensor("wv", [D, D], F32, kind="ExternalInput").ap()
    bv_d = nc.dram_tensor("bv", [D], F32, kind="ExternalInput").ap()
    wu_d = nc.dram_tensor("wu", [D, D], F32, kind="ExternalInput").ap()
    bu_d = nc.dram_tensor("bu", [D], F32, kind="ExternalInput").ap()
    wo_d = nc.dram_tensor("wo", [D, D], F32, kind="ExternalInput").ap()
    bo_d = nc.dram_tensor("bo", [D], F32, kind="ExternalInput").ap()
    lng_d = nc.dram_tensor("lng", [D], F32, kind="ExternalInput").ap()
    lnb_d = nc.dram_tensor("lnb", [D], F32, kind="ExternalInput").ap()
    rw_d = nc.dram_tensor("rw", [NE, D], F32, kind="ExternalInput").ap()
    nw_d = nc.dram_tensor("nw", [D], F32, kind="ExternalInput").ap()
    gw_d = nc.dram_tensor("gw", [NE, D, FE], F32, kind="ExternalInput").ap()
    uw_d = nc.dram_tensor("uw", [NE, D, FE], F32, kind="ExternalInput").ap()
    dw_d = nc.dram_tensor("dw", [NE, FE, D], F32, kind="ExternalInput").ap()
    out_d = nc.dram_tensor("out", [SEQ, D], F32, kind="ExternalOutput").ap()

    knd = "ExternalOutput" if dbg else "Internal"
    xnt_s = nc.dram_tensor("xnt_s", [KC, 128, SEQ], BF16, kind=knd).ap()
    wh1_s = nc.dram_tensor("wh1_s", [3, KC, 128, HALF], BF16, kind="Internal").ap()
    qt_s = nc.dram_tensor("qt_s", [NT, 2, 128, HALF], BF16, kind=knd).ap()
    u_s = nc.dram_tensor("u_s", [NT, 2, 128, HALF], BF16, kind=knd).ap()
    y_s = nc.dram_tensor("y_s", [SEQ, D], F32, kind=knd).ap()
    if dbg:
        dbg_cw = nc.dram_tensor("dbg_cw", [128, NT * 8], F32, kind="ExternalOutput").ap()
        dbg_kv = nc.dram_tensor("dbg_kv", [128, D], BF16, kind="ExternalOutput").ap()
        dbg_gr = nc.dram_tensor("dbg_gr", [128, 2 * D], F32, kind="ExternalOutput").ap()
    gbf = nc.dram_tensor("gbf", [NE, KC, 128, FE], BF16, kind="Internal").ap()
    ubf = nc.dram_tensor("ubf", [NE, KC, 128, FE], BF16, kind="Internal").ap()
    dbf = nc.dram_tensor("dbf", [NE, FC, 128, D], BF16, kind="Internal").ap()
    wobf = nc.dram_tensor("wobf", [KC, 128, D], BF16, kind="Internal").ap()

    def sb(name, shape, dtype):
        return nc.alloc_sbuf_tensor(name, shape, dtype).ap()

    id_bf = sb("id_bf", [128, 128], BF16)
    id_f32 = sb("id_f32", [128, 128], F32)
    iot = sb("iot", [128, 128], F32)
    ones_bf = sb("ones_bf", [1, 128], BF16)
    ones_f32 = sb("ones_f32", [1, 128], F32)
    bqk_r = sb("bqk_r", [1, D], BF16)
    bv_r = sb("bv_r", [1, D], BF16)
    bu_r = sb("bu_r", [1, D], BF16)
    bo_r = sb("bo_r", [1, D], BF16)
    nwT = sb("nwT", [128, 16], F32)
    g_rep = nc.alloc_sbuf_tensor_at(
        "g_rep", [128, D], F32, offset=SB_RIGHT_END - ARENA_RESERVE).ap()
    b_rep = nc.alloc_sbuf_tensor_at(
        "b_rep", [128, D], F32, offset=SB_RIGHT_END - ARENA_RESERVE + D * 4).ap()
    rwnT = sb("rwnT", [128, KC * 8], BF16)
    cw_res = sb("cw_res", [128, NT * 8], F32)
    ceps = sb("ceps", [128, 3], F32)
    kv_bf = sb("kv_bf", [128, D], BF16)
    lgf = sb("lgf", [128, 8], F32)
    sml = sb("sml", [128, 16], F32)
    cwt = sb("cwt", [128, 8], F32)

    PS = nc.alloc_psum_tensor("PS", [128, 4096], F32).ap()
    PS7_BF = PS[:, 3584:4096].bitcast(BF16)
    PS6_BF = PS[:, 3072:3584].bitcast(BF16)

    arena = Arena(nc, SB_RIGHT_END - ARENA_BYTES, ARENA_BYTES - ARENA_RESERVE)
    wq_sb = arena.alloc("wq", [128, KC * HALF], BF16)
    wv_sb = arena.alloc("wvs", [128, KC * HALF], BF16)
    wu_sb = arena.alloc("wus", [128, KC * HALF], BF16)

    P = Planner(nc)
    for s in ("tE", "vE", "sE", "pE", "bS", "bG", "bV", "bA", "bT"):
        P.sem(s)
    P.init_lanes()
    C = P.cnt
    ev = {}

    W_END = arena.off
    lng_row = arena.alloc("lng_row", [1, D], F32)
    lnb_row = arena.alloc("lnb_row", [1, D], F32)
    rw_sb = arena.alloc("rw_sb", [8, D], F32)
    nw16 = arena.alloc("nw16", [16, 128], F32)
    w_ready = {0: [
        P.dma("gpsimd", wsb.rearrange("p (c n) -> p c n", c=KC),
              wd[:, 0:HALF].rearrange("(c p) n -> p c n", p=128))
        for wd, wsb in ((wqk_d, wq_sb), (wv_d, wv_sb), (wu_d, wu_sb))]}
    conv_done = {}
    conv_done["wh1"] = [
        P.dma("gpsimd", wh1_s[i],
              wd[:, HALF:D].rearrange("(c p) n -> c p n", p=128))
        for i, wd in enumerate((wqk_d, wv_d, wu_d))]

    d_setup = [
        P.dma("sync", lng_row, lng_d.rearrange("(o d) -> o d", o=1)),
        P.dma("sync", lnb_row, lnb_d.rearrange("(o d) -> o d", o=1)),
        P.dma("sync", rw_sb, rw_d),
        P.dma("sync", nw16, nw_d.rearrange("(c p) -> c p", p=128)),
    ]
    b_rows = [
        P.dma("gpsimd", bqk_r, bqk_d.rearrange("(o d) -> o d", o=1)),
        P.dma("gpsimd", bv_r, bv_d.rearrange("(o d) -> o d", o=1)),
        P.dma("gpsimd", bu_r, bu_d.rearrange("(o d) -> o d", o=1)),
        P.dma("gpsimd", bo_r, bo_d.rearrange("(o d) -> o d", o=1)),
    ]
    P.step("gpsimd", lambda e: e.memset(ceps[:, 0:1], LN_EPS))
    P.step("gpsimd", lambda e: e.memset(ceps[:, 1:2], RMS_EPS))
    P.step("gpsimd", lambda e: e.memset(ceps[:, 2:3], LN_EPS * RMS_EPS))
    P.step("gpsimd", lambda e: e.memset(ones_bf, 1.0))
    P.step("gpsimd", lambda e: e.memset(ones_f32, 1.0))
    P.step("gpsimd", lambda e: e.iota(iot, [[1, 128]], channel_multiplier=-1,
                                      allow_small_or_imprecise_dtypes=True),
           inc=[("pE", 1)])
    pe_iota = C["pE"]
    P.step("vector", lambda e: e.tensor_single_scalar(id_bf, iot, 0.0, ALU.is_equal),
           wait=[("pE", pe_iota)])
    P.step("vector", lambda e: e.tensor_single_scalar(id_f32, iot, 0.0, ALU.is_equal),
           inc=[("vE", 1)])

    for ci in range(4):
        P.step("tensor",
               lambda e, ci=ci: (
                   e.matmul(PS[:, 0:512], ones_f32,
                            lng_row[:, ci * 512:(ci + 1) * 512], start=True, stop=True),
                   e.matmul(PS[:, 512:1024], ones_f32,
                            lnb_row[:, ci * 512:(ci + 1) * 512], start=True, stop=True))[-1],
               wait=d_setup + [("vE", C["vE"])], inc=[("tE", 1)])
        t_rep = C["tE"]
        P.step("vector", lambda e, ci=ci: e.tensor_copy(
            g_rep[:, ci * 512:(ci + 1) * 512], PS[:, 0:512]), wait=[("tE", t_rep)])
        P.step("vector", lambda e, ci=ci: e.tensor_copy(
            b_rep[:, ci * 512:(ci + 1) * 512], PS[:, 512:1024]),
            wait=[("tE", t_rep)], inc=[("vE", 1)])

    P.step("tensor", lambda e: e.transpose(PS[:, 1024:1040], nw16, id_f32[0:16, 0:16]),
           wait=d_setup + [("vE", C["vE"])], inc=[("tE", 1)])
    t_nw = C["tE"]
    P.step("vector", lambda e: e.tensor_copy(nwT, PS[:, 1024:1040]),
           wait=[("tE", t_nw)], inc=[("vE", 1)])
    v_nwT = C["vE"]
    for c in range(KC):
        P.step("tensor", lambda e, c=c: e.transpose(
            PS[:, 1536 + (c % 2) * 512: 1544 + (c % 2) * 512],
            rw_sb[0:8, c * 128:(c + 1) * 128], id_f32[0:8, 0:8]),
            wait=[("vE", C["vE"])], inc=[("tE", 1)])
        t_rw = C["tE"]
        P.step("vector", lambda e, c=c: e.tensor_scalar_mul(
            rwnT[:, c * 8:(c + 1) * 8],
            PS[:, 1536 + (c % 2) * 512: 1544 + (c % 2) * 512], nwT[:, c:c + 1]),
            wait=[("tE", t_rw), ("vE", v_nwT)], inc=[("vE", 1)])

    P.barrier()
    marks = {"setup": P.mark()}
    ARENA_BASE0 = arena.base
    arena.base = arena.off = ((W_END + 127) // 128 * 128)


    def convert_expert(e, wait=()):
        e1 = P.dma("gpsimd", gbf[e], gw_d[e].rearrange("(c p) f -> c p f", p=128),
                   wait=wait)
        e2 = P.dma("gpsimd", ubf[e], uw_d[e].rearrange("(c p) f -> c p f", p=128))
        e3 = P.dma("gpsimd", dbf[e], dw_d[e].rearrange("(c p) d -> c p d", p=128))
        conv_done[e] = [e1, e2, e3]

    x_buf = [arena.alloc(f"x{i}", [128, D], F32) for i in range(2)]
    xn_bf = [arena.alloc(f"xnbf{i}", [128, D], BF16) for i in range(2)]
    xnt_b = [arena.alloc(f"xnt{i}", [128, D], BF16) for i in range(2)]
    q_bf = [arena.alloc(f"qbf{i}", [128, HALF], BF16) for i in range(2)]
    v_bf = arena.alloc("vbf", [128, HALF], BF16)
    u_bf = [arena.alloc(f"ubfb{i}", [128, HALF], BF16) for i in range(2)]
    qt_b = [arena.alloc(f"qtb{i}", [128, HALF], BF16) for i in range(2)]
    el_m = arena.alloc("elm", [128, HALF], F32)
    el_e = arena.alloc("ele", [128, HALF], F32)
    el_r = arena.alloc("elr", [128, HALF], F32)
    kv_acc = arena.alloc("kvacc", [128, D], F32)

    q_ps = PS[:, 0:1024]
    v_ps = PS[:, 1024:2048]
    u_ps = PS[:, 2048:3072]
    log_ps = PS[:, 3072:3080]
    QPB = [PS[:, 1024:1536].bitcast(BF16), PS[:, 1536:2048].bitcast(BF16)]

    def load_w_half(h):
        evs = []
        if True:
            w = [("tE", ev[("mms", 0, NT - 1)])] + list(conv_done["wh1"])
            for i, wsb in enumerate((wq_sb, wv_sb, wu_sb)):
                evs.append(P.dma(
                    "sync", wsb.rearrange("p (c n) -> p c n", c=KC),
                    wh1_s[i].rearrange("c p n -> p c n"), wait=w))
        return evs


    for _e in range(3):
        convert_expert(_e)

    def emit_ln(t):
        bi = t % 2
        w = []
        if t >= 2:
            w = [("vE", ev[("xnbf", t - 2)]), ("sE", ev[("xsq", t - 2)])]
        ev[("x", t)] = P.dma("sync", x_buf[bi],
                             x_d[t * TT:(t + 1) * TT, :], wait=w)
        idw = [ev[("x", t)]]
        if t >= 2:
            idw.append(("tE", ev[("xpg", t - 2, 3)]))
        P.step("scalar", lambda e, bi=bi: e.activation(
            xn_bf[bi], x_buf[bi], AF.Identity, accum_out=sml[:, 0:1]),
            wait=idw)
        P.step("scalar", lambda e, bi=bi: e.activation(
            xn_bf[bi], x_buf[bi], AF.Square, accum_out=sml[:, 1:2]))
        ev[("xsq", t)] = C["sE"]

        lsw = [("sE", ev[("xsq", t)])]
        if ("lgfe", t - 1) in ev:
            lsw.append(("vE", ev[("lgfe", t - 1)]))
        P.step("scalar", lambda e: e.mul(sml[:, 2:3], sml[:, 0:1], 1.0 / D),
               wait=lsw)
        P.step("scalar", lambda e: e.mul(sml[:, 3:4], sml[:, 1:2], 1.0 / D))
        P.step("scalar", lambda e: e.activation(sml[:, 8:9], sml[:, 2:3], AF.Square))
        P.step("scalar", lambda e: e.activation(
            sml[:, 4:5], sml[:, 8:9], AF.Relu, scale=-1.0, bias=sml[:, 3:4]))
        P.step("scalar", lambda e: e.mul(sml[:, 6:7], sml[:, 0:1], -1.0 / D))
        P.step("scalar", lambda e: e.activation(
            sml[:, 7:8], sml[:, 4:5], AF.Ln, bias=ceps[:, 0:1]))
        P.step("scalar", lambda e: e.activation(
            sml[:, 5:6], sml[:, 7:8], AF.Exp, scale=-0.5))
        P.step("scalar", lambda e: e.mul(sml[:, 9:10], sml[:, 7:8], 0.5))
        P.step("scalar", lambda e: e.activation(
            sml[:, 10:11], sml[:, 4:5], AF.Ln, scale=1.0 + RMS_EPS,
            bias=ceps[:, 2:3]))
        P.step("scalar", lambda e: e.activation(
            sml[:, 11:12], sml[:, 10:11], AF.Exp, scale=-0.5, bias=sml[:, 9:10]))
        ev[("lnexp", t)] = C["sE"]
        P.step("vector", lambda e, bi=bi: e.tensor_scalar(
            xn_bf[bi], x_buf[bi], sml[:, 6:7], sml[:, 5:6], ALU.add, ALU.mult),
            wait=[("sE", ev[("lnexp", t)])], inc=[("vE", 1)])
        ev[("xnbf", t)] = C["vE"]

    def emit_xpose(tn):
        bj = tn % 2
        for g4 in range(4):
            xw = [("vE", ev[("xnbf", tn)])]
            if ("xpfree",) in ev:
                xw.append(("vE", ev[("xpfree",)]))

            def xpose(e, g4=g4, bj=bj):
                last = None
                for k in range(4):
                    c = g4 * 4 + k
                    last = e.transpose(
                        PS7_BF[:, (g4 % 2) * 512 + k * 128:
                               (g4 % 2) * 512 + (k + 1) * 128],
                        xn_bf[bj][:, c * 128:(c + 1) * 128], id_bf)
                return last
            P.step("tensor", xpose, wait=xw, inc=[("tE", 1)])
            ev[("xpg", tn, g4)] = C["tE"]
            xew = [("tE", ev[("xpg", tn, g4)])]
            if g4 == 0 and tn >= 2:
                xew += [("tE", ev[("mms", 0, tn - 2)]),
                        ev[("xnts", tn - 2)]]
            P.step("vector", lambda e, g4=g4, bj=bj: e.tensor_copy(
                xnt_b[bj][:, g4 * 512:(g4 + 1) * 512],
                PS7_BF[:, (g4 % 2) * 512:(g4 % 2) * 512 + 512]),
                wait=xew, inc=[("vE", 1)])
            ev[("xpfree",)] = C["vE"]
        ev[("xnt", tn)] = C["vE"]
        ev[("xnts", tn)] = P.dma(
            "sync",
            xnt_s[:, :, tn * TT:(tn + 1) * TT].rearrange("c p e -> p c e"),
            xnt_b[bj].rearrange("p (c e) -> p c e", c=KC),
            wait=[("vE", ev[("xnt", tn)])])

    for h in range(2):
        if h == 1:
            w_ready[1] = load_w_half(1)
            for e_i in range(3, NE):
                convert_expert(e_i)
            conv_done["wo"] = P.dma(
                "gpsimd", wobf, wo_d.rearrange("(c p) d -> c p d", p=128))

        if h == 0:
            emit_ln(0)
            emit_xpose(0)
        for t in range(NT):
            bi = t % 2
            if h == 1:
                if t >= 2:
                    w = [("tE", ev[("mms", 1, t - 2)])]
                else:
                    w = [("tE", ev[("mms", 0, NT - 1)]),
                         ev[("xnts", NT - 2 + t)]]
                ev[("xntl", t)] = P.dma(
                    "sync", xnt_b[bi].rearrange("p (c e) -> p c e", c=KC),
                    xnt_s[:, :, t * TT:(t + 1) * TT].rearrange("c p e -> p c e"),
                    wait=w)

            mmw = list(w_ready[h])
            if (h, t) == (0, 0):
                mmw += b_rows
            if h == 0:
                mmw.append(("vE", ev[("xnt", t)]))
            else:
                mmw.append(ev[("xntl", t)])
            if (h, t) != (0, 0):
                ph, pt = (h, t - 1) if t > 0 else (0, NT - 1)
                mmw += [("vE", ev[("elu_sub", ph, pt)]),
                        ("vE", ev[("qt", ph, pt)])]
                if ph == 0:
                    mmw.append(("vE", ev[("lgfe", pt)]))

            def wave_a(e, h=h, bi=bi):
                for ps_t, br in ((q_ps, bqk_r), (v_ps, bv_r)):
                    for s2 in range(2):
                        e.matmul(ps_t[:, s2 * 512:(s2 + 1) * 512], ones_bf,
                                 br[:, h * HALF + s2 * 512: h * HALF + (s2 + 1) * 512],
                                 start=True, stop=False)
                last = None
                for c in range(KC):
                    lhs = xnt_b[bi][:, c * 128:(c + 1) * 128]
                    for ps_t, wsb in ((q_ps, wq_sb), (v_ps, wv_sb)):
                        for s2 in range(2):
                            last = e.matmul(
                                ps_t[:, s2 * 512:(s2 + 1) * 512], lhs,
                                wsb[:, c * HALF + s2 * 512: c * HALF + (s2 + 1) * 512],
                                start=False, stop=(c == KC - 1))
                    if h == 0:
                        last = e.matmul(log_ps, lhs, rwnT[:, c * 8:(c + 1) * 8],
                                        start=(c == 0), stop=(c == KC - 1))
                return last
            P.step("tensor", wave_a, wait=mmw, inc=[("tE", 1)])
            ev[("mmsA", h, t)] = C["tE"]

            mbw = []
            if (h, t) != (0, 0):
                ph, pt = (h, t - 1) if t > 0 else (0, NT - 1)
                mbw.append(("vE", ev[("kv_evac", ph, pt)]))

            def wave_b(e, h=h, bi=bi):
                for s2 in range(2):
                    e.matmul(u_ps[:, s2 * 512:(s2 + 1) * 512], ones_bf,
                             bu_r[:, h * HALF + s2 * 512: h * HALF + (s2 + 1) * 512],
                             start=True, stop=False)
                last = None
                for c in range(KC):
                    lhs = xnt_b[bi][:, c * 128:(c + 1) * 128]
                    for s2 in range(2):
                        last = e.matmul(
                            u_ps[:, s2 * 512:(s2 + 1) * 512], lhs,
                            wu_sb[:, c * HALF + s2 * 512: c * HALF + (s2 + 1) * 512],
                            start=False, stop=(c == KC - 1))
                return last
            P.step("tensor", wave_b, wait=mbw, inc=[("tE", 1)])
            ev[("mms", h, t)] = C["tE"]

            P.step("vector", lambda e: e.tensor_scalar_min(el_m, q_ps, 0.0),
                   wait=[("tE", ev[("mmsA", h, t)])], inc=[("vE", 1)])
            ev[("elu_m", h, t)] = C["vE"]
            P.step("vector", lambda e: e.tensor_sub(el_r, q_ps, el_m),
                   inc=[("vE", 1)])
            ev[("elu_sub", h, t)] = C["vE"]
            P.step("scalar", lambda e: e.activation(el_e, el_m, AF.Exp),
                   wait=[("vE", ev[("elu_m", h, t)])], inc=[("sE", 1)])
            ev[("elu_e", h, t)] = C["sE"]
            P.step("vector", lambda e, bi=bi: e.scalar_tensor_tensor(
                q_bf[bi], el_e, -1.0, el_r, ALU.add, ALU.add),
                wait=[("sE", ev[("elu_e", h, t)])], inc=[("vE", 1)])
            ev[("elu", h, t)] = C["vE"]

            silw = [("tE", ev[("mmsA", h, t)])]
            if (h, t) != (0, 0):
                ph, pt = (h, t - 1) if t > 0 else (0, NT - 1)
                silw.append(("tE", ev[("kv", ph, pt)]))
            P.step("scalar", lambda e: e.activation(v_bf, v_ps, AF.Silu), wait=silw)
            ev[("silu_v", h, t)] = C["sE"]
            suw = [("tE", ev[("mms", h, t)])]
            if t >= 2:
                suw.append(ev[("us", h, t - 2)])
            elif h == 1:
                suw.append(ev[("us", 0, NT - 2 + t)])
            P.step("scalar", lambda e, bi=bi: e.activation(u_bf[bi], u_ps, AF.Silu),
                   wait=suw)
            ev[("silu", h, t)] = C["sE"]
            ev[("us", h, t)] = P.dma("sync", u_s[t, h], u_bf[bi],
                                     wait=[("sE", ev[("silu", h, t)])])

            def kv_mms(e, bi=bi):
                last = None
                for i in range(8):
                    last = e.matmul(u_ps[:, i * 128:(i + 1) * 128],
                                    q_bf[bi][:, i * 128:(i + 1) * 128],
                                    v_bf[:, i * 128:(i + 1) * 128],
                                    start=True, stop=True)
                return last
            P.step("tensor", kv_mms,
                   wait=[("vE", ev[("elu", h, t)]), ("sE", ev[("silu", h, t)])],
                   inc=[("tE", 1)])
            ev[("kv", h, t)] = C["tE"]
            if t == 0:
                P.step("vector", lambda e, h=h: e.tensor_copy(
                    kv_acc[:, h * HALF:(h + 1) * HALF], u_ps),
                    wait=[("tE", ev[("kv", h, t)])], inc=[("vE", 1)])
            else:
                P.step("vector", lambda e, h=h: e.tensor_add(
                    kv_acc[:, h * HALF:(h + 1) * HALF], u_ps,
                    kv_acc[:, h * HALF:(h + 1) * HALF]),
                    wait=[("tE", ev[("kv", h, t)])], inc=[("vE", 1)])
            ev[("kv_evac", h, t)] = C["vE"]

            for g4 in range(2):
                qxw = [("vE", ev[("elu", h, t)]), ("sE", ev[("silu_v", h, t)])]
                if ("qpe", g4) in ev:
                    qxw.append(("vE", ev[("qpe", g4)]))

                def qpose(e, g4=g4, bi=bi):
                    last = None
                    for k in range(4):
                        c = g4 * 4 + k
                        last = e.transpose(
                            QPB[g4][:, k * 128:(k + 1) * 128],
                            q_bf[bi][:, c * 128:(c + 1) * 128], id_bf)
                    return last
                P.step("tensor", qpose, wait=qxw, inc=[("tE", 1)])
                ev[("qpg", h, t, g4)] = C["tE"]
                qsw = [("tE", ev[("qpg", h, t, g4)])]
                if g4 == 0 and t >= 2:
                    qsw.append(ev[("qts", h, t - 2)])
                elif g4 == 0 and h == 1:
                    qsw.append(ev[("qts", 0, NT - 2 + t)])
                P.step("vector", lambda e, g4=g4, bi=bi: e.tensor_copy(
                    qt_b[bi][:, g4 * 512:(g4 + 1) * 512],
                    QPB[g4][:, 0:512]),
                    wait=qsw, inc=[("vE", 1)])
                ev[("qpe", g4)] = C["vE"]
            ev[("qt", h, t)] = C["vE"]
            ev[("qts", h, t)] = P.dma("sync", qt_s[t, h], qt_b[bi],
                                      wait=[("vE", ev[("qt", h, t)])])

            if h == 0:
                P.step("vector", lambda e: e.tensor_scalar_mul(
                    lgf, log_ps, sml[:, 11:12]),
                    wait=[("tE", ev[("mmsA", 0, t)]), ("sE", ev[("lnexp", t)])],
                    inc=[("vE", 1)])
                ev[("lgfe", t)] = C["vE"]

            if h == 0 and t + 1 < NT:
                emit_ln(t + 1)
                emit_xpose(t + 1)

            if h == 0:
                P.step("vector", lambda e: e.reduce_max(
                    sml[:, 12:13], lgf, axis=AX.X))
                P.step("vector", lambda e: e.reduce_max(
                    sml[:, 13:14], lgf, axis=AX.X, negate=True))
                P.step("vector", lambda e: e.tensor_single_scalar(
                    cwt, lgf, sml[:, 12:13], ALU.is_ge))
                P.step("vector", lambda e: e.scalar_tensor_tensor(
                    cwt, cwt, 1e30, lgf, ALU.mult, ALU.subtract))
                P.step("vector", lambda e: e.tensor_reduce(
                    sml[:, 14:15], cwt, axis=AX.X, op=ALU.min, negate=True))
                ev[("cwv1", t)] = C["vE"]

                P.step("scalar", lambda e: e.activation(
                    cwt, lgf, AF.Exp, bias=sml[:, 13:14]),
                    wait=[("vE", ev[("cwv1", t)])])
                P.step("scalar", lambda e: e.activation(
                    sml[:, 15:16], sml[:, 14:15], AF.Exp, bias=sml[:, 13:14]))
                ev[("cws", t)] = C["sE"]

                P.step("vector", lambda e: e.tensor_scalar_add(
                    sml[:, 15:16], sml[:, 15:16], 1.0),
                    wait=[("sE", ev[("cws", t)])])
                P.step("vector", lambda e: e.reciprocal(
                    sml[:, 15:16], sml[:, 15:16]))
                P.step("vector", lambda e: e.tensor_scalar_mul(
                    sml[:, 15:16], sml[:, 15:16], MOE_SCALE))
                P.step("vector", lambda e: e.tensor_single_scalar(
                    lgf, lgf, sml[:, 14:15], ALU.is_ge))
                P.step("vector", lambda e: e.tensor_mul(cwt, cwt, lgf))
                P.step("vector", lambda e, t=t: e.tensor_scalar_mul(
                    cw_res[:, t * 8:(t + 1) * 8], cwt, sml[:, 15:16]))
                ev[("cw", t)] = C["vE"]

    P.step("vector", lambda e: e.tensor_copy(kv_bf, kv_acc),
           wait=[("vE", ev[("kv_evac", 1, NT - 1)])], inc=[("vE", 1)])

    P.barrier()
    marks["p1"] = P.mark()
    arena.base = ARENA_BASE0
    arena.reset()

    xng = arena.alloc("xng", [128, KC * GT], BF16)
    y_g = arena.alloc("yg", [128, JT * D], F32)
    gate_b = [arena.alloc(f"gateb{i}", [128, KC * 256], BF16) for i in range(2)]
    up_b = [arena.alloc(f"upb{i}", [128, KC * 256], BF16) for i in range(2)]
    down_b = [arena.alloc(f"downb{i}", [128, FC * D], BF16) for i in range(2)]
    at_b = arena.alloc("atb", [128, FC * GT], BF16)
    sg_b = arena.alloc("sgb", [128, GT], F32)

    gps = [PS[:, 0:512], PS[:, 512:1024]]
    ups = [PS[:, 1024:1536], PS[:, 1536:2048]]
    yps = [PS[:, 2048:2560], PS[:, 2560:3072]]

    def load_gu_quarter(g, e_i, qq, wait):
        qbi = qq % 2
        qw = list(conv_done[e_i]) + list(wait)
        ev[("gql", g, e_i, qq)] = P.dma(
            "sync", gate_b[qbi].rearrange("p (c f) -> p c f", c=KC),
            gbf[e_i][:, :, qq * 256:(qq + 1) * 256].rearrange("c p f -> p c f"),
            wait=qw)
        ev[("uql", g, e_i, qq)] = P.dma(
            "sync", up_b[qbi].rearrange("p (c f) -> p c f", c=KC),
            ubf[e_i][:, :, qq * 256:(qq + 1) * 256].rearrange("c p f -> p c f"),
            wait=qw)

    load_gu_quarter(0, 0, 0, [])
    load_gu_quarter(0, 0, 1, [])

    for g in range(NG):
        xw = [("tE", ev[("ffn_mm", g - 1)])] if g > 0 else []
        ev[("xng", g)] = P.dma(
            "sync", xng.rearrange("p (c e) -> p c e", c=KC),
            xnt_s[:, :, g * GT:(g + 1) * GT].rearrange("c p e -> p c e"), wait=xw)

        for e_i in range(NE):
            dbi = e_i % 2
            dw_w = list(conv_done[e_i])
            gi = g * NE + e_i
            if gi >= 2:
                pg, pe = divmod(gi - 2, NE)
                dw_w.append(("tE", ev[("down_mm", pg, pe)]))
            ev[("downl", g, e_i)] = P.dma(
                "sync", down_b[dbi].rearrange("p (c d) -> p c d", c=FC),
                dbf[e_i].rearrange("c p d -> p c d"), wait=dw_w)

            for fc in range(FC):
                qq = fc // 2
                qbi = qq % 2
                fo = (fc % 2) * 128
                pp = fc % 2
                fi = gi * FC + fc
                mmw = [ev[("gql", g, e_i, qq)], ev[("uql", g, e_i, qq)],
                       ev[("xng", g)]]
                if fi >= 2:
                    pg2, pe2f = divmod(fi - 2, FC * NE)
                    pe2, pfc = divmod(pe2f, FC)
                    mmw += [("sE", ev[("silu2", pg2, pe2, pfc)]),
                            ("vE", ev[("at", pg2, pe2, pfc)])]

                def gu_mms(e, qbi=qbi, fo=fo, pp=pp):
                    last = None
                    for c in range(KC):
                        last = e.matmul(gps[pp],
                                        gate_b[qbi][:, c * 256 + fo: c * 256 + fo + 128],
                                        xng[:, c * GT:(c + 1) * GT],
                                        start=(c == 0), stop=(c == KC - 1))
                    for c in range(KC):
                        last = e.matmul(ups[pp],
                                        up_b[qbi][:, c * 256 + fo: c * 256 + fo + 128],
                                        xng[:, c * GT:(c + 1) * GT],
                                        start=(c == 0), stop=(c == KC - 1))
                    return last
                P.step("tensor", gu_mms, wait=mmw, inc=[("tE", 1)])
                ev[("guq_mm", g, e_i, qq)] = C["tE"]
                ev[("gu_mm", g, e_i, fc)] = C["tE"]

                s2w = [("tE", ev[("gu_mm", g, e_i, fc)])]
                if fi >= 1:
                    pg2, pe2f = divmod(fi - 1, FC * NE)
                    pe2, pfc = divmod(pe2f, FC)
                    s2w.append(("vE", ev[("at", pg2, pe2, pfc)]))
                P.step("scalar", lambda e, pp=pp: e.activation(sg_b, gps[pp], AF.Silu),
                       wait=s2w, inc=[("sE", 1)])
                ev[("silu2", g, e_i, fc)] = C["sE"]
                atw = [("sE", ev[("silu2", g, e_i, fc)])]
                if e_i == 0 and fc == 0 and g > 0:
                    atw.append(ev[("ys", g - 1)])
                P.step("vector", lambda e, fc=fc, pp=pp: e.tensor_mul(
                    at_b[:, fc * GT:(fc + 1) * GT], sg_b, ups[pp]),
                    wait=atw, inc=[("vE", 1)])
                ev[("at", g, e_i, fc)] = C["vE"]

                if fc in (1, 3):
                    load_gu_quarter(g, e_i, fc // 2 + 2,
                                    [("tE", ev[("guq_mm", g, e_i, fc // 2)])])
                elif fc in (5, 7):
                    ng_, ne_ = (g, e_i + 1) if e_i + 1 < NE else (g + 1, 0)
                    if ng_ < NG:
                        load_gu_quarter(ng_, ne_, fc // 2 - 2,
                                        [("tE", ev[("guq_mm", g, e_i, fc // 2)])])

            for j in range(JT):
                for dc in range(4):
                    pp = (j * 4 + dc) % 2
                    dmw = [ev[("downl", g, e_i)],
                           ("vE", ev[("at", g, e_i, FC - 1)])]
                    prev = gi * 16 + j * 4 + dc - 2
                    if prev >= 0:
                        dmw.append(("vE", ev[("yup", prev)]))

                    def down_mms(e, j=j, dc=dc, pp=pp, dbi=dbi):
                        last = None
                        for fcc in range(FC):
                            last = e.matmul(
                                yps[pp],
                                at_b[:, fcc * GT + j * 128: fcc * GT + (j + 1) * 128],
                                down_b[dbi][:, fcc * D + dc * 512:
                                            fcc * D + (dc + 1) * 512],
                                start=(fcc == 0), stop=(fcc == FC - 1))
                        return last
                    P.step("tensor", down_mms, wait=dmw, inc=[("tE", 1)])
                    ev[("down_mm", g, e_i)] = C["tE"]
                    t_dmm = C["tE"]

                    cw_col = cw_res[:, (g * JT + j) * 8 + e_i:
                                    (g * JT + j) * 8 + e_i + 1]
                    ysl = y_g[:, j * D + dc * 512: j * D + (dc + 1) * 512]
                    yw = [("tE", t_dmm)]
                    if e_i == 0 and g > 0:
                        yw.append(ev[("ys", g - 1)])
                    if e_i == 0:
                        P.step("vector", lambda e, pp=pp, ysl=ysl, cw_col=cw_col:
                               e.tensor_scalar_mul(ysl, yps[pp], cw_col),
                               wait=yw, inc=[("vE", 1)])
                    else:
                        P.step("vector", lambda e, pp=pp, ysl=ysl, cw_col=cw_col:
                               e.scalar_tensor_tensor(ysl, yps[pp], cw_col, ysl,
                                                      ALU.mult, ALU.add),
                               wait=yw, inc=[("vE", 1)])
                    ev[("yup", gi * 16 + j * 4 + dc)] = C["vE"]
        ev[("ffn_mm", g)] = C["tE"]

        ev[("ys", g)] = P.dma(
            "sync",
            y_s[g * GT:(g + 1) * GT, :].rearrange("(j p) d -> p j d", p=128),
            y_g.rearrange("p (j d) -> p j d", j=JT),
            wait=[("vE", ev[("yup", (g * NE + NE - 1) * 16 + 15)])])

    P.barrier()
    marks["p2"] = P.mark()
    arena.reset()

    wo_sb = arena.alloc("wosb", [128, KC * D], BF16)
    qt_t = [arena.alloc(f"qtt{i}", [128, D], BF16) for i in range(2)]
    u_t = [arena.alloc(f"ut{i}", [128, D], BF16) for i in range(2)]
    x_t = arena.alloc("xt0", [128, D], F32)
    y_t = arena.alloc("yt0", [128, D], F32)
    o_t = arena.alloc("ot0", [128, D], F32)
    n_sb = arena.alloc("nsb", [128, D], F32)
    t1_sb = arena.alloc("t1sb", [128, D], F32)
    t4_sb = arena.alloc("t4sb", [128, D], F32)
    ug_sb = arena.alloc("ugsb", [128, D], F32)
    ub_sb = arena.alloc("ubsb", [128, D], F32)
    z_bf = arena.alloc("zbf", [128, D], BF16)
    zt_sb = [arena.alloc(f"ztsb{i}", [128, D], BF16) for i in range(2)]
    sg_c = [arena.alloc(f"sgc{i}", [128, 512], F32) for i in range(2)]
    sm3 = arena.alloc("sm3", [128, 8], F32)

    att_ps = PS[:, 0:2048]
    sps = [PS[:, 2048:2560], PS[:, 2560:3072]]
    ZPB = [PS6_BF, PS7_BF]

    ev[("wol",)] = P.dma("sync", wo_sb.rearrange("p (c d) -> p c d", c=KC),
                         wobf.rearrange("c p d -> p c d"),
                         wait=[conv_done["wo"]])

    def emit_s_combine(t):
        zi = t % 2
        sgw = [ev[("yl", t)]]
        if t > 0:
            sgw.append(("vE", ev[("t4", (t - 1) * 4 + 3)]))
        for dc in range(4):
            pp = dc % 2
            smw = [("vE", ev[("zte", t)]), ev[("wol",)]]
            prev_dc = t * 4 + dc - 2
            if prev_dc >= 0:
                smw.append(("vE", ev[("t4", prev_dc)]))

            def s_mms(e, dc=dc, pp=pp, zi=zi):
                e.matmul(sps[pp], ones_bf, bo_r[:, dc * 512:(dc + 1) * 512],
                         start=True, stop=False)
                last = None
                for c in range(KC):
                    last = e.matmul(sps[pp], zt_sb[zi][:, c * 128:(c + 1) * 128],
                                    wo_sb[:, c * D + dc * 512: c * D + (dc + 1) * 512],
                                    start=False, stop=(c == KC - 1))
                return last
            P.step("tensor", s_mms, wait=smw, inc=[("tE", 1)])
            ev[("s_mm", t, dc)] = C["tE"]

            if dc == 0:
                scw = list(sgw)
            elif dc == 1:
                scw = [("vE", ev[("t4", (t - 1) * 4 + 3)])] if t > 0 else []
            else:
                scw = [("vE", ev[("t4", t * 4 + dc - 2)])]
            P.step("scalar", lambda e, dc=dc: e.activation(
                sg_c[dc % 2], y_t[:, dc * 512:(dc + 1) * 512], AF.Silu),
                wait=scw)
            ev[("sgc", t, dc)] = C["sE"]

            t4w = [("tE", ev[("s_mm", t, dc)]), ("sE", ev[("sgc", t, dc)])]
            if prev_dc >= 0:
                t4w.append(("pE", ev[("comb_c", prev_dc)]))
            P.step("vector", lambda e, dc=dc, pp=pp: e.tensor_mul(
                t4_sb[:, dc * 512:(dc + 1) * 512], sps[pp], sg_c[dc % 2]),
                wait=t4w, inc=[("vE", 1)])
            ev[("t4", t * 4 + dc)] = C["vE"]

            cbw = [("vE", ev[("t4", t * 4 + dc)]), ev[("xl", t)]]
            if t >= 1 and dc == 0:
                cbw.append(ev[("outs", t - 1)])
            P.step("gpsimd", lambda e, dc=dc: e.tensor_add(
                o_t[:, dc * 512:(dc + 1) * 512],
                t4_sb[:, dc * 512:(dc + 1) * 512],
                x_t[:, dc * 512:(dc + 1) * 512]),
                wait=cbw, inc=[("pE", 1)])
            ev[("comb_c", t * 4 + dc)] = C["pE"]
        ev[("comb", t)] = C["pE"]
        ev[("outs", t)] = P.dma("sync", out_d[t * TT:(t + 1) * TT, :], o_t,
                                wait=[("pE", ev[("comb", t)])])
        if t + 1 < NT:
            ev[("xl", t + 1)] = P.dma(
                "sync", x_t, x_d[(t + 1) * TT:(t + 2) * TT, :],
                wait=[("pE", ev[("comb", t)])])
            ev[("yl", t + 1)] = P.dma(
                "sync", y_t, y_s[(t + 1) * TT:(t + 2) * TT, :],
                wait=[("sE", ev[("sgc", t, 3)]), ev[("ys", (t + 1) // JT)]])

    ev[("xl", 0)] = P.dma("sync", x_t, x_d[0:TT, :])
    ev[("yl", 0)] = P.dma("sync", y_t, y_s[0:TT, :], wait=[ev[("ys", 0)]])

    for t in range(NT):
        bi = t % 2
        lw = [("tE", ev[("att", t - 2)])] if t >= 2 else []
        ev[("qtl0", t)] = P.dma("sync", qt_t[bi][:, 0:HALF], qt_s[t, 0], wait=lw)
        ev[("qtl", t)] = P.dma("sync", qt_t[bi][:, HALF:D], qt_s[t, 1], wait=lw)
        uw = [("pE", ev[("ug", t - 2)])] if t >= 2 else []
        ev[("ul0", t)] = P.dma("sync", u_t[bi][:, 0:HALF], u_s[t, 0], wait=uw)
        ev[("ul", t)] = P.dma("sync", u_t[bi][:, HALF:D], u_s[t, 1], wait=uw)

        aw = [ev[("qtl0", t)], ev[("qtl", t)]]
        if t > 0:
            aw.append(("vE", ev[("nev", t - 1)]))

        def att_mms(e, bi=bi):
            last = None
            for hh in range(NH):
                last = e.matmul(att_ps[:, hh * 128:(hh + 1) * 128],
                                qt_t[bi][:, hh * 128:(hh + 1) * 128],
                                kv_bf[:, hh * 128:(hh + 1) * 128],
                                start=True, stop=True)
            return last
        P.step("tensor", att_mms, wait=aw, inc=[("tE", 1)])
        ev[("att", t)] = C["tE"]
        marks[f"p3_att{t}"] = P.mark()

        ugw = [ev[("ul0", t)], ev[("ul", t)]]
        if t > 0:
            ugw.append(("vE", ev[("tmul", t - 1)]))
        P.step("gpsimd", lambda e, bi=bi: e.tensor_mul(ug_sb, u_t[bi], g_rep),
               wait=ugw)
        zbw = [("vE", ev[("z", t - 1)])] if t > 0 else []
        P.step("gpsimd", lambda e, bi=bi: e.tensor_mul(ub_sb, u_t[bi], b_rep),
               wait=zbw, inc=[("pE", 1)])
        ev[("ug", t)] = C["pE"]

        if t >= 1:
            emit_s_combine(t - 1)
            marks[f"p3_comb{t-1}"] = P.mark()
        sqw = [("tE", ev[("att", t)])]
        if t > 0:
            sqw.append(("vE", ev[("tmul", t - 1)]))
        P.step("scalar", lambda e: e.activation(
            n_sb, att_ps, AF.Identity, accum_out=sm3[:, 0:1]), wait=sqw)
        P.step("scalar", lambda e: e.activation(
            n_sb, att_ps, AF.Square, accum_out=sm3[:, 1:2]))
        ev[("asq", t)] = C["sE"]

        P.step("scalar", lambda e: e.mul(sm3[:, 2:3], sm3[:, 0:1], 1.0 / D))
        P.step("scalar", lambda e: e.mul(sm3[:, 3:4], sm3[:, 1:2], 1.0 / D))
        P.step("scalar", lambda e: e.activation(sm3[:, 7:8], sm3[:, 2:3], AF.Square))
        P.step("scalar", lambda e: e.activation(
            sm3[:, 4:5], sm3[:, 7:8], AF.Relu, scale=-1.0, bias=sm3[:, 3:4]))
        P.step("scalar", lambda e: e.mul(sm3[:, 6:7], sm3[:, 0:1], -1.0 / D))
        P.step("scalar", lambda e: e.activation(
            sm3[:, 3:4], sm3[:, 4:5], AF.Ln, bias=ceps[:, 0:1]))
        P.step("scalar", lambda e: e.activation(
            sm3[:, 5:6], sm3[:, 3:4], AF.Exp, scale=-0.5))
        ev[("l3e", t)] = C["sE"]
        marks[f"p3_nev{t}"] = P.mark()

        P.step("vector", lambda e: e.tensor_scalar(
            n_sb, att_ps, sm3[:, 6:7], sm3[:, 5:6], ALU.add, ALU.mult),
            wait=[("sE", ev[("l3e", t)])], inc=[("vE", 1)])
        ev[("nev", t)] = C["vE"]

        P.step("vector", lambda e: e.tensor_mul(t1_sb, n_sb, ug_sb),
               wait=[("pE", ev[("ug", t)])], inc=[("vE", 1)])
        ev[("tmul", t)] = C["vE"]
        zw = []
        if t > 0:
            zw.append(("tE", ev[("zpg", t - 1, 3)]))
        P.step("vector", lambda e: e.tensor_add(z_bf, t1_sb, ub_sb),
               wait=zw, inc=[("vE", 1)])
        ev[("z", t)] = C["vE"]
        marks[f"p3_t1b{t}"] = P.mark()

        zi = t % 2
        for g4 in range(4):
            zpw = [("vE", ev[("z", t)])]
            if ("zpe", g4 % 2) in ev:
                zpw.append(("vE", ev[("zpe", g4 % 2)]))

            def zpose(e, g4=g4):
                last = None
                for k in range(4):
                    c = g4 * 4 + k
                    last = e.transpose(
                        ZPB[g4 % 2][:, k * 128:(k + 1) * 128],
                        z_bf[:, c * 128:(c + 1) * 128], id_bf)
                return last
            P.step("tensor", zpose, wait=zpw, inc=[("tE", 1)])
            ev[("zpg", t, g4)] = C["tE"]
            zew = [("tE", ev[("zpg", t, g4)])]
            if g4 < 2 and t >= 2:
                zew.append(("tE", ev[("s_mm", t - 2, 3)]))
            P.step("vector", lambda e, g4=g4, zi=zi: e.tensor_copy(
                zt_sb[zi][:, g4 * 512:(g4 + 1) * 512],
                ZPB[g4 % 2][:, 0:512]),
                wait=zew, inc=[("vE", 1)])
            ev[("zpe", g4 % 2)] = C["vE"]
        ev[("zte", t)] = C["vE"]
        marks[f"p3_zte{t}"] = P.mark()

    emit_s_combine(NT - 1)

    P.step("sync", lambda e: e.nop(), wait=P.all_dma_events())

    with nc.Block() as block:
        P.emit(block, upto=None if upto is None else marks[upto])
    return nc


def kernel(L0_attn_x, L0_attn_Wqk, L0_attn_bqk, L0_attn_Wv, L0_attn_bv,
           L0_attn_Wu, L0_attn_bu, L0_attn_Wo, L0_attn_bo, L0_attn_ln_g,
           L0_attn_ln_b, L0_ffn_router_w, L0_ffn_norm_w, L0_ffn_gate_w,
           L0_ffn_up_w, L0_ffn_down_w):
    nc = build()
    shared = {
        "wqk": np.ascontiguousarray(L0_attn_Wqk),
        "bqk": np.ascontiguousarray(L0_attn_bqk),
        "wv": np.ascontiguousarray(L0_attn_Wv),
        "bv": np.ascontiguousarray(L0_attn_bv),
        "wu": np.ascontiguousarray(L0_attn_Wu),
        "bu": np.ascontiguousarray(L0_attn_bu),
        "wo": np.ascontiguousarray(L0_attn_Wo),
        "bo": np.ascontiguousarray(L0_attn_bo),
        "lng": np.ascontiguousarray(L0_attn_ln_g),
        "lnb": np.ascontiguousarray(L0_attn_ln_b),
        "rw": np.ascontiguousarray(L0_ffn_router_w),
        "nw": np.ascontiguousarray(L0_ffn_norm_w),
        "gw": np.ascontiguousarray(L0_ffn_gate_w),
        "uw": np.ascontiguousarray(L0_ffn_up_w),
        "dw": np.ascontiguousarray(L0_ffn_down_w),
    }
    in_maps = [dict(shared, x=np.ascontiguousarray(L0_attn_x[i]))
               for i in range(NCORES)]
    res = run_bass_kernel_spmd(nc, in_maps, core_ids=list(range(NCORES)))
    return np.stack([r["out"] for r in res.results], axis=0)


# version 08
# speedup vs baseline: 1.0379x; 1.0379x over previous
import math
import numpy as np
import concourse.bass as bass
import concourse.mybir as mybir
from concourse.bass_utils import run_bass_kernel_spmd

F32 = mybir.dt.float32
BF16 = mybir.dt.bfloat16
AF = mybir.ActivationFunctionType
ALU = mybir.AluOpType
AX = mybir.AxisListType

NCORES = 8
D = 2048
SEQ = 4096
TT = 128
NT = SEQ // TT
KC = 16
HALF = 1024
NH = 16
NE = 8
FE = 1024
FC = 8
GT = 512
NG = SEQ // GT
JT = GT // TT

LN_EPS = 1e-5
RMS_EPS = 1e-6
MOE_SCALE = 4.0

DEBUG_SERIAL = False
HALF_LN_D = 0.5 * math.log(float(2048))

SB_RIGHT_END = 229344
ARENA_BYTES = 178 * 1024 + 256
ARENA_RESERVE = 16 * 1024


class Planner:
    ENGINES = ("sync", "gpsimd", "vector", "scalar", "tensor")

    def __init__(self, nc):
        self.nc = nc
        self.items = {e: [] for e in self.ENGINES}
        self.sems = {}
        self.cnt = {}

    def sem(self, name):
        self.sems[name] = self.nc.alloc_semaphore(name)
        self.cnt[name] = 0

    SERIAL_SEM = {"vector": "vE", "scalar": "sE", "gpsimd": "pE"}

    def step(self, engine, fn, wait=(), inc=(), _dma=False):
        wait = list(wait)
        inc = list(inc)
        own = self.SERIAL_SEM.get(engine)
        if own is not None and not _dma:
            wait.append((own, self.cnt[own]))
            if not any(s == own for s, _ in inc):
                inc.append((own, 1))
        wait = [(s, v) for (s, v) in wait if v > 0]
        if DEBUG_SERIAL:
            wait = [(s, self.cnt[s]) for s in self.cnt if self.cnt[s] > 0]
        self.items[engine].append((fn, tuple(wait), tuple(inc)))
        for s, a in inc:
            self.cnt[s] += a

    DMA_LANES = {"sync": 8, "scalar": 4, "gpsimd": 6}

    def init_lanes(self):
        self.lanes = {}
        self.lane_rr = {}
        for eng, n in self.DMA_LANES.items():
            pfx = {"sync": "dS", "scalar": "dA", "gpsimd": "dG"}[eng]
            self.lanes[eng] = [f"{pfx}{i}" for i in range(n)]
            for s in self.lanes[eng]:
                self.sem(s)
            self.lane_rr[eng] = 0

    def dma(self, engine, out, in_, wait=(), **kw):
        lane = self.lanes[engine][self.lane_rr[engine] % len(self.lanes[engine])]
        self.lane_rr[engine] += 1
        wait = list(wait) + [(lane, self.cnt[lane])]
        self.step(engine, lambda e: e.dma_start(out=out, in_=in_, **kw),
                  wait=wait, inc=[(lane, 16)], _dma=True)
        return (lane, self.cnt[lane])

    def all_lanes(self, engine):
        return [(s, self.cnt[s]) for s in self.lanes[engine]]

    def all_dma_events(self):
        out = []
        for eng in self.lanes:
            out += self.all_lanes(eng)
        return out

    def barrier(self):
        eng_sem = {"sync": "bS", "gpsimd": "bG", "vector": "bV",
                   "scalar": "bA", "tensor": "bT"}
        for e in self.ENGINES:
            w = self.all_lanes(e) if (e in self.lanes and e != "gpsimd") else []
            if e in self.SERIAL_SEM:
                w.append((self.SERIAL_SEM[e], self.cnt[self.SERIAL_SEM[e]]))
            self.step(e, lambda eng: eng.nop(), wait=w, inc=[(eng_sem[e], 1)],
                      _dma=True)
        targets = [(s, self.cnt[s]) for s in eng_sem.values()]
        for e in self.ENGINES:
            self.step(e, lambda eng: eng.nop(), wait=targets, _dma=True)

    def mark(self):
        return {e: len(self.items[e]) for e in self.ENGINES}

    def emit(self, block, upto=None):
        def make(engine):
            def run(eng):
                last = {}
                items = self.items[engine]
                if upto is not None:
                    items = items[:upto[engine]]
                for fn, waits, incs in items:
                    for s, v in waits:
                        if last.get(s, 0) >= v:
                            continue
                        eng.wait_ge(self.sems[s], v)
                        last[s] = v
                    inst = fn(eng)
                    for s, a in incs:
                        inst = inst.then_inc(self.sems[s], a)
            return run
        block.sync(make("sync"))
        block.gpsimd(make("gpsimd"))
        block.vector(make("vector"))
        block.scalar(make("scalar"))
        block.tensor(make("tensor"))


class Arena:
    def __init__(self, nc, base, size):
        self.nc, self.base, self.off, self.n = nc, base, base, 0
        self.limit = base + size

    def alloc(self, name, shape, dtype):
        esz = 2 if dtype == BF16 else 4
        nbytes = shape[1] * esz
        off = (self.off + 127) // 128 * 128
        assert off + nbytes <= self.limit, (
            f"arena overflow: {name} needs {nbytes}B at +{off - self.base}")
        self.n += 1
        t = self.nc.alloc_sbuf_tensor_at(f"{name}_{self.n}", shape, dtype, offset=off)
        self.off = off + nbytes
        return t.ap()

    def reset(self):
        self.off = self.base


def build(upto=None, dbg=False):
    nc = bass.Bass("TRN2", target_bir_lowering=False, debug=False)

    x_d = nc.dram_tensor("x", [SEQ, D], F32, kind="ExternalInput").ap()
    wqk_d = nc.dram_tensor("wqk", [D, D], F32, kind="ExternalInput").ap()
    bqk_d = nc.dram_tensor("bqk", [D], F32, kind="ExternalInput").ap()
    wv_d = nc.dram_tensor("wv", [D, D], F32, kind="ExternalInput").ap()
    bv_d = nc.dram_tensor("bv", [D], F32, kind="ExternalInput").ap()
    wu_d = nc.dram_tensor("wu", [D, D], F32, kind="ExternalInput").ap()
    bu_d = nc.dram_tensor("bu", [D], F32, kind="ExternalInput").ap()
    wo_d = nc.dram_tensor("wo", [D, D], F32, kind="ExternalInput").ap()
    bo_d = nc.dram_tensor("bo", [D], F32, kind="ExternalInput").ap()
    lng_d = nc.dram_tensor("lng", [D], F32, kind="ExternalInput").ap()
    lnb_d = nc.dram_tensor("lnb", [D], F32, kind="ExternalInput").ap()
    rw_d = nc.dram_tensor("rw", [NE, D], F32, kind="ExternalInput").ap()
    nw_d = nc.dram_tensor("nw", [D], F32, kind="ExternalInput").ap()
    gw_d = nc.dram_tensor("gw", [NE, D, FE], F32, kind="ExternalInput").ap()
    uw_d = nc.dram_tensor("uw", [NE, D, FE], F32, kind="ExternalInput").ap()
    dw_d = nc.dram_tensor("dw", [NE, FE, D], F32, kind="ExternalInput").ap()
    out_d = nc.dram_tensor("out", [SEQ, D], F32, kind="ExternalOutput").ap()

    knd = "ExternalOutput" if dbg else "Internal"
    xnt_s = nc.dram_tensor("xnt_s", [KC, 128, SEQ], BF16, kind=knd).ap()
    wh1_s = nc.dram_tensor("wh1_s", [3, KC, 128, HALF], BF16, kind="Internal").ap()
    qt_s = nc.dram_tensor("qt_s", [NT, 2, 128, HALF], BF16, kind=knd).ap()
    u_s = nc.dram_tensor("u_s", [NT, 2, 128, HALF], BF16, kind=knd).ap()
    y_s = nc.dram_tensor("y_s", [SEQ, D], F32, kind=knd).ap()
    if dbg:
        dbg_cw = nc.dram_tensor("dbg_cw", [128, NT * 8], F32, kind="ExternalOutput").ap()
        dbg_kv = nc.dram_tensor("dbg_kv", [128, D], BF16, kind="ExternalOutput").ap()
        dbg_gr = nc.dram_tensor("dbg_gr", [128, 2 * D], F32, kind="ExternalOutput").ap()
    gbf = nc.dram_tensor("gbf", [NE, KC, 128, FE], BF16, kind="Internal").ap()
    ubf = nc.dram_tensor("ubf", [NE, KC, 128, FE], BF16, kind="Internal").ap()
    dbf = nc.dram_tensor("dbf", [NE, FC, 128, D], BF16, kind="Internal").ap()
    wobf = nc.dram_tensor("wobf", [KC, 128, D], BF16, kind="Internal").ap()

    def sb(name, shape, dtype):
        return nc.alloc_sbuf_tensor(name, shape, dtype).ap()

    id_bf = sb("id_bf", [128, 128], BF16)
    id_f32 = sb("id_f32", [128, 128], F32)
    iot = sb("iot", [128, 128], F32)
    ones_bf = sb("ones_bf", [1, 128], BF16)
    ones_f32 = sb("ones_f32", [1, 128], F32)
    bqk_r = sb("bqk_r", [1, D], BF16)
    bv_r = sb("bv_r", [1, D], BF16)
    bu_r = sb("bu_r", [1, D], BF16)
    bo_r = sb("bo_r", [1, D], BF16)
    nwT = sb("nwT", [128, 16], F32)
    g_rep = nc.alloc_sbuf_tensor_at(
        "g_rep", [128, D], F32, offset=SB_RIGHT_END - ARENA_RESERVE).ap()
    b_rep = nc.alloc_sbuf_tensor_at(
        "b_rep", [128, D], F32, offset=SB_RIGHT_END - ARENA_RESERVE + D * 4).ap()
    rwnT = sb("rwnT", [128, KC * 8], BF16)
    cw_res = sb("cw_res", [128, NT * 8], F32)
    ceps = sb("ceps", [128, 4], F32)
    kv_bf = sb("kv_bf", [128, D], BF16)
    lgf = sb("lgf", [128, 8], F32)
    sml = sb("sml", [128, 16], F32)
    cwt = sb("cwt", [128, 8], F32)

    PS = nc.alloc_psum_tensor("PS", [128, 4096], F32).ap()
    PS7_BF = PS[:, 3584:4096].bitcast(BF16)
    PS6_BF = PS[:, 3072:3584].bitcast(BF16)

    arena = Arena(nc, SB_RIGHT_END - ARENA_BYTES, ARENA_BYTES - ARENA_RESERVE)
    wq_sb = arena.alloc("wq", [128, KC * HALF], BF16)
    wv_sb = arena.alloc("wvs", [128, KC * HALF], BF16)
    wu_sb = arena.alloc("wus", [128, KC * HALF], BF16)

    P = Planner(nc)
    for s in ("tE", "vE", "sE", "pE", "bS", "bG", "bV", "bA", "bT"):
        P.sem(s)
    P.init_lanes()
    C = P.cnt
    ev = {}

    W_END = arena.off
    lng_row = arena.alloc("lng_row", [1, D], F32)
    lnb_row = arena.alloc("lnb_row", [1, D], F32)
    rw_sb = arena.alloc("rw_sb", [8, D], F32)
    nw16 = arena.alloc("nw16", [16, 128], F32)
    w_ready = {0: [
        P.dma("gpsimd", wsb.rearrange("p (c n) -> p c n", c=KC),
              wd[:, 0:HALF].rearrange("(c p) n -> p c n", p=128))
        for wd, wsb in ((wqk_d, wq_sb), (wv_d, wv_sb), (wu_d, wu_sb))]}
    conv_done = {}
    conv_done["wh1"] = [
        P.dma("gpsimd", wh1_s[i],
              wd[:, HALF:D].rearrange("(c p) n -> c p n", p=128))
        for i, wd in enumerate((wqk_d, wv_d, wu_d))]

    d_setup = [
        P.dma("sync", lng_row, lng_d.rearrange("(o d) -> o d", o=1)),
        P.dma("sync", lnb_row, lnb_d.rearrange("(o d) -> o d", o=1)),
        P.dma("sync", rw_sb, rw_d),
        P.dma("sync", nw16, nw_d.rearrange("(c p) -> c p", p=128)),
    ]
    b_rows = [
        P.dma("gpsimd", bqk_r, bqk_d.rearrange("(o d) -> o d", o=1)),
        P.dma("gpsimd", bv_r, bv_d.rearrange("(o d) -> o d", o=1)),
        P.dma("gpsimd", bu_r, bu_d.rearrange("(o d) -> o d", o=1)),
        P.dma("gpsimd", bo_r, bo_d.rearrange("(o d) -> o d", o=1)),
    ]
    P.step("gpsimd", lambda e: e.memset(ceps[:, 0:1], D * LN_EPS))
    P.step("gpsimd", lambda e: e.memset(ceps[:, 1:2], RMS_EPS))
    P.step("gpsimd", lambda e: e.memset(ceps[:, 2:3], D * LN_EPS * RMS_EPS))
    P.step("gpsimd", lambda e: e.memset(ceps[:, 3:4], HALF_LN_D))
    P.step("gpsimd", lambda e: e.memset(ones_bf, 1.0))
    P.step("gpsimd", lambda e: e.memset(ones_f32, 1.0))
    P.step("gpsimd", lambda e: e.iota(iot, [[1, 128]], channel_multiplier=-1,
                                      allow_small_or_imprecise_dtypes=True),
           inc=[("pE", 1)])
    pe_iota = C["pE"]
    P.step("vector", lambda e: e.tensor_single_scalar(id_bf, iot, 0.0, ALU.is_equal),
           wait=[("pE", pe_iota)])
    P.step("vector", lambda e: e.tensor_single_scalar(id_f32, iot, 0.0, ALU.is_equal),
           inc=[("vE", 1)])

    for ci in range(4):
        P.step("tensor",
               lambda e, ci=ci: (
                   e.matmul(PS[:, 0:512], ones_f32,
                            lng_row[:, ci * 512:(ci + 1) * 512], start=True, stop=True),
                   e.matmul(PS[:, 512:1024], ones_f32,
                            lnb_row[:, ci * 512:(ci + 1) * 512], start=True, stop=True))[-1],
               wait=d_setup + [("vE", C["vE"])], inc=[("tE", 1)])
        t_rep = C["tE"]
        P.step("vector", lambda e, ci=ci: e.tensor_copy(
            g_rep[:, ci * 512:(ci + 1) * 512], PS[:, 0:512]), wait=[("tE", t_rep)])
        P.step("vector", lambda e, ci=ci: e.tensor_copy(
            b_rep[:, ci * 512:(ci + 1) * 512], PS[:, 512:1024]),
            wait=[("tE", t_rep)], inc=[("vE", 1)])

    P.step("tensor", lambda e: e.transpose(PS[:, 1024:1040], nw16, id_f32[0:16, 0:16]),
           wait=d_setup + [("vE", C["vE"])], inc=[("tE", 1)])
    t_nw = C["tE"]
    P.step("vector", lambda e: e.tensor_copy(nwT, PS[:, 1024:1040]),
           wait=[("tE", t_nw)], inc=[("vE", 1)])
    v_nwT = C["vE"]
    for c in range(KC):
        P.step("tensor", lambda e, c=c: e.transpose(
            PS[:, 1536 + (c % 2) * 512: 1544 + (c % 2) * 512],
            rw_sb[0:8, c * 128:(c + 1) * 128], id_f32[0:8, 0:8]),
            wait=[("vE", C["vE"])], inc=[("tE", 1)])
        t_rw = C["tE"]
        P.step("vector", lambda e, c=c: e.tensor_scalar_mul(
            rwnT[:, c * 8:(c + 1) * 8],
            PS[:, 1536 + (c % 2) * 512: 1544 + (c % 2) * 512], nwT[:, c:c + 1]),
            wait=[("tE", t_rw), ("vE", v_nwT)], inc=[("vE", 1)])

    P.barrier()
    marks = {"setup": P.mark()}
    ARENA_BASE0 = arena.base
    arena.base = arena.off = ((W_END + 127) // 128 * 128)


    def convert_expert(e, wait=()):
        e1 = P.dma("gpsimd", gbf[e], gw_d[e].rearrange("(c p) f -> c p f", p=128),
                   wait=wait)
        e2 = P.dma("gpsimd", ubf[e], uw_d[e].rearrange("(c p) f -> c p f", p=128))
        e3 = P.dma("gpsimd", dbf[e], dw_d[e].rearrange("(c p) d -> c p d", p=128))
        conv_done[e] = [e1, e2, e3]

    x_buf = [arena.alloc(f"x{i}", [128, D], F32) for i in range(2)]
    xn_bf = [arena.alloc(f"xnbf{i}", [128, D], BF16) for i in range(2)]
    xnt_b = [arena.alloc(f"xnt{i}", [128, D], BF16) for i in range(2)]
    q_bf = [arena.alloc(f"qbf{i}", [128, HALF], BF16) for i in range(2)]
    v_bf = arena.alloc("vbf", [128, HALF], BF16)
    u_bf = [arena.alloc(f"ubfb{i}", [128, HALF], BF16) for i in range(2)]
    qt_b = [arena.alloc(f"qtb{i}", [128, HALF], BF16) for i in range(2)]
    el_m = arena.alloc("elm", [128, HALF], F32)
    el_e = arena.alloc("ele", [128, HALF], F32)
    el_r = arena.alloc("elr", [128, HALF], F32)
    kv_acc = arena.alloc("kvacc", [128, D], F32)

    q_ps = PS[:, 0:1024]
    v_ps = PS[:, 1024:2048]
    u_ps = PS[:, 2048:3072]
    log_ps = PS[:, 3072:3080]
    QPB = [PS[:, 1024:1536].bitcast(BF16), PS[:, 1536:2048].bitcast(BF16)]

    def load_w_half(h):
        evs = []
        if True:
            w = [("tE", ev[("mms", 0, NT - 1)])] + list(conv_done["wh1"])
            for i, wsb in enumerate((wq_sb, wv_sb, wu_sb)):
                evs.append(P.dma(
                    "sync", wsb.rearrange("p (c n) -> p c n", c=KC),
                    wh1_s[i].rearrange("c p n -> p c n"), wait=w))
        return evs


    for _e in range(3):
        convert_expert(_e)

    def emit_ln(t):
        bi = t % 2
        w = []
        if t >= 2:
            w = [("vE", ev[("xnbf", t - 2)]), ("sE", ev[("xsq", t - 2)])]
        ev[("x", t)] = P.dma("sync", x_buf[bi],
                             x_d[t * TT:(t + 1) * TT, :], wait=w)
        idw = [ev[("x", t)]]
        if t >= 2:
            idw.append(("tE", ev[("xpg", t - 2, 1)]))
        P.step("scalar", lambda e, bi=bi: e.activation(
            xn_bf[bi], x_buf[bi], AF.Identity, accum_out=sml[:, 0:1]),
            wait=idw)
        P.step("scalar", lambda e, bi=bi: e.activation(
            xn_bf[bi], x_buf[bi], AF.Square, accum_out=sml[:, 1:2]))
        ev[("xsq", t)] = C["sE"]

        lsw = [("sE", ev[("xsq", t)])]
        if ("lgfe", t - 1) in ev:
            lsw.append(("vE", ev[("lgfe", t - 1)]))
        P.step("scalar", lambda e: e.activation(sml[:, 8:9], sml[:, 0:1], AF.Square),
               wait=lsw)
        P.step("scalar", lambda e: e.activation(
            sml[:, 4:5], sml[:, 8:9], AF.Relu, scale=-1.0 / D, bias=sml[:, 1:2]))
        P.step("scalar", lambda e: e.activation(
            sml[:, 7:8], sml[:, 4:5], AF.Ln, bias=ceps[:, 0:1]))
        P.step("scalar", lambda e: e.activation(
            sml[:, 5:6], sml[:, 7:8], AF.Exp, scale=-0.5, bias=ceps[:, 3:4]))
        P.step("scalar", lambda e: e.mul(sml[:, 2:3], sml[:, 0:1], 1.0 / D))
        P.step("scalar", lambda e: e.mul(sml[:, 9:10], sml[:, 7:8], 0.5))
        P.step("scalar", lambda e: e.activation(
            sml[:, 10:11], sml[:, 4:5], AF.Ln, scale=1.0 + RMS_EPS,
            bias=ceps[:, 2:3]))
        P.step("scalar", lambda e: e.activation(
            sml[:, 11:12], sml[:, 10:11], AF.Exp, scale=-0.5, bias=sml[:, 9:10]))
        ev[("lnexp", t)] = C["sE"]
        P.step("vector", lambda e, bi=bi: e.tensor_scalar(
            xn_bf[bi], x_buf[bi], sml[:, 2:3], sml[:, 5:6],
            ALU.subtract, ALU.mult),
            wait=[("sE", ev[("lnexp", t)])], inc=[("vE", 1)])
        ev[("xnbf", t)] = C["vE"]

    def emit_xpose(tn):
        bj = tn % 2
        xbanks = [PS7_BF, PS6_BF]
        for g2 in range(2):
            xw = [("vE", ev[("xnbf", tn)])]
            if ("xpfree", g2) in ev:
                xw.append(("vE", ev[("xpfree", g2)]))
            if g2 == 1 and ("lgfe", tn - 1) in ev:
                xw.append(("vE", ev[("lgfe", tn - 1)]))

            def xpose(e, g2=g2, bj=bj):
                last = None
                for k in range(8):
                    c = g2 * 8 + k
                    last = e.transpose(
                        xbanks[g2][:, k * 128:(k + 1) * 128],
                        xn_bf[bj][:, c * 128:(c + 1) * 128], id_bf)
                return last
            P.step("tensor", xpose, wait=xw, inc=[("tE", 1)])
            ev[("xpg", tn, g2)] = C["tE"]
            xew = [("tE", ev[("xpg", tn, g2)])]
            if g2 == 0 and tn >= 2:
                xew += [("tE", ev[("mms", 0, tn - 2)]),
                        ev[("xnts", tn - 2)]]
            P.step("vector", lambda e, g2=g2, bj=bj: e.tensor_copy(
                xnt_b[bj][:, g2 * 1024:(g2 + 1) * 1024],
                xbanks[g2][:, 0:1024]),
                wait=xew, inc=[("vE", 1)])
            ev[("xpfree", g2)] = C["vE"]
        ev[("xnt", tn)] = C["vE"]
        ev[("xnts", tn)] = P.dma(
            "sync",
            xnt_s[:, :, tn * TT:(tn + 1) * TT].rearrange("c p e -> p c e"),
            xnt_b[bj].rearrange("p (c e) -> p c e", c=KC),
            wait=[("vE", ev[("xnt", tn)])])

    for h in range(2):
        if h == 1:
            w_ready[1] = load_w_half(1)
            for e_i in range(3, NE):
                convert_expert(e_i)
            conv_done["wo"] = P.dma(
                "gpsimd", wobf, wo_d.rearrange("(c p) d -> c p d", p=128))

        if h == 0:
            emit_ln(0)
            emit_xpose(0)
        for t in range(NT):
            bi = t % 2
            if h == 1:
                if t >= 2:
                    w = [("tE", ev[("mms", 1, t - 2)])]
                else:
                    w = [("tE", ev[("mms", 0, NT - 1)]),
                         ev[("xnts", NT - 2 + t)]]
                ev[("xntl", t)] = P.dma(
                    "sync", xnt_b[bi].rearrange("p (c e) -> p c e", c=KC),
                    xnt_s[:, :, t * TT:(t + 1) * TT].rearrange("c p e -> p c e"),
                    wait=w)

            mmw = list(w_ready[h])
            if (h, t) == (0, 0):
                mmw += b_rows
            if h == 0:
                mmw.append(("vE", ev[("xnt", t)]))
            else:
                mmw.append(ev[("xntl", t)])
            if (h, t) != (0, 0):
                ph, pt = (h, t - 1) if t > 0 else (0, NT - 1)
                mmw += [("vE", ev[("elu_sub", ph, pt)]),
                        ("vE", ev[("qt", ph, pt)])]
                if ph == 0:
                    mmw.append(("vE", ev[("lgfe", pt)]))

            def wave_a(e, h=h, bi=bi):
                for ps_t, br in ((q_ps, bqk_r), (v_ps, bv_r)):
                    for s2 in range(2):
                        e.matmul(ps_t[:, s2 * 512:(s2 + 1) * 512], ones_bf,
                                 br[:, h * HALF + s2 * 512: h * HALF + (s2 + 1) * 512],
                                 start=True, stop=False)
                last = None
                for c in range(KC):
                    lhs = xnt_b[bi][:, c * 128:(c + 1) * 128]
                    for ps_t, wsb in ((q_ps, wq_sb), (v_ps, wv_sb)):
                        for s2 in range(2):
                            last = e.matmul(
                                ps_t[:, s2 * 512:(s2 + 1) * 512], lhs,
                                wsb[:, c * HALF + s2 * 512: c * HALF + (s2 + 1) * 512],
                                start=False, stop=(c == KC - 1))
                    if h == 0:
                        last = e.matmul(log_ps, lhs, rwnT[:, c * 8:(c + 1) * 8],
                                        start=(c == 0), stop=(c == KC - 1))
                return last
            P.step("tensor", wave_a, wait=mmw, inc=[("tE", 1)])
            ev[("mmsA", h, t)] = C["tE"]

            mbw = []
            if (h, t) != (0, 0):
                ph, pt = (h, t - 1) if t > 0 else (0, NT - 1)
                mbw.append(("vE", ev[("kv_evac", ph, pt)]))

            def wave_b(e, h=h, bi=bi):
                for s2 in range(2):
                    e.matmul(u_ps[:, s2 * 512:(s2 + 1) * 512], ones_bf,
                             bu_r[:, h * HALF + s2 * 512: h * HALF + (s2 + 1) * 512],
                             start=True, stop=False)
                last = None
                for c in range(KC):
                    lhs = xnt_b[bi][:, c * 128:(c + 1) * 128]
                    for s2 in range(2):
                        last = e.matmul(
                            u_ps[:, s2 * 512:(s2 + 1) * 512], lhs,
                            wu_sb[:, c * HALF + s2 * 512: c * HALF + (s2 + 1) * 512],
                            start=False, stop=(c == KC - 1))
                return last
            P.step("tensor", wave_b, wait=mbw, inc=[("tE", 1)])
            ev[("mms", h, t)] = C["tE"]

            P.step("vector", lambda e: e.tensor_scalar_min(el_m, q_ps, 0.0),
                   wait=[("tE", ev[("mmsA", h, t)])], inc=[("vE", 1)])
            ev[("elu_m", h, t)] = C["vE"]
            P.step("vector", lambda e: e.tensor_sub(el_r, q_ps, el_m),
                   inc=[("vE", 1)])
            ev[("elu_sub", h, t)] = C["vE"]
            P.step("scalar", lambda e: e.activation(el_e, el_m, AF.Exp),
                   wait=[("vE", ev[("elu_m", h, t)])], inc=[("sE", 1)])
            ev[("elu_e", h, t)] = C["sE"]
            P.step("vector", lambda e, bi=bi: e.scalar_tensor_tensor(
                q_bf[bi], el_e, -1.0, el_r, ALU.add, ALU.add),
                wait=[("sE", ev[("elu_e", h, t)])], inc=[("vE", 1)])
            ev[("elu", h, t)] = C["vE"]

            silw = [("tE", ev[("mmsA", h, t)])]
            if (h, t) != (0, 0):
                ph, pt = (h, t - 1) if t > 0 else (0, NT - 1)
                silw.append(("tE", ev[("kv", ph, pt)]))
            P.step("scalar", lambda e: e.activation(v_bf, v_ps, AF.Silu), wait=silw)
            ev[("silu_v", h, t)] = C["sE"]
            suw = [("tE", ev[("mms", h, t)])]
            if t >= 2:
                suw.append(ev[("us", h, t - 2)])
            elif h == 1:
                suw.append(ev[("us", 0, NT - 2 + t)])
            P.step("scalar", lambda e, bi=bi: e.activation(u_bf[bi], u_ps, AF.Silu),
                   wait=suw)
            ev[("silu", h, t)] = C["sE"]
            ev[("us", h, t)] = P.dma("sync", u_s[t, h], u_bf[bi],
                                     wait=[("sE", ev[("silu", h, t)])])

            def kv_mms(e, bi=bi):
                last = None
                for i in range(8):
                    last = e.matmul(u_ps[:, i * 128:(i + 1) * 128],
                                    q_bf[bi][:, i * 128:(i + 1) * 128],
                                    v_bf[:, i * 128:(i + 1) * 128],
                                    start=True, stop=True)
                return last
            P.step("tensor", kv_mms,
                   wait=[("vE", ev[("elu", h, t)]), ("sE", ev[("silu", h, t)])],
                   inc=[("tE", 1)])
            ev[("kv", h, t)] = C["tE"]
            if t == 0:
                P.step("vector", lambda e, h=h: e.tensor_copy(
                    kv_acc[:, h * HALF:(h + 1) * HALF], u_ps),
                    wait=[("tE", ev[("kv", h, t)])], inc=[("vE", 1)])
            else:
                P.step("vector", lambda e, h=h: e.tensor_add(
                    kv_acc[:, h * HALF:(h + 1) * HALF], u_ps,
                    kv_acc[:, h * HALF:(h + 1) * HALF]),
                    wait=[("tE", ev[("kv", h, t)])], inc=[("vE", 1)])
            ev[("kv_evac", h, t)] = C["vE"]

            qxw = [("vE", ev[("elu", h, t)]), ("sE", ev[("silu_v", h, t)])]
            if ("qtprev",) in ev:
                qxw.append(("vE", ev[("qtprev",)]))

            def qpose(e, bi=bi):
                last = None
                for k in range(8):
                    last = e.transpose(
                        QPB[0][:, k * 128:(k + 1) * 128],
                        q_bf[bi][:, k * 128:(k + 1) * 128], id_bf)
                return last
            P.step("tensor", qpose, wait=qxw, inc=[("tE", 1)])
            ev[("qpg", h, t)] = C["tE"]
            qsw = [("tE", ev[("qpg", h, t)])]
            if t >= 2:
                qsw.append(ev[("qts", h, t - 2)])
            elif h == 1:
                qsw.append(ev[("qts", 0, NT - 2 + t)])
            P.step("vector", lambda e, bi=bi: e.tensor_copy(
                qt_b[bi], QPB[0][:, 0:1024]),
                wait=qsw, inc=[("vE", 1)])
            ev[("qtprev",)] = C["vE"]
            ev[("qt", h, t)] = C["vE"]
            ev[("qts", h, t)] = P.dma("sync", qt_s[t, h], qt_b[bi],
                                      wait=[("vE", ev[("qt", h, t)])])

            if h == 0:
                P.step("vector", lambda e: e.tensor_scalar_mul(
                    lgf, log_ps, sml[:, 11:12]),
                    wait=[("tE", ev[("mmsA", 0, t)]), ("sE", ev[("lnexp", t)])],
                    inc=[("vE", 1)])
                ev[("lgfe", t)] = C["vE"]

            if h == 0 and t + 1 < NT:
                emit_ln(t + 1)
                emit_xpose(t + 1)

            if h == 0:
                P.step("vector", lambda e: e.reduce_max(
                    sml[:, 12:13], lgf, axis=AX.X))
                P.step("vector", lambda e: e.reduce_max(
                    sml[:, 13:14], lgf, axis=AX.X, negate=True))
                P.step("vector", lambda e: e.tensor_single_scalar(
                    cwt, lgf, sml[:, 12:13], ALU.is_ge))
                P.step("vector", lambda e: e.scalar_tensor_tensor(
                    cwt, cwt, 1e30, lgf, ALU.mult, ALU.subtract))
                P.step("vector", lambda e: e.tensor_reduce(
                    sml[:, 14:15], cwt, axis=AX.X, op=ALU.min, negate=True))
                ev[("cwv1", t)] = C["vE"]

                P.step("scalar", lambda e: e.activation(
                    cwt, lgf, AF.Exp, bias=sml[:, 13:14]),
                    wait=[("vE", ev[("cwv1", t)])])
                P.step("scalar", lambda e: e.activation(
                    sml[:, 15:16], sml[:, 14:15], AF.Exp, bias=sml[:, 13:14]))
                ev[("cws", t)] = C["sE"]

                P.step("vector", lambda e: e.tensor_scalar_add(
                    sml[:, 15:16], sml[:, 15:16], 1.0),
                    wait=[("sE", ev[("cws", t)])])
                P.step("vector", lambda e: e.reciprocal(
                    sml[:, 15:16], sml[:, 15:16]))
                P.step("vector", lambda e: e.tensor_scalar_mul(
                    sml[:, 15:16], sml[:, 15:16], MOE_SCALE))
                P.step("vector", lambda e: e.tensor_single_scalar(
                    lgf, lgf, sml[:, 14:15], ALU.is_ge))
                P.step("vector", lambda e: e.tensor_mul(cwt, cwt, lgf))
                P.step("vector", lambda e, t=t: e.tensor_scalar_mul(
                    cw_res[:, t * 8:(t + 1) * 8], cwt, sml[:, 15:16]))
                ev[("cw", t)] = C["vE"]

    P.step("vector", lambda e: e.tensor_copy(kv_bf, kv_acc),
           wait=[("vE", ev[("kv_evac", 1, NT - 1)])], inc=[("vE", 1)])

    P.barrier()
    marks["p1"] = P.mark()
    arena.base = ARENA_BASE0
    arena.reset()

    xng = arena.alloc("xng", [128, KC * GT], BF16)
    y_g = arena.alloc("yg", [128, JT * D], F32)
    gate_b = [arena.alloc(f"gateb{i}", [128, KC * 256], BF16) for i in range(2)]
    up_b = [arena.alloc(f"upb{i}", [128, KC * 256], BF16) for i in range(2)]
    down_b = [arena.alloc(f"downb{i}", [128, FC * D], BF16) for i in range(2)]
    at_b = arena.alloc("atb", [128, FC * GT], BF16)
    sg_b = arena.alloc("sgb", [128, GT], F32)

    gps = [PS[:, 0:512], PS[:, 512:1024]]
    ups = [PS[:, 1024:1536], PS[:, 1536:2048]]
    yps = [PS[:, 2048:2560], PS[:, 2560:3072]]

    def load_gu_quarter(g, e_i, qq, wait):
        qbi = qq % 2
        qw = list(conv_done[e_i]) + list(wait)
        ev[("gql", g, e_i, qq)] = P.dma(
            "sync", gate_b[qbi].rearrange("p (c f) -> p c f", c=KC),
            gbf[e_i][:, :, qq * 256:(qq + 1) * 256].rearrange("c p f -> p c f"),
            wait=qw)
        ev[("uql", g, e_i, qq)] = P.dma(
            "sync", up_b[qbi].rearrange("p (c f) -> p c f", c=KC),
            ubf[e_i][:, :, qq * 256:(qq + 1) * 256].rearrange("c p f -> p c f"),
            wait=qw)

    load_gu_quarter(0, 0, 0, [])
    load_gu_quarter(0, 0, 1, [])

    for g in range(NG):
        xw = [("tE", ev[("ffn_mm", g - 1)])] if g > 0 else []
        ev[("xng", g)] = P.dma(
            "sync", xng.rearrange("p (c e) -> p c e", c=KC),
            xnt_s[:, :, g * GT:(g + 1) * GT].rearrange("c p e -> p c e"), wait=xw)

        for e_i in range(NE):
            dbi = e_i % 2
            dw_w = list(conv_done[e_i])
            gi = g * NE + e_i
            if gi >= 2:
                pg, pe = divmod(gi - 2, NE)
                dw_w.append(("tE", ev[("down_mm", pg, pe)]))
            ev[("downl", g, e_i)] = P.dma(
                "sync", down_b[dbi].rearrange("p (c d) -> p c d", c=FC),
                dbf[e_i].rearrange("c p d -> p c d"), wait=dw_w)

            for fc in range(FC):
                qq = fc // 2
                qbi = qq % 2
                fo = (fc % 2) * 128
                pp = fc % 2
                fi = gi * FC + fc
                mmw = [ev[("gql", g, e_i, qq)], ev[("uql", g, e_i, qq)],
                       ev[("xng", g)]]
                if fi >= 2:
                    pg2, pe2f = divmod(fi - 2, FC * NE)
                    pe2, pfc = divmod(pe2f, FC)
                    mmw += [("sE", ev[("silu2", pg2, pe2, pfc)]),
                            ("vE", ev[("at", pg2, pe2, pfc)])]

                def gu_mms(e, qbi=qbi, fo=fo, pp=pp):
                    last = None
                    for c in range(KC):
                        last = e.matmul(gps[pp],
                                        gate_b[qbi][:, c * 256 + fo: c * 256 + fo + 128],
                                        xng[:, c * GT:(c + 1) * GT],
                                        start=(c == 0), stop=(c == KC - 1))
                    for c in range(KC):
                        last = e.matmul(ups[pp],
                                        up_b[qbi][:, c * 256 + fo: c * 256 + fo + 128],
                                        xng[:, c * GT:(c + 1) * GT],
                                        start=(c == 0), stop=(c == KC - 1))
                    return last
                P.step("tensor", gu_mms, wait=mmw, inc=[("tE", 1)])
                ev[("guq_mm", g, e_i, qq)] = C["tE"]
                ev[("gu_mm", g, e_i, fc)] = C["tE"]

                s2w = [("tE", ev[("gu_mm", g, e_i, fc)])]
                if fi >= 1:
                    pg2, pe2f = divmod(fi - 1, FC * NE)
                    pe2, pfc = divmod(pe2f, FC)
                    s2w.append(("vE", ev[("at", pg2, pe2, pfc)]))
                P.step("scalar", lambda e, pp=pp: e.activation(sg_b, gps[pp], AF.Silu),
                       wait=s2w, inc=[("sE", 1)])
                ev[("silu2", g, e_i, fc)] = C["sE"]
                atw = [("sE", ev[("silu2", g, e_i, fc)])]
                if e_i == 0 and fc == 0 and g > 0:
                    atw.append(ev[("ys", g - 1)])
                P.step("vector", lambda e, fc=fc, pp=pp: e.tensor_mul(
                    at_b[:, fc * GT:(fc + 1) * GT], sg_b, ups[pp]),
                    wait=atw, inc=[("vE", 1)])
                ev[("at", g, e_i, fc)] = C["vE"]

                if fc in (1, 3):
                    load_gu_quarter(g, e_i, fc // 2 + 2,
                                    [("tE", ev[("guq_mm", g, e_i, fc // 2)])])
                elif fc in (5, 7):
                    ng_, ne_ = (g, e_i + 1) if e_i + 1 < NE else (g + 1, 0)
                    if ng_ < NG:
                        load_gu_quarter(ng_, ne_, fc // 2 - 2,
                                        [("tE", ev[("guq_mm", g, e_i, fc // 2)])])

            for j in range(JT):
                for dc in range(4):
                    pp = (j * 4 + dc) % 2
                    dmw = [ev[("downl", g, e_i)],
                           ("vE", ev[("at", g, e_i, FC - 1)])]
                    prev = gi * 16 + j * 4 + dc - 2
                    if prev >= 0:
                        dmw.append(("vE", ev[("yup", prev)]))

                    def down_mms(e, j=j, dc=dc, pp=pp, dbi=dbi):
                        last = None
                        for fcc in range(FC):
                            last = e.matmul(
                                yps[pp],
                                at_b[:, fcc * GT + j * 128: fcc * GT + (j + 1) * 128],
                                down_b[dbi][:, fcc * D + dc * 512:
                                            fcc * D + (dc + 1) * 512],
                                start=(fcc == 0), stop=(fcc == FC - 1))
                        return last
                    P.step("tensor", down_mms, wait=dmw, inc=[("tE", 1)])
                    ev[("down_mm", g, e_i)] = C["tE"]
                    t_dmm = C["tE"]

                    cw_col = cw_res[:, (g * JT + j) * 8 + e_i:
                                    (g * JT + j) * 8 + e_i + 1]
                    ysl = y_g[:, j * D + dc * 512: j * D + (dc + 1) * 512]
                    yw = [("tE", t_dmm)]
                    if e_i == 0 and g > 0:
                        yw.append(ev[("ys", g - 1)])
                    if e_i == 0:
                        P.step("vector", lambda e, pp=pp, ysl=ysl, cw_col=cw_col:
                               e.tensor_scalar_mul(ysl, yps[pp], cw_col),
                               wait=yw, inc=[("vE", 1)])
                    else:
                        P.step("vector", lambda e, pp=pp, ysl=ysl, cw_col=cw_col:
                               e.scalar_tensor_tensor(ysl, yps[pp], cw_col, ysl,
                                                      ALU.mult, ALU.add),
                               wait=yw, inc=[("vE", 1)])
                    ev[("yup", gi * 16 + j * 4 + dc)] = C["vE"]
        ev[("ffn_mm", g)] = C["tE"]

        ev[("ys", g)] = P.dma(
            "sync",
            y_s[g * GT:(g + 1) * GT, :].rearrange("(j p) d -> p j d", p=128),
            y_g.rearrange("p (j d) -> p j d", j=JT),
            wait=[("vE", ev[("yup", (g * NE + NE - 1) * 16 + 15)])])

    P.barrier()
    marks["p2"] = P.mark()
    arena.reset()

    wo_sb = arena.alloc("wosb", [128, KC * D], BF16)
    qt_t = [arena.alloc(f"qtt{i}", [128, D], BF16) for i in range(2)]
    u_t = [arena.alloc(f"ut{i}", [128, D], BF16) for i in range(2)]
    x_t = arena.alloc("xt0", [128, D], F32)
    y_t = arena.alloc("yt0", [128, D], F32)
    o_t = arena.alloc("ot0", [128, D], F32)
    n_sb = arena.alloc("nsb", [128, D], F32)
    t1_sb = arena.alloc("t1sb", [128, D], F32)
    t4_sb = arena.alloc("t4sb", [128, D], F32)
    ug_sb = arena.alloc("ugsb", [128, D], F32)
    ub_sb = arena.alloc("ubsb", [128, D], F32)
    z_bf = arena.alloc("zbf", [128, D], BF16)
    zt_sb = [arena.alloc(f"ztsb{i}", [128, D], BF16) for i in range(2)]
    sg_c = [arena.alloc(f"sgc{i}", [128, 512], F32) for i in range(2)]
    sm3 = arena.alloc("sm3", [128, 8], F32)

    att_ps = PS[:, 0:2048]
    sps = [PS[:, 2048:2560], PS[:, 2560:3072]]
    ZPB = [PS6_BF, PS7_BF]

    ev[("wol",)] = P.dma("sync", wo_sb.rearrange("p (c d) -> p c d", c=KC),
                         wobf.rearrange("c p d -> p c d"),
                         wait=[conv_done["wo"]])

    def emit_s_combine(t):
        zi = t % 2
        sgw = [ev[("yl", t)]]
        if t > 0:
            sgw.append(("vE", ev[("t4", (t - 1) * 4 + 3)]))
        for dc in range(4):
            pp = dc % 2
            smw = [("vE", ev[("zte", t)]), ev[("wol",)]]
            prev_dc = t * 4 + dc - 2
            if prev_dc >= 0:
                smw.append(("vE", ev[("t4", prev_dc)]))

            def s_mms(e, dc=dc, pp=pp, zi=zi):
                e.matmul(sps[pp], ones_bf, bo_r[:, dc * 512:(dc + 1) * 512],
                         start=True, stop=False)
                last = None
                for c in range(KC):
                    last = e.matmul(sps[pp], zt_sb[zi][:, c * 128:(c + 1) * 128],
                                    wo_sb[:, c * D + dc * 512: c * D + (dc + 1) * 512],
                                    start=False, stop=(c == KC - 1))
                return last
            P.step("tensor", s_mms, wait=smw, inc=[("tE", 1)])
            ev[("s_mm", t, dc)] = C["tE"]

            if dc == 0:
                scw = list(sgw)
            elif dc == 1:
                scw = [("vE", ev[("t4", (t - 1) * 4 + 3)])] if t > 0 else []
            else:
                scw = [("vE", ev[("t4", t * 4 + dc - 2)])]
            P.step("scalar", lambda e, dc=dc: e.activation(
                sg_c[dc % 2], y_t[:, dc * 512:(dc + 1) * 512], AF.Silu),
                wait=scw)
            ev[("sgc", t, dc)] = C["sE"]

            t4w = [("tE", ev[("s_mm", t, dc)]), ("sE", ev[("sgc", t, dc)])]
            if prev_dc >= 0:
                t4w.append(("pE", ev[("comb_c", prev_dc)]))
            P.step("vector", lambda e, dc=dc, pp=pp: e.tensor_mul(
                t4_sb[:, dc * 512:(dc + 1) * 512], sps[pp], sg_c[dc % 2]),
                wait=t4w, inc=[("vE", 1)])
            ev[("t4", t * 4 + dc)] = C["vE"]

            cbw = [("vE", ev[("t4", t * 4 + dc)]), ev[("xl", t)]]
            if t >= 1 and dc == 0:
                cbw.append(ev[("outs", t - 1)])
            P.step("gpsimd", lambda e, dc=dc: e.tensor_add(
                o_t[:, dc * 512:(dc + 1) * 512],
                t4_sb[:, dc * 512:(dc + 1) * 512],
                x_t[:, dc * 512:(dc + 1) * 512]),
                wait=cbw, inc=[("pE", 1)])
            ev[("comb_c", t * 4 + dc)] = C["pE"]
        ev[("comb", t)] = C["pE"]
        ev[("outs", t)] = P.dma("sync", out_d[t * TT:(t + 1) * TT, :], o_t,
                                wait=[("pE", ev[("comb", t)])])
        if t + 1 < NT:
            ev[("xl", t + 1)] = P.dma(
                "sync", x_t, x_d[(t + 1) * TT:(t + 2) * TT, :],
                wait=[("pE", ev[("comb", t)])])
            ev[("yl", t + 1)] = P.dma(
                "sync", y_t, y_s[(t + 1) * TT:(t + 2) * TT, :],
                wait=[("sE", ev[("sgc", t, 3)]), ev[("ys", (t + 1) // JT)]])

    ev[("xl", 0)] = P.dma("sync", x_t, x_d[0:TT, :])
    ev[("yl", 0)] = P.dma("sync", y_t, y_s[0:TT, :], wait=[ev[("ys", 0)]])

    for t in range(NT):
        bi = t % 2
        lw = [("tE", ev[("att", t - 2)])] if t >= 2 else []
        ev[("qtl0", t)] = P.dma("sync", qt_t[bi][:, 0:HALF], qt_s[t, 0], wait=lw)
        ev[("qtl", t)] = P.dma("sync", qt_t[bi][:, HALF:D], qt_s[t, 1], wait=lw)
        uw = [("pE", ev[("ug", t - 2)])] if t >= 2 else []
        ev[("ul0", t)] = P.dma("sync", u_t[bi][:, 0:HALF], u_s[t, 0], wait=uw)
        ev[("ul", t)] = P.dma("sync", u_t[bi][:, HALF:D], u_s[t, 1], wait=uw)

        aw = [ev[("qtl0", t)], ev[("qtl", t)]]
        if t > 0:
            aw.append(("vE", ev[("nev", t - 1)]))

        def att_mms(e, bi=bi):
            last = None
            for hh in range(NH):
                last = e.matmul(att_ps[:, hh * 128:(hh + 1) * 128],
                                qt_t[bi][:, hh * 128:(hh + 1) * 128],
                                kv_bf[:, hh * 128:(hh + 1) * 128],
                                start=True, stop=True)
            return last
        P.step("tensor", att_mms, wait=aw, inc=[("tE", 1)])
        ev[("att", t)] = C["tE"]
        marks[f"p3_att{t}"] = P.mark()

        ugw = [ev[("ul0", t)], ev[("ul", t)]]
        if t > 0:
            ugw.append(("vE", ev[("tmul", t - 1)]))
        P.step("gpsimd", lambda e, bi=bi: e.tensor_mul(ug_sb, u_t[bi], g_rep),
               wait=ugw)
        zbw = [("vE", ev[("z", t - 1)])] if t > 0 else []
        P.step("gpsimd", lambda e, bi=bi: e.tensor_mul(ub_sb, u_t[bi], b_rep),
               wait=zbw, inc=[("pE", 1)])
        ev[("ug", t)] = C["pE"]

        if t >= 1:
            emit_s_combine(t - 1)
            marks[f"p3_comb{t-1}"] = P.mark()
        sqw = [("tE", ev[("att", t)])]
        if t > 0:
            sqw.append(("vE", ev[("tmul", t - 1)]))
        P.step("scalar", lambda e: e.activation(
            n_sb, att_ps, AF.Identity, accum_out=sm3[:, 0:1]), wait=sqw)
        P.step("scalar", lambda e: e.activation(
            n_sb, att_ps, AF.Square, accum_out=sm3[:, 1:2]))
        ev[("asq", t)] = C["sE"]

        P.step("scalar", lambda e: e.activation(sm3[:, 7:8], sm3[:, 0:1], AF.Square))
        P.step("scalar", lambda e: e.activation(
            sm3[:, 4:5], sm3[:, 7:8], AF.Relu, scale=-1.0 / D, bias=sm3[:, 1:2]))
        P.step("scalar", lambda e: e.activation(
            sm3[:, 3:4], sm3[:, 4:5], AF.Ln, bias=ceps[:, 0:1]))
        P.step("scalar", lambda e: e.activation(
            sm3[:, 5:6], sm3[:, 3:4], AF.Exp, scale=-0.5, bias=ceps[:, 3:4]))
        P.step("scalar", lambda e: e.mul(sm3[:, 2:3], sm3[:, 0:1], 1.0 / D))
        ev[("l3e", t)] = C["sE"]
        marks[f"p3_nev{t}"] = P.mark()

        P.step("vector", lambda e: e.tensor_scalar(
            n_sb, att_ps, sm3[:, 2:3], sm3[:, 5:6], ALU.subtract, ALU.mult),
            wait=[("sE", ev[("l3e", t)])], inc=[("vE", 1)])
        ev[("nev", t)] = C["vE"]

        P.step("vector", lambda e: e.tensor_mul(t1_sb, n_sb, ug_sb),
               wait=[("pE", ev[("ug", t)])], inc=[("vE", 1)])
        ev[("tmul", t)] = C["vE"]
        zw = []
        if t > 0:
            zw.append(("tE", ev[("zpg", t - 1, 3)]))
        P.step("vector", lambda e: e.tensor_add(z_bf, t1_sb, ub_sb),
               wait=zw, inc=[("vE", 1)])
        ev[("z", t)] = C["vE"]
        marks[f"p3_t1b{t}"] = P.mark()

        zi = t % 2
        for g4 in range(4):
            zpw = [("vE", ev[("z", t)])]
            if ("zpe", g4 % 2) in ev:
                zpw.append(("vE", ev[("zpe", g4 % 2)]))

            def zpose(e, g4=g4):
                last = None
                for k in range(4):
                    c = g4 * 4 + k
                    last = e.transpose(
                        ZPB[g4 % 2][:, k * 128:(k + 1) * 128],
                        z_bf[:, c * 128:(c + 1) * 128], id_bf)
                return last
            P.step("tensor", zpose, wait=zpw, inc=[("tE", 1)])
            ev[("zpg", t, g4)] = C["tE"]
            zew = [("tE", ev[("zpg", t, g4)])]
            if g4 < 2 and t >= 2:
                zew.append(("tE", ev[("s_mm", t - 2, 3)]))
            P.step("vector", lambda e, g4=g4, zi=zi: e.tensor_copy(
                zt_sb[zi][:, g4 * 512:(g4 + 1) * 512],
                ZPB[g4 % 2][:, 0:512]),
                wait=zew, inc=[("vE", 1)])
            ev[("zpe", g4 % 2)] = C["vE"]
        ev[("zte", t)] = C["vE"]
        marks[f"p3_zte{t}"] = P.mark()

    emit_s_combine(NT - 1)

    P.step("sync", lambda e: e.nop(), wait=P.all_dma_events())

    with nc.Block() as block:
        P.emit(block, upto=None if upto is None else marks[upto])
    return nc


def kernel(L0_attn_x, L0_attn_Wqk, L0_attn_bqk, L0_attn_Wv, L0_attn_bv,
           L0_attn_Wu, L0_attn_bu, L0_attn_Wo, L0_attn_bo, L0_attn_ln_g,
           L0_attn_ln_b, L0_ffn_router_w, L0_ffn_norm_w, L0_ffn_gate_w,
           L0_ffn_up_w, L0_ffn_down_w):
    nc = build()
    shared = {
        "wqk": np.ascontiguousarray(L0_attn_Wqk),
        "bqk": np.ascontiguousarray(L0_attn_bqk),
        "wv": np.ascontiguousarray(L0_attn_Wv),
        "bv": np.ascontiguousarray(L0_attn_bv),
        "wu": np.ascontiguousarray(L0_attn_Wu),
        "bu": np.ascontiguousarray(L0_attn_bu),
        "wo": np.ascontiguousarray(L0_attn_Wo),
        "bo": np.ascontiguousarray(L0_attn_bo),
        "lng": np.ascontiguousarray(L0_attn_ln_g),
        "lnb": np.ascontiguousarray(L0_attn_ln_b),
        "rw": np.ascontiguousarray(L0_ffn_router_w),
        "nw": np.ascontiguousarray(L0_ffn_norm_w),
        "gw": np.ascontiguousarray(L0_ffn_gate_w),
        "uw": np.ascontiguousarray(L0_ffn_up_w),
        "dw": np.ascontiguousarray(L0_ffn_down_w),
    }
    in_maps = [dict(shared, x=np.ascontiguousarray(L0_attn_x[i]))
               for i in range(NCORES)]
    res = run_bass_kernel_spmd(nc, in_maps, core_ids=list(range(NCORES)))
    return np.stack([r["out"] for r in res.results], axis=0)


# version 09
# speedup vs baseline: 1.1436x; 1.1019x over previous
import math
import numpy as np
import concourse.bass as bass
import concourse.mybir as mybir
from concourse.bass_utils import run_bass_kernel_spmd

F32 = mybir.dt.float32
BF16 = mybir.dt.bfloat16
AF = mybir.ActivationFunctionType
ALU = mybir.AluOpType
AX = mybir.AxisListType

NCORES = 8
D = 2048
SEQ = 4096
TT = 128
NT = SEQ // TT
KC = 16
HALF = 1024
NH = 16
NE = 8
FE = 1024
FC = 8
GT = 512
NG = SEQ // GT
JT = GT // TT

LN_EPS = 1e-5
RMS_EPS = 1e-6
MOE_SCALE = 4.0

DEBUG_SERIAL = False
HALF_LN_D = 0.5 * math.log(float(2048))

SB_RIGHT_END = 229344
ARENA_BYTES = 178 * 1024 + 256
ARENA_RESERVE = 16 * 1024


class Planner:
    ENGINES = ("sync", "gpsimd", "vector", "scalar", "tensor")

    def __init__(self, nc):
        self.nc = nc
        self.items = {e: [] for e in self.ENGINES}
        self.sems = {}
        self.cnt = {}

    def sem(self, name):
        self.sems[name] = self.nc.alloc_semaphore(name)
        self.cnt[name] = 0

    SERIAL_SEM = {"vector": "vE", "scalar": "sE", "gpsimd": "pE"}

    def step(self, engine, fn, wait=(), inc=(), _dma=False):
        wait = list(wait)
        inc = list(inc)
        own = self.SERIAL_SEM.get(engine)
        if own is not None and not _dma:
            wait.append((own, self.cnt[own]))
            if not any(s == own for s, _ in inc):
                inc.append((own, 1))
        wait = [(s, v) for (s, v) in wait if v > 0]
        if DEBUG_SERIAL:
            wait = [(s, self.cnt[s]) for s in self.cnt if self.cnt[s] > 0]
        self.items[engine].append((fn, tuple(wait), tuple(inc)))
        for s, a in inc:
            self.cnt[s] += a

    DMA_LANES = {"sync": 8, "scalar": 4, "gpsimd": 6}

    def init_lanes(self):
        self.lanes = {}
        self.lane_rr = {}
        for eng, n in self.DMA_LANES.items():
            pfx = {"sync": "dS", "scalar": "dA", "gpsimd": "dG"}[eng]
            self.lanes[eng] = [f"{pfx}{i}" for i in range(n)]
            for s in self.lanes[eng]:
                self.sem(s)
            self.lane_rr[eng] = 0

    def dma(self, engine, out, in_, wait=(), **kw):
        lane = self.lanes[engine][self.lane_rr[engine] % len(self.lanes[engine])]
        self.lane_rr[engine] += 1
        wait = list(wait) + [(lane, self.cnt[lane])]
        self.step(engine, lambda e: e.dma_start(out=out, in_=in_, **kw),
                  wait=wait, inc=[(lane, 16)], _dma=True)
        return (lane, self.cnt[lane])

    def all_lanes(self, engine):
        return [(s, self.cnt[s]) for s in self.lanes[engine]]

    def all_dma_events(self):
        out = []
        for eng in self.lanes:
            out += self.all_lanes(eng)
        return out

    def barrier(self):
        eng_sem = {"sync": "bS", "gpsimd": "bG", "vector": "bV",
                   "scalar": "bA", "tensor": "bT"}
        for e in self.ENGINES:
            w = self.all_lanes(e) if (e in self.lanes and e != "gpsimd") else []
            if e in self.SERIAL_SEM:
                w.append((self.SERIAL_SEM[e], self.cnt[self.SERIAL_SEM[e]]))
            self.step(e, lambda eng: eng.nop(), wait=w, inc=[(eng_sem[e], 1)],
                      _dma=True)
        targets = [(s, self.cnt[s]) for s in eng_sem.values()]
        for e in self.ENGINES:
            self.step(e, lambda eng: eng.nop(), wait=targets, _dma=True)

    def mark(self):
        return {e: len(self.items[e]) for e in self.ENGINES}

    def emit(self, block, upto=None):
        def make(engine):
            def run(eng):
                last = {}
                items = self.items[engine]
                if upto is not None:
                    items = items[:upto[engine]]
                for fn, waits, incs in items:
                    for s, v in waits:
                        if last.get(s, 0) >= v:
                            continue
                        eng.wait_ge(self.sems[s], v)
                        last[s] = v
                    inst = fn(eng)
                    for s, a in incs:
                        inst = inst.then_inc(self.sems[s], a)
            return run
        block.sync(make("sync"))
        block.gpsimd(make("gpsimd"))
        block.vector(make("vector"))
        block.scalar(make("scalar"))
        block.tensor(make("tensor"))


class Arena:
    def __init__(self, nc, base, size):
        self.nc, self.base, self.off, self.n = nc, base, base, 0
        self.limit = base + size

    def alloc(self, name, shape, dtype):
        esz = 2 if dtype == BF16 else 4
        nbytes = shape[1] * esz
        off = (self.off + 127) // 128 * 128
        assert off + nbytes <= self.limit, (
            f"arena overflow: {name} needs {nbytes}B at +{off - self.base}")
        self.n += 1
        t = self.nc.alloc_sbuf_tensor_at(f"{name}_{self.n}", shape, dtype, offset=off)
        self.off = off + nbytes
        return t.ap()

    def reset(self):
        self.off = self.base


def build(upto=None, dbg=False):
    nc = bass.Bass("TRN2", target_bir_lowering=False, debug=False)

    x_d = nc.dram_tensor("x", [SEQ, D], F32, kind="ExternalInput").ap()
    wqk_d = nc.dram_tensor("wqk", [D, D], F32, kind="ExternalInput").ap()
    bqk_d = nc.dram_tensor("bqk", [D], F32, kind="ExternalInput").ap()
    wv_d = nc.dram_tensor("wv", [D, D], F32, kind="ExternalInput").ap()
    bv_d = nc.dram_tensor("bv", [D], F32, kind="ExternalInput").ap()
    wu_d = nc.dram_tensor("wu", [D, D], F32, kind="ExternalInput").ap()
    bu_d = nc.dram_tensor("bu", [D], F32, kind="ExternalInput").ap()
    wo_d = nc.dram_tensor("wo", [D, D], F32, kind="ExternalInput").ap()
    bo_d = nc.dram_tensor("bo", [D], F32, kind="ExternalInput").ap()
    lng_d = nc.dram_tensor("lng", [D], F32, kind="ExternalInput").ap()
    lnb_d = nc.dram_tensor("lnb", [D], F32, kind="ExternalInput").ap()
    rw_d = nc.dram_tensor("rw", [NE, D], F32, kind="ExternalInput").ap()
    nw_d = nc.dram_tensor("nw", [D], F32, kind="ExternalInput").ap()
    gw_d = nc.dram_tensor("gw", [NE, D, FE], F32, kind="ExternalInput").ap()
    uw_d = nc.dram_tensor("uw", [NE, D, FE], F32, kind="ExternalInput").ap()
    dw_d = nc.dram_tensor("dw", [NE, FE, D], F32, kind="ExternalInput").ap()
    out_d = nc.dram_tensor("out", [SEQ, D], F32, kind="ExternalOutput").ap()

    knd = "ExternalOutput" if dbg else "Internal"
    xnt_s = nc.dram_tensor("xnt_s", [KC, 128, SEQ], BF16, kind=knd).ap()
    wh1_s = nc.dram_tensor("wh1_s", [3, KC, 128, HALF], BF16, kind="Internal").ap()
    qt_s = nc.dram_tensor("qt_s", [NT, 2, 128, HALF], BF16, kind=knd).ap()
    u_s = nc.dram_tensor("u_s", [NT, 2, 128, HALF], BF16, kind=knd).ap()
    y_s = nc.dram_tensor("y_s", [SEQ, D], F32, kind=knd).ap()
    if dbg:
        dbg_cw = nc.dram_tensor("dbg_cw", [128, NT * 8], F32, kind="ExternalOutput").ap()
        dbg_kv = nc.dram_tensor("dbg_kv", [128, D], BF16, kind="ExternalOutput").ap()
        dbg_gr = nc.dram_tensor("dbg_gr", [128, 2 * D], F32, kind="ExternalOutput").ap()
    gbf = nc.dram_tensor("gbf", [NE, KC, 128, FE], BF16, kind="Internal").ap()
    ubf = nc.dram_tensor("ubf", [NE, KC, 128, FE], BF16, kind="Internal").ap()
    dbf = nc.dram_tensor("dbf", [NE, FC, 128, D], BF16, kind="Internal").ap()
    wobf = nc.dram_tensor("wobf", [KC, 128, D], BF16, kind="Internal").ap()

    def sb(name, shape, dtype):
        return nc.alloc_sbuf_tensor(name, shape, dtype).ap()

    id_bf = sb("id_bf", [128, 128], BF16)
    id_f32 = sb("id_f32", [128, 128], F32)
    iot = sb("iot", [128, 128], F32)
    ones_bf = sb("ones_bf", [1, 128], BF16)
    ones_f32 = sb("ones_f32", [1, 128], F32)
    bqk_r = sb("bqk_r", [1, D], BF16)
    bv_r = sb("bv_r", [1, D], BF16)
    bu_r = sb("bu_r", [1, D], BF16)
    bo_r = sb("bo_r", [1, D], BF16)
    nwT = sb("nwT", [128, 16], F32)
    g_rep = nc.alloc_sbuf_tensor_at(
        "g_rep", [128, D], F32, offset=SB_RIGHT_END - ARENA_RESERVE).ap()
    b_rep = nc.alloc_sbuf_tensor_at(
        "b_rep", [128, D], F32, offset=SB_RIGHT_END - ARENA_RESERVE + D * 4).ap()
    rwnT = sb("rwnT", [128, KC * 8], BF16)
    cw_res = sb("cw_res", [128, NT * 8], F32)
    ceps = sb("ceps", [128, 4], F32)
    kv_bf = sb("kv_bf", [128, D], BF16)
    lgf = sb("lgf", [128, 8], F32)
    sml = sb("sml", [128, 16], F32)
    cwt = sb("cwt", [128, 8], F32)

    PS = nc.alloc_psum_tensor("PS", [128, 4096], F32).ap()
    PS7_BF = PS[:, 3584:4096].bitcast(BF16)
    PS6_BF = PS[:, 3072:3584].bitcast(BF16)

    arena = Arena(nc, SB_RIGHT_END - ARENA_BYTES, ARENA_BYTES - ARENA_RESERVE)
    wq_sb = arena.alloc("wq", [128, KC * HALF], BF16)
    wv_sb = arena.alloc("wvs", [128, KC * HALF], BF16)
    wu_sb = arena.alloc("wus", [128, KC * HALF], BF16)

    P = Planner(nc)
    for s in ("tE", "vE", "sE", "pE", "bS", "bG", "bV", "bA", "bT"):
        P.sem(s)
    P.init_lanes()
    C = P.cnt
    ev = {}

    W_END = arena.off
    lng_row = arena.alloc("lng_row", [1, D], F32)
    lnb_row = arena.alloc("lnb_row", [1, D], F32)
    rw_sb = arena.alloc("rw_sb", [8, D], F32)
    nw16 = arena.alloc("nw16", [16, 128], F32)
    w_ready = {0: [
        P.dma("gpsimd", wsb.rearrange("p (c n) -> p c n", c=KC),
              wd[:, 0:HALF].rearrange("(c p) n -> p c n", p=128))
        for wd, wsb in ((wqk_d, wq_sb), (wv_d, wv_sb), (wu_d, wu_sb))]}
    conv_done = {}
    conv_done["wh1"] = [
        P.dma("gpsimd", wh1_s[i],
              wd[:, HALF:D].rearrange("(c p) n -> c p n", p=128))
        for i, wd in enumerate((wqk_d, wv_d, wu_d))]

    d_setup = [
        P.dma("sync", lng_row, lng_d.rearrange("(o d) -> o d", o=1)),
        P.dma("sync", lnb_row, lnb_d.rearrange("(o d) -> o d", o=1)),
        P.dma("sync", rw_sb, rw_d),
        P.dma("sync", nw16, nw_d.rearrange("(c p) -> c p", p=128)),
    ]
    b_rows = [
        P.dma("gpsimd", bqk_r, bqk_d.rearrange("(o d) -> o d", o=1)),
        P.dma("gpsimd", bv_r, bv_d.rearrange("(o d) -> o d", o=1)),
        P.dma("gpsimd", bu_r, bu_d.rearrange("(o d) -> o d", o=1)),
        P.dma("gpsimd", bo_r, bo_d.rearrange("(o d) -> o d", o=1)),
    ]
    P.step("gpsimd", lambda e: e.memset(ceps[:, 0:1], D * LN_EPS))
    P.step("gpsimd", lambda e: e.memset(ceps[:, 1:2], RMS_EPS))
    P.step("gpsimd", lambda e: e.memset(ceps[:, 2:3], D * LN_EPS * RMS_EPS))
    P.step("gpsimd", lambda e: e.memset(ceps[:, 3:4], HALF_LN_D))
    P.step("gpsimd", lambda e: e.memset(ones_bf, 1.0))
    P.step("gpsimd", lambda e: e.memset(ones_f32, 1.0))
    P.step("gpsimd", lambda e: e.iota(iot, [[1, 128]], channel_multiplier=-1,
                                      allow_small_or_imprecise_dtypes=True),
           inc=[("pE", 1)])
    pe_iota = C["pE"]
    P.step("vector", lambda e: e.tensor_single_scalar(id_bf, iot, 0.0, ALU.is_equal),
           wait=[("pE", pe_iota)])
    P.step("vector", lambda e: e.tensor_single_scalar(id_f32, iot, 0.0, ALU.is_equal),
           inc=[("vE", 1)])

    for ci in range(4):
        P.step("tensor",
               lambda e, ci=ci: (
                   e.matmul(PS[:, 0:512], ones_f32,
                            lng_row[:, ci * 512:(ci + 1) * 512], start=True, stop=True),
                   e.matmul(PS[:, 512:1024], ones_f32,
                            lnb_row[:, ci * 512:(ci + 1) * 512], start=True, stop=True))[-1],
               wait=d_setup + [("vE", C["vE"])], inc=[("tE", 1)])
        t_rep = C["tE"]
        P.step("vector", lambda e, ci=ci: e.tensor_copy(
            g_rep[:, ci * 512:(ci + 1) * 512], PS[:, 0:512]), wait=[("tE", t_rep)])
        P.step("vector", lambda e, ci=ci: e.tensor_copy(
            b_rep[:, ci * 512:(ci + 1) * 512], PS[:, 512:1024]),
            wait=[("tE", t_rep)], inc=[("vE", 1)])

    P.step("tensor", lambda e: e.transpose(PS[:, 1024:1040], nw16, id_f32[0:16, 0:16]),
           wait=d_setup + [("vE", C["vE"])], inc=[("tE", 1)])
    t_nw = C["tE"]
    P.step("vector", lambda e: e.tensor_copy(nwT, PS[:, 1024:1040]),
           wait=[("tE", t_nw)], inc=[("vE", 1)])
    v_nwT = C["vE"]
    for c in range(KC):
        P.step("tensor", lambda e, c=c: e.transpose(
            PS[:, 1536 + (c % 2) * 512: 1544 + (c % 2) * 512],
            rw_sb[0:8, c * 128:(c + 1) * 128], id_f32[0:8, 0:8]),
            wait=[("vE", C["vE"])], inc=[("tE", 1)])
        t_rw = C["tE"]
        P.step("vector", lambda e, c=c: e.tensor_scalar_mul(
            rwnT[:, c * 8:(c + 1) * 8],
            PS[:, 1536 + (c % 2) * 512: 1544 + (c % 2) * 512], nwT[:, c:c + 1]),
            wait=[("tE", t_rw), ("vE", v_nwT)], inc=[("vE", 1)])

    P.barrier()
    marks = {"setup": P.mark()}
    ARENA_BASE0 = arena.base
    arena.base = arena.off = ((W_END + 127) // 128 * 128)


    def convert_expert(e, wait=()):
        e1 = P.dma("gpsimd", gbf[e], gw_d[e].rearrange("(c p) f -> c p f", p=128),
                   wait=wait)
        e2 = P.dma("gpsimd", ubf[e], uw_d[e].rearrange("(c p) f -> c p f", p=128))
        e3 = P.dma("gpsimd", dbf[e], dw_d[e].rearrange("(c p) d -> c p d", p=128))
        conv_done[e] = [e1, e2, e3]

    x_buf = [arena.alloc(f"x{i}", [128, D], F32) for i in range(2)]
    xn_bf = [arena.alloc(f"xnbf{i}", [128, D], BF16) for i in range(2)]
    xnt_b = [arena.alloc(f"xnt{i}", [128, D], BF16) for i in range(2)]
    q_bf = [arena.alloc(f"qbf{i}", [128, HALF], BF16) for i in range(2)]
    v_bf = arena.alloc("vbf", [128, HALF], BF16)
    u_bf = [arena.alloc(f"ubfb{i}", [128, HALF], BF16) for i in range(2)]
    qt_b = [arena.alloc(f"qtb{i}", [128, HALF], BF16) for i in range(2)]
    el_m = arena.alloc("elm", [128, HALF], F32)
    el_e = arena.alloc("ele", [128, HALF], F32)
    el_r = arena.alloc("elr", [128, HALF], F32)
    kv_acc = arena.alloc("kvacc", [128, D], F32)

    q_ps = PS[:, 0:1024]
    v_ps = PS[:, 1024:2048]
    u_ps = PS[:, 2048:3072]
    log_ps = PS[:, 3072:3080]
    QPB = [PS[:, 1024:1536].bitcast(BF16), PS[:, 1536:2048].bitcast(BF16)]

    def load_w_half(h):
        evs = []
        if True:
            w = [("tE", ev[("mms", 0, NT - 1)])] + list(conv_done["wh1"])
            for i, wsb in enumerate((wq_sb, wv_sb, wu_sb)):
                evs.append(P.dma(
                    "sync", wsb.rearrange("p (c n) -> p c n", c=KC),
                    wh1_s[i].rearrange("c p n -> p c n"), wait=w))
        return evs


    for _e in range(3):
        convert_expert(_e)

    def emit_ln(t):
        bi = t % 2
        w = []
        if t >= 2:
            w = [("vE", ev[("xnbf", t - 2)]), ("sE", ev[("xsq", t - 2)])]
        ev[("x", t)] = P.dma("sync", x_buf[bi],
                             x_d[t * TT:(t + 1) * TT, :], wait=w)
        idw = [ev[("x", t)]]
        if t >= 2:
            idw.append(("tE", ev[("xpg", t - 2, 1)]))
        P.step("scalar", lambda e, bi=bi: e.activation(
            xn_bf[bi], x_buf[bi], AF.Identity, accum_out=sml[:, 0:1]),
            wait=idw)
        P.step("scalar", lambda e, bi=bi: e.activation(
            xn_bf[bi], x_buf[bi], AF.Square, accum_out=sml[:, 1:2]))
        ev[("xsq", t)] = C["sE"]

        lsw = [("sE", ev[("xsq", t)])]
        if ("lgfe", t - 1) in ev:
            lsw.append(("vE", ev[("lgfe", t - 1)]))
        P.step("scalar", lambda e: e.activation(sml[:, 8:9], sml[:, 0:1], AF.Square),
               wait=lsw)
        P.step("scalar", lambda e: e.activation(
            sml[:, 4:5], sml[:, 8:9], AF.Relu, scale=-1.0 / D, bias=sml[:, 1:2]))
        P.step("scalar", lambda e: e.activation(
            sml[:, 7:8], sml[:, 4:5], AF.Ln, bias=ceps[:, 0:1]))
        P.step("scalar", lambda e: e.activation(
            sml[:, 5:6], sml[:, 7:8], AF.Exp, scale=-0.5, bias=ceps[:, 3:4]))
        P.step("scalar", lambda e: e.mul(sml[:, 2:3], sml[:, 0:1], 1.0 / D))
        P.step("scalar", lambda e: e.mul(sml[:, 9:10], sml[:, 7:8], 0.5))
        P.step("scalar", lambda e: e.activation(
            sml[:, 10:11], sml[:, 4:5], AF.Ln, scale=1.0 + RMS_EPS,
            bias=ceps[:, 2:3]))
        P.step("scalar", lambda e: e.activation(
            sml[:, 11:12], sml[:, 10:11], AF.Exp, scale=-0.5, bias=sml[:, 9:10]))
        ev[("lnexp", t)] = C["sE"]
        P.step("vector", lambda e, bi=bi: e.tensor_scalar(
            xn_bf[bi], x_buf[bi], sml[:, 2:3], sml[:, 5:6],
            ALU.subtract, ALU.mult),
            wait=[("sE", ev[("lnexp", t)])], inc=[("vE", 1)])
        ev[("xnbf", t)] = C["vE"]

    def emit_xpose(tn):
        bj = tn % 2
        xbanks = [PS7_BF, PS6_BF]
        for g2 in range(2):
            xw = [("vE", ev[("xnbf", tn)])]
            if ("xpfree", g2) in ev:
                xw.append(("vE", ev[("xpfree", g2)]))
            if g2 == 1 and ("lgfe", tn - 1) in ev:
                xw.append(("vE", ev[("lgfe", tn - 1)]))

            def xpose(e, g2=g2, bj=bj):
                last = None
                for k in range(8):
                    c = g2 * 8 + k
                    last = e.transpose(
                        xbanks[g2][:, k * 128:(k + 1) * 128],
                        xn_bf[bj][:, c * 128:(c + 1) * 128], id_bf)
                return last
            P.step("tensor", xpose, wait=xw, inc=[("tE", 1)])
            ev[("xpg", tn, g2)] = C["tE"]
            xew = [("tE", ev[("xpg", tn, g2)])]
            if g2 == 0 and tn >= 2:
                xew += [("tE", ev[("mms", 0, tn - 2)]),
                        ev[("xnts", tn - 2)]]
            P.step("vector", lambda e, g2=g2, bj=bj: e.tensor_copy(
                xnt_b[bj][:, g2 * 1024:(g2 + 1) * 1024],
                xbanks[g2][:, 0:1024]),
                wait=xew, inc=[("vE", 1)])
            ev[("xpfree", g2)] = C["vE"]
        ev[("xnt", tn)] = C["vE"]
        ev[("xnts", tn)] = P.dma(
            "sync",
            xnt_s[:, :, tn * TT:(tn + 1) * TT].rearrange("c p e -> p c e"),
            xnt_b[bj].rearrange("p (c e) -> p c e", c=KC),
            wait=[("vE", ev[("xnt", tn)])])

    for h in range(2):
        if h == 1:
            w_ready[1] = load_w_half(1)
            for e_i in range(3, NE):
                convert_expert(e_i)
            conv_done["wo"] = P.dma(
                "gpsimd", wobf, wo_d.rearrange("(c p) d -> c p d", p=128))

        if h == 0:
            emit_ln(0)
            emit_xpose(0)
        for t in range(NT):
            bi = t % 2
            if h == 1:
                if t >= 2:
                    w = [("tE", ev[("mms", 1, t - 2)])]
                else:
                    w = [("tE", ev[("mms", 0, NT - 1)]),
                         ev[("xnts", NT - 2 + t)]]
                ev[("xntl", t)] = P.dma(
                    "sync", xnt_b[bi].rearrange("p (c e) -> p c e", c=KC),
                    xnt_s[:, :, t * TT:(t + 1) * TT].rearrange("c p e -> p c e"),
                    wait=w)

            mmw = list(w_ready[h])
            if (h, t) == (0, 0):
                mmw += b_rows
            if h == 0:
                mmw.append(("vE", ev[("xnt", t)]))
            else:
                mmw.append(ev[("xntl", t)])
            if (h, t) != (0, 0):
                ph, pt = (h, t - 1) if t > 0 else (0, NT - 1)
                mmw += [("vE", ev[("elu_sub", ph, pt)]),
                        ("vE", ev[("qt", ph, pt)])]
                if ph == 0:
                    mmw.append(("vE", ev[("lgfe", pt)]))

            def wave_a(e, h=h, bi=bi):
                for ps_t, br in ((q_ps, bqk_r), (v_ps, bv_r)):
                    for s2 in range(2):
                        e.matmul(ps_t[:, s2 * 512:(s2 + 1) * 512], ones_bf,
                                 br[:, h * HALF + s2 * 512: h * HALF + (s2 + 1) * 512],
                                 start=True, stop=False)
                last = None
                for c in range(KC):
                    lhs = xnt_b[bi][:, c * 128:(c + 1) * 128]
                    for ps_t, wsb in ((q_ps, wq_sb), (v_ps, wv_sb)):
                        for s2 in range(2):
                            last = e.matmul(
                                ps_t[:, s2 * 512:(s2 + 1) * 512], lhs,
                                wsb[:, c * HALF + s2 * 512: c * HALF + (s2 + 1) * 512],
                                start=False, stop=(c == KC - 1))
                    if h == 0:
                        last = e.matmul(log_ps, lhs, rwnT[:, c * 8:(c + 1) * 8],
                                        start=(c == 0), stop=(c == KC - 1))
                return last
            P.step("tensor", wave_a, wait=mmw, inc=[("tE", 1)])
            ev[("mmsA", h, t)] = C["tE"]

            mbw = []
            if (h, t) != (0, 0):
                ph, pt = (h, t - 1) if t > 0 else (0, NT - 1)
                mbw.append(("vE", ev[("kv_evac", ph, pt)]))

            def wave_b(e, h=h, bi=bi):
                for s2 in range(2):
                    e.matmul(u_ps[:, s2 * 512:(s2 + 1) * 512], ones_bf,
                             bu_r[:, h * HALF + s2 * 512: h * HALF + (s2 + 1) * 512],
                             start=True, stop=False)
                last = None
                for c in range(KC):
                    lhs = xnt_b[bi][:, c * 128:(c + 1) * 128]
                    for s2 in range(2):
                        last = e.matmul(
                            u_ps[:, s2 * 512:(s2 + 1) * 512], lhs,
                            wu_sb[:, c * HALF + s2 * 512: c * HALF + (s2 + 1) * 512],
                            start=False, stop=(c == KC - 1))
                return last
            P.step("tensor", wave_b, wait=mbw, inc=[("tE", 1)])
            ev[("mms", h, t)] = C["tE"]

            P.step("vector", lambda e: e.tensor_scalar_min(el_m, q_ps, 0.0),
                   wait=[("tE", ev[("mmsA", h, t)])], inc=[("vE", 1)])
            ev[("elu_m", h, t)] = C["vE"]
            P.step("vector", lambda e: e.tensor_sub(el_r, q_ps, el_m),
                   inc=[("vE", 1)])
            ev[("elu_sub", h, t)] = C["vE"]
            P.step("scalar", lambda e: e.activation(el_e, el_m, AF.Exp),
                   wait=[("vE", ev[("elu_m", h, t)])], inc=[("sE", 1)])
            ev[("elu_e", h, t)] = C["sE"]
            P.step("vector", lambda e, bi=bi: e.scalar_tensor_tensor(
                q_bf[bi], el_e, -1.0, el_r, ALU.add, ALU.add),
                wait=[("sE", ev[("elu_e", h, t)])], inc=[("vE", 1)])
            ev[("elu", h, t)] = C["vE"]

            silw = [("tE", ev[("mmsA", h, t)])]
            if (h, t) != (0, 0):
                ph, pt = (h, t - 1) if t > 0 else (0, NT - 1)
                silw.append(("tE", ev[("kv", ph, pt)]))
            P.step("scalar", lambda e: e.activation(v_bf, v_ps, AF.Silu), wait=silw)
            ev[("silu_v", h, t)] = C["sE"]
            suw = [("tE", ev[("mms", h, t)])]
            if t >= 2:
                suw.append(ev[("us", h, t - 2)])
            elif h == 1:
                suw.append(ev[("us", 0, NT - 2 + t)])
            P.step("scalar", lambda e, bi=bi: e.activation(u_bf[bi], u_ps, AF.Silu),
                   wait=suw)
            ev[("silu", h, t)] = C["sE"]
            ev[("us", h, t)] = P.dma("sync", u_s[t, h], u_bf[bi],
                                     wait=[("sE", ev[("silu", h, t)])])

            def kv_mms(e, bi=bi):
                last = None
                for i in range(8):
                    last = e.matmul(u_ps[:, i * 128:(i + 1) * 128],
                                    q_bf[bi][:, i * 128:(i + 1) * 128],
                                    v_bf[:, i * 128:(i + 1) * 128],
                                    start=True, stop=True)
                return last
            P.step("tensor", kv_mms,
                   wait=[("vE", ev[("elu", h, t)]), ("sE", ev[("silu", h, t)])],
                   inc=[("tE", 1)])
            ev[("kv", h, t)] = C["tE"]
            if t == 0:
                P.step("vector", lambda e, h=h: e.tensor_copy(
                    kv_acc[:, h * HALF:(h + 1) * HALF], u_ps),
                    wait=[("tE", ev[("kv", h, t)])], inc=[("vE", 1)])
            else:
                P.step("vector", lambda e, h=h: e.tensor_add(
                    kv_acc[:, h * HALF:(h + 1) * HALF], u_ps,
                    kv_acc[:, h * HALF:(h + 1) * HALF]),
                    wait=[("tE", ev[("kv", h, t)])], inc=[("vE", 1)])
            ev[("kv_evac", h, t)] = C["vE"]

            qxw = [("vE", ev[("elu", h, t)]), ("sE", ev[("silu_v", h, t)])]
            if ("qtprev",) in ev:
                qxw.append(("vE", ev[("qtprev",)]))

            def qpose(e, bi=bi):
                last = None
                for k in range(8):
                    last = e.transpose(
                        QPB[0][:, k * 128:(k + 1) * 128],
                        q_bf[bi][:, k * 128:(k + 1) * 128], id_bf)
                return last
            P.step("tensor", qpose, wait=qxw, inc=[("tE", 1)])
            ev[("qpg", h, t)] = C["tE"]
            qsw = [("tE", ev[("qpg", h, t)])]
            if t >= 2:
                qsw.append(ev[("qts", h, t - 2)])
            elif h == 1:
                qsw.append(ev[("qts", 0, NT - 2 + t)])
            P.step("vector", lambda e, bi=bi: e.tensor_copy(
                qt_b[bi], QPB[0][:, 0:1024]),
                wait=qsw, inc=[("vE", 1)])
            ev[("qtprev",)] = C["vE"]
            ev[("qt", h, t)] = C["vE"]
            ev[("qts", h, t)] = P.dma("sync", qt_s[t, h], qt_b[bi],
                                      wait=[("vE", ev[("qt", h, t)])])

            if h == 0:
                P.step("vector", lambda e: e.tensor_scalar_mul(
                    lgf, log_ps, sml[:, 11:12]),
                    wait=[("tE", ev[("mmsA", 0, t)]), ("sE", ev[("lnexp", t)])],
                    inc=[("vE", 1)])
                ev[("lgfe", t)] = C["vE"]

            if h == 0 and t + 1 < NT:
                emit_ln(t + 1)
                emit_xpose(t + 1)

            if h == 0:
                P.step("vector", lambda e: e.reduce_max(
                    sml[:, 12:13], lgf, axis=AX.X))
                P.step("vector", lambda e: e.reduce_max(
                    sml[:, 13:14], lgf, axis=AX.X, negate=True))
                P.step("vector", lambda e: e.tensor_single_scalar(
                    cwt, lgf, sml[:, 12:13], ALU.is_ge))
                P.step("vector", lambda e: e.scalar_tensor_tensor(
                    cwt, cwt, 1e30, lgf, ALU.mult, ALU.subtract))
                P.step("vector", lambda e: e.tensor_reduce(
                    sml[:, 14:15], cwt, axis=AX.X, op=ALU.min, negate=True))
                ev[("cwv1", t)] = C["vE"]

                P.step("scalar", lambda e: e.activation(
                    cwt, lgf, AF.Exp, bias=sml[:, 13:14]),
                    wait=[("vE", ev[("cwv1", t)])])
                P.step("scalar", lambda e: e.activation(
                    sml[:, 15:16], sml[:, 14:15], AF.Exp, bias=sml[:, 13:14]))
                ev[("cws", t)] = C["sE"]

                P.step("vector", lambda e: e.tensor_scalar_add(
                    sml[:, 15:16], sml[:, 15:16], 1.0),
                    wait=[("sE", ev[("cws", t)])])
                P.step("vector", lambda e: e.reciprocal(
                    sml[:, 15:16], sml[:, 15:16]))
                P.step("vector", lambda e: e.tensor_scalar_mul(
                    sml[:, 15:16], sml[:, 15:16], MOE_SCALE))
                P.step("vector", lambda e: e.tensor_single_scalar(
                    lgf, lgf, sml[:, 14:15], ALU.is_ge))
                P.step("vector", lambda e: e.tensor_mul(cwt, cwt, lgf))
                P.step("vector", lambda e, t=t: e.tensor_scalar_mul(
                    cw_res[:, t * 8:(t + 1) * 8], cwt, sml[:, 15:16]))
                ev[("cw", t)] = C["vE"]

    P.step("vector", lambda e: e.tensor_copy(kv_bf, kv_acc),
           wait=[("vE", ev[("kv_evac", 1, NT - 1)])], inc=[("vE", 1)])

    P.barrier()
    marks["p1"] = P.mark()
    arena.base = ARENA_BASE0
    arena.reset()

    xng = arena.alloc("xng", [128, KC * GT], BF16)
    y_g = arena.alloc("yg", [128, JT * D], F32)
    gate_b = [arena.alloc(f"gateb{i}", [128, KC * 256], BF16) for i in range(2)]
    up_b = [arena.alloc(f"upb{i}", [128, KC * 256], BF16) for i in range(2)]
    down_b = [arena.alloc(f"downb{i}", [128, FC * D], BF16) for i in range(2)]
    at_b = arena.alloc("atb", [128, FC * GT], BF16)
    sg_b = arena.alloc("sgb", [128, GT], F32)

    gps = [PS[:, 0:512], PS[:, 512:1024]]
    ups = [PS[:, 1024:1536], PS[:, 1536:2048]]
    yp2 = [PS[:, 2048:3072], PS[:, 3072:4096]]

    def load_gu_quarter(g, e_i, qq, wait):
        qbi = qq % 2
        qw = list(conv_done[e_i]) + list(wait)
        ev[("gql", g, e_i, qq)] = P.dma(
            "sync", gate_b[qbi].rearrange("p (c f) -> p c f", c=KC),
            gbf[e_i][:, :, qq * 256:(qq + 1) * 256].rearrange("c p f -> p c f"),
            wait=qw)
        ev[("uql", g, e_i, qq)] = P.dma(
            "sync", up_b[qbi].rearrange("p (c f) -> p c f", c=KC),
            ubf[e_i][:, :, qq * 256:(qq + 1) * 256].rearrange("c p f -> p c f"),
            wait=qw)

    load_gu_quarter(0, 0, 0, [])
    load_gu_quarter(0, 0, 1, [])

    for g in range(NG):
        xw = [("tE", ev[("ffn_mm", g - 1)])] if g > 0 else []
        ev[("xng", g)] = P.dma(
            "sync", xng.rearrange("p (c e) -> p c e", c=KC),
            xnt_s[:, :, g * GT:(g + 1) * GT].rearrange("c p e -> p c e"), wait=xw)

        for e_i in range(NE):
            dbi = e_i % 2
            dw_w = list(conv_done[e_i])
            gi = g * NE + e_i
            if gi >= 2:
                pg, pe = divmod(gi - 2, NE)
                dw_w.append(("tE", ev[("down_mm", pg, pe)]))
            ev[("downl", g, e_i)] = P.dma(
                "sync", down_b[dbi].rearrange("p (c d) -> p c d", c=FC),
                dbf[e_i].rearrange("c p d -> p c d"), wait=dw_w)

            for fc in range(FC):
                qq = fc // 2
                qbi = qq % 2
                fo = (fc % 2) * 128
                pp = fc % 2
                fi = gi * FC + fc
                mmw = [ev[("gql", g, e_i, qq)], ev[("uql", g, e_i, qq)],
                       ev[("xng", g)]]
                if fi >= 2:
                    pg2, pe2f = divmod(fi - 2, FC * NE)
                    pe2, pfc = divmod(pe2f, FC)
                    mmw += [("sE", ev[("silu2", pg2, pe2, pfc)]),
                            ("vE", ev[("at", pg2, pe2, pfc)])]

                def gu_mms(e, qbi=qbi, fo=fo, pp=pp):
                    last = None
                    for c in range(KC):
                        last = e.matmul(gps[pp],
                                        gate_b[qbi][:, c * 256 + fo: c * 256 + fo + 128],
                                        xng[:, c * GT:(c + 1) * GT],
                                        start=(c == 0), stop=(c == KC - 1))
                    for c in range(KC):
                        last = e.matmul(ups[pp],
                                        up_b[qbi][:, c * 256 + fo: c * 256 + fo + 128],
                                        xng[:, c * GT:(c + 1) * GT],
                                        start=(c == 0), stop=(c == KC - 1))
                    return last
                P.step("tensor", gu_mms, wait=mmw, inc=[("tE", 1)])
                ev[("guq_mm", g, e_i, qq)] = C["tE"]
                ev[("gu_mm", g, e_i, fc)] = C["tE"]

                s2w = [("tE", ev[("gu_mm", g, e_i, fc)])]
                if fi >= 1:
                    pg2, pe2f = divmod(fi - 1, FC * NE)
                    pe2, pfc = divmod(pe2f, FC)
                    s2w.append(("vE", ev[("at", pg2, pe2, pfc)]))
                P.step("scalar", lambda e, pp=pp: e.activation(sg_b, gps[pp], AF.Silu),
                       wait=s2w, inc=[("sE", 1)])
                ev[("silu2", g, e_i, fc)] = C["sE"]
                atw = [("sE", ev[("silu2", g, e_i, fc)])]
                if e_i == 0 and fc == 0 and g > 0:
                    atw.append(ev[("ys", g - 1)])
                P.step("vector", lambda e, fc=fc, pp=pp: e.tensor_mul(
                    at_b[:, fc * GT:(fc + 1) * GT], sg_b, ups[pp]),
                    wait=atw, inc=[("vE", 1)])
                ev[("at", g, e_i, fc)] = C["vE"]

                if fc in (1, 3):
                    load_gu_quarter(g, e_i, fc // 2 + 2,
                                    [("tE", ev[("guq_mm", g, e_i, fc // 2)])])
                elif fc in (5, 7):
                    ng_, ne_ = (g, e_i + 1) if e_i + 1 < NE else (g + 1, 0)
                    if ng_ < NG:
                        load_gu_quarter(ng_, ne_, fc // 2 - 2,
                                        [("tE", ev[("guq_mm", g, e_i, fc // 2)])])

            for j in range(JT):
                for dh in range(2):
                    jdh = j * 2 + dh
                    pp = jdh % 2
                    dmw = [ev[("downl", g, e_i)],
                           ("vE", ev[("at", g, e_i, FC - 1)])]
                    prev = gi * 8 + jdh - 2
                    if prev >= 0:
                        dmw.append(("vE", ev[("yup", prev)]))

                    def down_mms(e, j=j, dh=dh, pp=pp, dbi=dbi):
                        last = None
                        for fcc in range(FC):
                            lhs = at_b[:, fcc * GT + j * 128: fcc * GT + (j + 1) * 128]
                            for q in range(2):
                                dc = dh * 2 + q
                                last = e.matmul(
                                    yp2[pp][:, q * 512:(q + 1) * 512], lhs,
                                    down_b[dbi][:, fcc * D + dc * 512:
                                                fcc * D + (dc + 1) * 512],
                                    start=(fcc == 0), stop=(fcc == FC - 1))
                        return last
                    P.step("tensor", down_mms, wait=dmw, inc=[("tE", 1)])
                    ev[("down_mm", g, e_i)] = C["tE"]
                    t_dmm = C["tE"]

                    cw_col = cw_res[:, (g * JT + j) * 8 + e_i:
                                    (g * JT + j) * 8 + e_i + 1]
                    for q in range(2):
                        dc = dh * 2 + q
                        ysl = y_g[:, j * D + dc * 512: j * D + (dc + 1) * 512]
                        yw = [("tE", t_dmm)]
                        if e_i == 0 and g > 0 and jdh == 0 and q == 0:
                            yw.append(ev[("ys", g - 1)])
                        if e_i == 0:
                            P.step("vector",
                                   lambda e, pp=pp, q=q, ysl=ysl, cw_col=cw_col:
                                   e.tensor_scalar_mul(
                                       ysl, yp2[pp][:, q * 512:(q + 1) * 512],
                                       cw_col),
                                   wait=yw, inc=[("vE", 1)])
                        else:
                            P.step("vector",
                                   lambda e, pp=pp, q=q, ysl=ysl, cw_col=cw_col:
                                   e.scalar_tensor_tensor(
                                       ysl, yp2[pp][:, q * 512:(q + 1) * 512],
                                       cw_col, ysl, ALU.mult, ALU.add),
                                   wait=yw, inc=[("vE", 1)])
                    ev[("yup", gi * 8 + jdh)] = C["vE"]
        ev[("ffn_mm", g)] = C["tE"]

        ev[("ys", g)] = P.dma(
            "sync",
            y_s[g * GT:(g + 1) * GT, :].rearrange("(j p) d -> p j d", p=128),
            y_g.rearrange("p (j d) -> p j d", j=JT),
            wait=[("vE", ev[("yup", (g * NE + NE - 1) * 8 + 7)])])

    P.barrier()
    marks["p2"] = P.mark()
    arena.reset()

    wo_sb = arena.alloc("wosb", [128, KC * D], BF16)
    qt_t = [arena.alloc(f"qtt{i}", [128, D], BF16) for i in range(2)]
    u_t = [arena.alloc(f"ut{i}", [128, D], BF16) for i in range(2)]
    x_t = arena.alloc("xt0", [128, D], F32)
    y_t = arena.alloc("yt0", [128, D], F32)
    o_t = arena.alloc("ot0", [128, D], F32)
    n_sb = arena.alloc("nsb", [128, D], F32)
    t1_sb = arena.alloc("t1sb", [128, D], F32)
    t4_sb = arena.alloc("t4sb", [128, D], F32)
    ug_sb = arena.alloc("ugsb", [128, D], F32)
    ub_sb = arena.alloc("ubsb", [128, D], F32)
    z_bf = arena.alloc("zbf", [128, D], BF16)
    zt_sb = [arena.alloc(f"ztsb{i}", [128, D], BF16) for i in range(2)]
    sg_c = [arena.alloc(f"sgc{i}", [128, 512], F32) for i in range(2)]
    sm3 = arena.alloc("sm3", [128, 8], F32)

    att_ps = PS[:, 0:2048]
    sps = [PS[:, 2048:2560], PS[:, 2560:3072]]
    ZPB = [PS6_BF, PS7_BF]

    ev[("wol",)] = P.dma("sync", wo_sb.rearrange("p (c d) -> p c d", c=KC),
                         wobf.rearrange("c p d -> p c d"),
                         wait=[conv_done["wo"]])

    def emit_s_combine(t):
        zi = t % 2
        sgw = [ev[("yl", t)]]
        if t > 0:
            sgw.append(("vE", ev[("t4", (t - 1) * 4 + 3)]))
        for dc in range(4):
            pp = dc % 2
            smw = [("vE", ev[("zte", t)]), ev[("wol",)]]
            prev_dc = t * 4 + dc - 2
            if prev_dc >= 0:
                smw.append(("vE", ev[("t4", prev_dc)]))

            def s_mms(e, dc=dc, pp=pp, zi=zi):
                e.matmul(sps[pp], ones_bf, bo_r[:, dc * 512:(dc + 1) * 512],
                         start=True, stop=False)
                last = None
                for c in range(KC):
                    last = e.matmul(sps[pp], zt_sb[zi][:, c * 128:(c + 1) * 128],
                                    wo_sb[:, c * D + dc * 512: c * D + (dc + 1) * 512],
                                    start=False, stop=(c == KC - 1))
                return last
            P.step("tensor", s_mms, wait=smw, inc=[("tE", 1)])
            ev[("s_mm", t, dc)] = C["tE"]

            if dc == 0:
                scw = list(sgw)
            elif dc == 1:
                scw = [("vE", ev[("t4", (t - 1) * 4 + 3)])] if t > 0 else []
            else:
                scw = [("vE", ev[("t4", t * 4 + dc - 2)])]
            P.step("scalar", lambda e, dc=dc: e.activation(
                sg_c[dc % 2], y_t[:, dc * 512:(dc + 1) * 512], AF.Silu),
                wait=scw)
            ev[("sgc", t, dc)] = C["sE"]

            t4w = [("tE", ev[("s_mm", t, dc)]), ("sE", ev[("sgc", t, dc)])]
            if prev_dc >= 0:
                t4w.append(("pE", ev[("comb_c", prev_dc)]))
            P.step("vector", lambda e, dc=dc, pp=pp: e.tensor_mul(
                t4_sb[:, dc * 512:(dc + 1) * 512], sps[pp], sg_c[dc % 2]),
                wait=t4w, inc=[("vE", 1)])
            ev[("t4", t * 4 + dc)] = C["vE"]

            cbw = [("vE", ev[("t4", t * 4 + dc)]), ev[("xl", t)]]
            if t >= 1 and dc == 0:
                cbw.append(ev[("outs", t - 1)])
            P.step("gpsimd", lambda e, dc=dc: e.tensor_add(
                o_t[:, dc * 512:(dc + 1) * 512],
                t4_sb[:, dc * 512:(dc + 1) * 512],
                x_t[:, dc * 512:(dc + 1) * 512]),
                wait=cbw, inc=[("pE", 1)])
            ev[("comb_c", t * 4 + dc)] = C["pE"]
        ev[("comb", t)] = C["pE"]
        ev[("outs", t)] = P.dma("sync", out_d[t * TT:(t + 1) * TT, :], o_t,
                                wait=[("pE", ev[("comb", t)])])
        if t + 1 < NT:
            ev[("xl", t + 1)] = P.dma(
                "sync", x_t, x_d[(t + 1) * TT:(t + 2) * TT, :],
                wait=[("pE", ev[("comb", t)])])
            ev[("yl", t + 1)] = P.dma(
                "sync", y_t, y_s[(t + 1) * TT:(t + 2) * TT, :],
                wait=[("sE", ev[("sgc", t, 3)]), ev[("ys", (t + 1) // JT)]])

    ev[("xl", 0)] = P.dma("sync", x_t, x_d[0:TT, :])
    ev[("yl", 0)] = P.dma("sync", y_t, y_s[0:TT, :], wait=[ev[("ys", 0)]])

    for t in range(NT):
        bi = t % 2
        lw = [("tE", ev[("att", t - 2)])] if t >= 2 else []
        ev[("qtl0", t)] = P.dma("sync", qt_t[bi][:, 0:HALF], qt_s[t, 0], wait=lw)
        ev[("qtl", t)] = P.dma("sync", qt_t[bi][:, HALF:D], qt_s[t, 1], wait=lw)
        uw = [("pE", ev[("ug", t - 2)])] if t >= 2 else []
        ev[("ul0", t)] = P.dma("sync", u_t[bi][:, 0:HALF], u_s[t, 0], wait=uw)
        ev[("ul", t)] = P.dma("sync", u_t[bi][:, HALF:D], u_s[t, 1], wait=uw)

        aw = [ev[("qtl0", t)], ev[("qtl", t)]]
        if t > 0:
            aw.append(("vE", ev[("nev", t - 1)]))

        def att_mms(e, bi=bi):
            last = None
            for hh in range(NH):
                last = e.matmul(att_ps[:, hh * 128:(hh + 1) * 128],
                                qt_t[bi][:, hh * 128:(hh + 1) * 128],
                                kv_bf[:, hh * 128:(hh + 1) * 128],
                                start=True, stop=True)
            return last
        P.step("tensor", att_mms, wait=aw, inc=[("tE", 1)])
        ev[("att", t)] = C["tE"]
        marks[f"p3_att{t}"] = P.mark()

        ugw = [ev[("ul0", t)], ev[("ul", t)]]
        if t > 0:
            ugw.append(("vE", ev[("tmul", t - 1)]))
        P.step("gpsimd", lambda e, bi=bi: e.tensor_mul(ug_sb, u_t[bi], g_rep),
               wait=ugw)
        zbw = [("vE", ev[("z", t - 1)])] if t > 0 else []
        P.step("gpsimd", lambda e, bi=bi: e.tensor_mul(ub_sb, u_t[bi], b_rep),
               wait=zbw, inc=[("pE", 1)])
        ev[("ug", t)] = C["pE"]

        if t >= 1:
            emit_s_combine(t - 1)
            marks[f"p3_comb{t-1}"] = P.mark()
        sqw = [("tE", ev[("att", t)])]
        if t > 0:
            sqw.append(("vE", ev[("tmul", t - 1)]))
        P.step("scalar", lambda e: e.activation(
            n_sb, att_ps, AF.Identity, accum_out=sm3[:, 0:1]), wait=sqw)
        P.step("scalar", lambda e: e.activation(
            n_sb, att_ps, AF.Square, accum_out=sm3[:, 1:2]))
        ev[("asq", t)] = C["sE"]

        P.step("scalar", lambda e: e.activation(sm3[:, 7:8], sm3[:, 0:1], AF.Square))
        P.step("scalar", lambda e: e.activation(
            sm3[:, 4:5], sm3[:, 7:8], AF.Relu, scale=-1.0 / D, bias=sm3[:, 1:2]))
        P.step("scalar", lambda e: e.activation(
            sm3[:, 3:4], sm3[:, 4:5], AF.Ln, bias=ceps[:, 0:1]))
        P.step("scalar", lambda e: e.activation(
            sm3[:, 5:6], sm3[:, 3:4], AF.Exp, scale=-0.5, bias=ceps[:, 3:4]))
        P.step("scalar", lambda e: e.mul(sm3[:, 2:3], sm3[:, 0:1], 1.0 / D))
        ev[("l3e", t)] = C["sE"]
        marks[f"p3_nev{t}"] = P.mark()

        P.step("vector", lambda e: e.tensor_scalar(
            n_sb, att_ps, sm3[:, 2:3], sm3[:, 5:6], ALU.subtract, ALU.mult),
            wait=[("sE", ev[("l3e", t)])], inc=[("vE", 1)])
        ev[("nev", t)] = C["vE"]

        P.step("vector", lambda e: e.tensor_mul(t1_sb, n_sb, ug_sb),
               wait=[("pE", ev[("ug", t)])], inc=[("vE", 1)])
        ev[("tmul", t)] = C["vE"]
        zw = []
        if t > 0:
            zw.append(("tE", ev[("zpg", t - 1, 3)]))
        P.step("vector", lambda e: e.tensor_add(z_bf, t1_sb, ub_sb),
               wait=zw, inc=[("vE", 1)])
        ev[("z", t)] = C["vE"]
        marks[f"p3_t1b{t}"] = P.mark()

        zi = t % 2
        for g4 in range(4):
            zpw = [("vE", ev[("z", t)])]
            if ("zpe", g4 % 2) in ev:
                zpw.append(("vE", ev[("zpe", g4 % 2)]))

            def zpose(e, g4=g4):
                last = None
                for k in range(4):
                    c = g4 * 4 + k
                    last = e.transpose(
                        ZPB[g4 % 2][:, k * 128:(k + 1) * 128],
                        z_bf[:, c * 128:(c + 1) * 128], id_bf)
                return last
            P.step("tensor", zpose, wait=zpw, inc=[("tE", 1)])
            ev[("zpg", t, g4)] = C["tE"]
            zew = [("tE", ev[("zpg", t, g4)])]
            if g4 < 2 and t >= 2:
                zew.append(("tE", ev[("s_mm", t - 2, 3)]))
            P.step("vector", lambda e, g4=g4, zi=zi: e.tensor_copy(
                zt_sb[zi][:, g4 * 512:(g4 + 1) * 512],
                ZPB[g4 % 2][:, 0:512]),
                wait=zew, inc=[("vE", 1)])
            ev[("zpe", g4 % 2)] = C["vE"]
        ev[("zte", t)] = C["vE"]
        marks[f"p3_zte{t}"] = P.mark()

    emit_s_combine(NT - 1)

    P.step("sync", lambda e: e.nop(), wait=P.all_dma_events())

    with nc.Block() as block:
        P.emit(block, upto=None if upto is None else marks[upto])
    return nc


def kernel(L0_attn_x, L0_attn_Wqk, L0_attn_bqk, L0_attn_Wv, L0_attn_bv,
           L0_attn_Wu, L0_attn_bu, L0_attn_Wo, L0_attn_bo, L0_attn_ln_g,
           L0_attn_ln_b, L0_ffn_router_w, L0_ffn_norm_w, L0_ffn_gate_w,
           L0_ffn_up_w, L0_ffn_down_w):
    nc = build()
    shared = {
        "wqk": np.ascontiguousarray(L0_attn_Wqk),
        "bqk": np.ascontiguousarray(L0_attn_bqk),
        "wv": np.ascontiguousarray(L0_attn_Wv),
        "bv": np.ascontiguousarray(L0_attn_bv),
        "wu": np.ascontiguousarray(L0_attn_Wu),
        "bu": np.ascontiguousarray(L0_attn_bu),
        "wo": np.ascontiguousarray(L0_attn_Wo),
        "bo": np.ascontiguousarray(L0_attn_bo),
        "lng": np.ascontiguousarray(L0_attn_ln_g),
        "lnb": np.ascontiguousarray(L0_attn_ln_b),
        "rw": np.ascontiguousarray(L0_ffn_router_w),
        "nw": np.ascontiguousarray(L0_ffn_norm_w),
        "gw": np.ascontiguousarray(L0_ffn_gate_w),
        "uw": np.ascontiguousarray(L0_ffn_up_w),
        "dw": np.ascontiguousarray(L0_ffn_down_w),
    }
    in_maps = [dict(shared, x=np.ascontiguousarray(L0_attn_x[i]))
               for i in range(NCORES)]
    res = run_bass_kernel_spmd(nc, in_maps, core_ids=list(range(NCORES)))
    return np.stack([r["out"] for r in res.results], axis=0)


# version 10
# speedup vs baseline: 1.1615x; 1.0156x over previous
import math
import numpy as np
import concourse.bass as bass
import concourse.mybir as mybir
from concourse.bass_utils import run_bass_kernel_spmd

F32 = mybir.dt.float32
BF16 = mybir.dt.bfloat16
AF = mybir.ActivationFunctionType
ALU = mybir.AluOpType
AX = mybir.AxisListType

NCORES = 8
D = 2048
SEQ = 4096
TT = 128
NT = SEQ // TT
KC = 16
HALF = 1024
NH = 16
NE = 8
FE = 1024
FC = 8
GT = 512
NG = SEQ // GT
JT = GT // TT

LN_EPS = 1e-5
RMS_EPS = 1e-6
MOE_SCALE = 4.0

DEBUG_SERIAL = False
HALF_LN_D = 0.5 * math.log(float(2048))

SB_RIGHT_END = 229344
ARENA_BYTES = 178 * 1024 + 256
ARENA_RESERVE = 16 * 1024


class Planner:
    ENGINES = ("sync", "gpsimd", "vector", "scalar", "tensor")

    def __init__(self, nc):
        self.nc = nc
        self.items = {e: [] for e in self.ENGINES}
        self.sems = {}
        self.cnt = {}

    def sem(self, name):
        self.sems[name] = self.nc.alloc_semaphore(name)
        self.cnt[name] = 0

    SERIAL_SEM = {"vector": "vE", "scalar": "sE", "gpsimd": "pE"}

    def step(self, engine, fn, wait=(), inc=(), _dma=False):
        wait = list(wait)
        inc = list(inc)
        own = self.SERIAL_SEM.get(engine)
        if own is not None and not _dma:
            wait.append((own, self.cnt[own]))
            if not any(s == own for s, _ in inc):
                inc.append((own, 1))
        wait = [(s, v) for (s, v) in wait if v > 0]
        if DEBUG_SERIAL:
            wait = [(s, self.cnt[s]) for s in self.cnt if self.cnt[s] > 0]
        self.items[engine].append((fn, tuple(wait), tuple(inc)))
        for s, a in inc:
            self.cnt[s] += a

    DMA_LANES = {"sync": 8, "scalar": 4, "gpsimd": 6}

    def init_lanes(self):
        self.lanes = {}
        self.lane_rr = {}
        for eng, n in self.DMA_LANES.items():
            pfx = {"sync": "dS", "scalar": "dA", "gpsimd": "dG"}[eng]
            self.lanes[eng] = [f"{pfx}{i}" for i in range(n)]
            for s in self.lanes[eng]:
                self.sem(s)
            self.lane_rr[eng] = 0

    def dma(self, engine, out, in_, wait=(), **kw):
        lane = self.lanes[engine][self.lane_rr[engine] % len(self.lanes[engine])]
        self.lane_rr[engine] += 1
        wait = list(wait) + [(lane, self.cnt[lane])]
        self.step(engine, lambda e: e.dma_start(out=out, in_=in_, **kw),
                  wait=wait, inc=[(lane, 16)], _dma=True)
        return (lane, self.cnt[lane])

    def all_lanes(self, engine):
        return [(s, self.cnt[s]) for s in self.lanes[engine]]

    def all_dma_events(self):
        out = []
        for eng in self.lanes:
            out += self.all_lanes(eng)
        return out

    def barrier(self):
        eng_sem = {"sync": "bS", "gpsimd": "bG", "vector": "bV",
                   "scalar": "bA", "tensor": "bT"}
        for e in self.ENGINES:
            w = self.all_lanes(e) if (e in self.lanes and e != "gpsimd") else []
            if e in self.SERIAL_SEM:
                w.append((self.SERIAL_SEM[e], self.cnt[self.SERIAL_SEM[e]]))
            self.step(e, lambda eng: eng.nop(), wait=w, inc=[(eng_sem[e], 1)],
                      _dma=True)
        targets = [(s, self.cnt[s]) for s in eng_sem.values()]
        for e in self.ENGINES:
            self.step(e, lambda eng: eng.nop(), wait=targets, _dma=True)

    def mark(self):
        return {e: len(self.items[e]) for e in self.ENGINES}

    def emit(self, block, upto=None):
        def make(engine):
            def run(eng):
                last = {}
                items = self.items[engine]
                if upto is not None:
                    items = items[:upto[engine]]
                for fn, waits, incs in items:
                    for s, v in waits:
                        if last.get(s, 0) >= v:
                            continue
                        eng.wait_ge(self.sems[s], v)
                        last[s] = v
                    inst = fn(eng)
                    for s, a in incs:
                        inst = inst.then_inc(self.sems[s], a)
            return run
        block.sync(make("sync"))
        block.gpsimd(make("gpsimd"))
        block.vector(make("vector"))
        block.scalar(make("scalar"))
        block.tensor(make("tensor"))


class Arena:
    def __init__(self, nc, base, size):
        self.nc, self.base, self.off, self.n = nc, base, base, 0
        self.limit = base + size

    def alloc(self, name, shape, dtype):
        esz = 2 if dtype == BF16 else 4
        nbytes = shape[1] * esz
        off = (self.off + 127) // 128 * 128
        assert off + nbytes <= self.limit, (
            f"arena overflow: {name} needs {nbytes}B at +{off - self.base}")
        self.n += 1
        t = self.nc.alloc_sbuf_tensor_at(f"{name}_{self.n}", shape, dtype, offset=off)
        self.off = off + nbytes
        return t.ap()

    def reset(self):
        self.off = self.base


def build(upto=None, dbg=False):
    nc = bass.Bass("TRN2", target_bir_lowering=False, debug=False)

    x_d = nc.dram_tensor("x", [SEQ, D], F32, kind="ExternalInput").ap()
    wqk_d = nc.dram_tensor("wqk", [D, D], F32, kind="ExternalInput").ap()
    bqk_d = nc.dram_tensor("bqk", [D], F32, kind="ExternalInput").ap()
    wv_d = nc.dram_tensor("wv", [D, D], F32, kind="ExternalInput").ap()
    bv_d = nc.dram_tensor("bv", [D], F32, kind="ExternalInput").ap()
    wu_d = nc.dram_tensor("wu", [D, D], F32, kind="ExternalInput").ap()
    bu_d = nc.dram_tensor("bu", [D], F32, kind="ExternalInput").ap()
    wo_d = nc.dram_tensor("wo", [D, D], F32, kind="ExternalInput").ap()
    bo_d = nc.dram_tensor("bo", [D], F32, kind="ExternalInput").ap()
    lng_d = nc.dram_tensor("lng", [D], F32, kind="ExternalInput").ap()
    lnb_d = nc.dram_tensor("lnb", [D], F32, kind="ExternalInput").ap()
    rw_d = nc.dram_tensor("rw", [NE, D], F32, kind="ExternalInput").ap()
    nw_d = nc.dram_tensor("nw", [D], F32, kind="ExternalInput").ap()
    gw_d = nc.dram_tensor("gw", [NE, D, FE], F32, kind="ExternalInput").ap()
    uw_d = nc.dram_tensor("uw", [NE, D, FE], F32, kind="ExternalInput").ap()
    dw_d = nc.dram_tensor("dw", [NE, FE, D], F32, kind="ExternalInput").ap()
    out_d = nc.dram_tensor("out", [SEQ, D], F32, kind="ExternalOutput").ap()

    knd = "ExternalOutput" if dbg else "Internal"
    xnt_s = nc.dram_tensor("xnt_s", [KC, 128, SEQ], BF16, kind=knd).ap()
    wh1_s = nc.dram_tensor("wh1_s", [3, KC, 128, HALF], BF16, kind="Internal").ap()
    qt_s = nc.dram_tensor("qt_s", [NT, 2, 128, HALF], BF16, kind=knd).ap()
    u_s = nc.dram_tensor("u_s", [NT, 2, 128, HALF], BF16, kind=knd).ap()
    y_s = nc.dram_tensor("y_s", [SEQ, D], F32, kind=knd).ap()
    if dbg:
        dbg_cw = nc.dram_tensor("dbg_cw", [128, NT * 8], F32, kind="ExternalOutput").ap()
        dbg_kv = nc.dram_tensor("dbg_kv", [128, D], BF16, kind="ExternalOutput").ap()
        dbg_gr = nc.dram_tensor("dbg_gr", [128, 2 * D], F32, kind="ExternalOutput").ap()
    gbf = nc.dram_tensor("gbf", [NE, KC, 128, FE], BF16, kind="Internal").ap()
    ubf = nc.dram_tensor("ubf", [NE, KC, 128, FE], BF16, kind="Internal").ap()
    dbf = nc.dram_tensor("dbf", [NE, FC, 128, D], BF16, kind="Internal").ap()
    wobf = nc.dram_tensor("wobf", [KC, 128, D], BF16, kind="Internal").ap()

    def sb(name, shape, dtype):
        return nc.alloc_sbuf_tensor(name, shape, dtype).ap()

    id_bf = sb("id_bf", [128, 128], BF16)
    id_f32 = sb("id_f32", [128, 128], F32)
    iot = sb("iot", [128, 128], F32)
    ones_bf = sb("ones_bf", [1, 128], BF16)
    ones_f32 = sb("ones_f32", [1, 128], F32)
    bqk_r = sb("bqk_r", [1, D], BF16)
    bv_r = sb("bv_r", [1, D], BF16)
    bu_r = sb("bu_r", [1, D], BF16)
    bo_r = sb("bo_r", [1, D], BF16)
    nwT = sb("nwT", [128, 16], F32)
    g_rep = nc.alloc_sbuf_tensor_at(
        "g_rep", [128, D], F32, offset=SB_RIGHT_END - ARENA_RESERVE).ap()
    b_rep = nc.alloc_sbuf_tensor_at(
        "b_rep", [128, D], F32, offset=SB_RIGHT_END - ARENA_RESERVE + D * 4).ap()
    rwnT = sb("rwnT", [128, KC * 8], BF16)
    cw_res = sb("cw_res", [128, NT * 8], F32)
    ceps = sb("ceps", [128, 4], F32)
    kv_bf = sb("kv_bf", [128, D], BF16)
    lgf = sb("lgf", [128, 8], F32)
    sml = sb("sml", [128, 16], F32)
    cwt = sb("cwt", [128, 8], F32)

    PS = nc.alloc_psum_tensor("PS", [128, 4096], F32).ap()
    PS7_BF = PS[:, 3584:4096].bitcast(BF16)
    PS6_BF = PS[:, 3072:3584].bitcast(BF16)

    arena = Arena(nc, SB_RIGHT_END - ARENA_BYTES, ARENA_BYTES - ARENA_RESERVE)
    wq_sb = arena.alloc("wq", [128, KC * HALF], BF16)
    wv_sb = arena.alloc("wvs", [128, KC * HALF], BF16)
    wu_sb = arena.alloc("wus", [128, KC * HALF], BF16)

    P = Planner(nc)
    for s in ("tE", "vE", "sE", "pE", "bS", "bG", "bV", "bA", "bT"):
        P.sem(s)
    P.init_lanes()
    C = P.cnt
    ev = {}

    W_END = arena.off
    lng_row = arena.alloc("lng_row", [1, D], F32)
    lnb_row = arena.alloc("lnb_row", [1, D], F32)
    rw_sb = arena.alloc("rw_sb", [8, D], F32)
    nw16 = arena.alloc("nw16", [16, 128], F32)
    w_ready = {0: [
        P.dma("gpsimd", wsb.rearrange("p (c n) -> p c n", c=KC),
              wd[:, 0:HALF].rearrange("(c p) n -> p c n", p=128))
        for wd, wsb in ((wqk_d, wq_sb), (wv_d, wv_sb), (wu_d, wu_sb))]}
    conv_done = {}
    conv_done["wh1"] = [
        P.dma("gpsimd", wh1_s[i],
              wd[:, HALF:D].rearrange("(c p) n -> c p n", p=128))
        for i, wd in enumerate((wqk_d, wv_d, wu_d))]

    d_setup = [
        P.dma("sync", lng_row, lng_d.rearrange("(o d) -> o d", o=1)),
        P.dma("sync", lnb_row, lnb_d.rearrange("(o d) -> o d", o=1)),
        P.dma("sync", rw_sb, rw_d),
        P.dma("sync", nw16, nw_d.rearrange("(c p) -> c p", p=128)),
    ]
    b_rows = [
        P.dma("gpsimd", bqk_r, bqk_d.rearrange("(o d) -> o d", o=1)),
        P.dma("gpsimd", bv_r, bv_d.rearrange("(o d) -> o d", o=1)),
        P.dma("gpsimd", bu_r, bu_d.rearrange("(o d) -> o d", o=1)),
        P.dma("gpsimd", bo_r, bo_d.rearrange("(o d) -> o d", o=1)),
    ]
    P.step("gpsimd", lambda e: e.memset(ceps[:, 0:1], D * LN_EPS))
    P.step("gpsimd", lambda e: e.memset(ceps[:, 1:2], RMS_EPS))
    P.step("gpsimd", lambda e: e.memset(ceps[:, 2:3], D * LN_EPS * RMS_EPS))
    P.step("gpsimd", lambda e: e.memset(ceps[:, 3:4], HALF_LN_D))
    P.step("gpsimd", lambda e: e.memset(ones_bf, 1.0))
    P.step("gpsimd", lambda e: e.memset(ones_f32, 1.0))
    P.step("gpsimd", lambda e: e.iota(iot, [[1, 128]], channel_multiplier=-1,
                                      allow_small_or_imprecise_dtypes=True),
           inc=[("pE", 1)])
    pe_iota = C["pE"]
    P.step("vector", lambda e: e.tensor_single_scalar(id_bf, iot, 0.0, ALU.is_equal),
           wait=[("pE", pe_iota)])
    P.step("vector", lambda e: e.tensor_single_scalar(id_f32, iot, 0.0, ALU.is_equal),
           inc=[("vE", 1)])

    for ci in range(4):
        P.step("tensor",
               lambda e, ci=ci: (
                   e.matmul(PS[:, 0:512], ones_f32,
                            lng_row[:, ci * 512:(ci + 1) * 512], start=True, stop=True),
                   e.matmul(PS[:, 512:1024], ones_f32,
                            lnb_row[:, ci * 512:(ci + 1) * 512], start=True, stop=True))[-1],
               wait=d_setup + [("vE", C["vE"])], inc=[("tE", 1)])
        t_rep = C["tE"]
        P.step("vector", lambda e, ci=ci: e.tensor_copy(
            g_rep[:, ci * 512:(ci + 1) * 512], PS[:, 0:512]), wait=[("tE", t_rep)])
        P.step("vector", lambda e, ci=ci: e.tensor_copy(
            b_rep[:, ci * 512:(ci + 1) * 512], PS[:, 512:1024]),
            wait=[("tE", t_rep)], inc=[("vE", 1)])

    P.step("tensor", lambda e: e.transpose(PS[:, 1024:1040], nw16, id_f32[0:16, 0:16]),
           wait=d_setup + [("vE", C["vE"])], inc=[("tE", 1)])
    t_nw = C["tE"]
    P.step("vector", lambda e: e.tensor_copy(nwT, PS[:, 1024:1040]),
           wait=[("tE", t_nw)], inc=[("vE", 1)])
    v_nwT = C["vE"]
    for c in range(KC):
        P.step("tensor", lambda e, c=c: e.transpose(
            PS[:, 1536 + (c % 2) * 512: 1544 + (c % 2) * 512],
            rw_sb[0:8, c * 128:(c + 1) * 128], id_f32[0:8, 0:8]),
            wait=[("vE", C["vE"])], inc=[("tE", 1)])
        t_rw = C["tE"]
        P.step("vector", lambda e, c=c: e.tensor_scalar_mul(
            rwnT[:, c * 8:(c + 1) * 8],
            PS[:, 1536 + (c % 2) * 512: 1544 + (c % 2) * 512], nwT[:, c:c + 1]),
            wait=[("tE", t_rw), ("vE", v_nwT)], inc=[("vE", 1)])

    P.barrier()
    marks = {"setup": P.mark()}
    ARENA_BASE0 = arena.base
    arena.base = arena.off = ((W_END + 127) // 128 * 128)


    def convert_expert(e, wait=()):
        e1 = P.dma("gpsimd", gbf[e], gw_d[e].rearrange("(c p) f -> c p f", p=128),
                   wait=wait)
        e2 = P.dma("gpsimd", ubf[e], uw_d[e].rearrange("(c p) f -> c p f", p=128))
        e3 = P.dma("gpsimd", dbf[e], dw_d[e].rearrange("(c p) d -> c p d", p=128))
        conv_done[e] = [e1, e2, e3]

    x_buf = [arena.alloc(f"x{i}", [128, D], F32) for i in range(2)]
    xn_bf = [arena.alloc(f"xnbf{i}", [128, D], BF16) for i in range(2)]
    xnt_b = [arena.alloc(f"xnt{i}", [128, D], BF16) for i in range(2)]
    q_bf = [arena.alloc(f"qbf{i}", [128, HALF], BF16) for i in range(2)]
    v_bf = arena.alloc("vbf", [128, HALF], BF16)
    u_bf = [arena.alloc(f"ubfb{i}", [128, HALF], BF16) for i in range(2)]
    qt_b = [arena.alloc(f"qtb{i}", [128, HALF], BF16) for i in range(2)]
    el_m = arena.alloc("elm", [128, HALF], F32)
    el_e = arena.alloc("ele", [128, HALF], F32)
    el_r = arena.alloc("elr", [128, HALF], F32)
    kv_acc = arena.alloc("kvacc", [128, D], F32)

    q_ps = PS[:, 0:1024]
    v_ps = PS[:, 1024:2048]
    u_ps = PS[:, 2048:3072]
    log_ps = PS[:, 3072:3080]
    QPB = [PS[:, 1024:1536].bitcast(BF16), PS[:, 1536:2048].bitcast(BF16)]

    def load_w_half(h):
        evs = []
        if True:
            w = [("tE", ev[("mms", 0, NT - 1)])] + list(conv_done["wh1"])
            for i, wsb in enumerate((wq_sb, wv_sb, wu_sb)):
                evs.append(P.dma(
                    "sync", wsb.rearrange("p (c n) -> p c n", c=KC),
                    wh1_s[i].rearrange("c p n -> p c n"), wait=w))
        return evs


    for _e in range(3):
        convert_expert(_e)

    def emit_ln(t):
        bi = t % 2
        w = []
        if t >= 2:
            w = [("vE", ev[("xnbf", t - 2)]), ("sE", ev[("xsq", t - 2)])]
        ev[("x", t)] = P.dma("sync", x_buf[bi],
                             x_d[t * TT:(t + 1) * TT, :], wait=w)
        idw = [ev[("x", t)]]
        if t >= 2:
            idw.append(("tE", ev[("xpg", t - 2, 1)]))
        P.step("scalar", lambda e, bi=bi: e.activation(
            xn_bf[bi], x_buf[bi], AF.Identity, accum_out=sml[:, 0:1]),
            wait=idw)
        P.step("scalar", lambda e, bi=bi: e.activation(
            xn_bf[bi], x_buf[bi], AF.Square, accum_out=sml[:, 1:2]))
        ev[("xsq", t)] = C["sE"]

        lsw = [("sE", ev[("xsq", t)])]
        if ("lgfe", t - 2) in ev:
            lsw.append(("vE", ev[("lgfe", t - 2)]))
        if ("xnbf", t - 1) in ev:
            lsw.append(("vE", ev[("xnbf", t - 1)]))
        P.step("scalar", lambda e: e.activation(sml[:, 8:9], sml[:, 0:1], AF.Square),
               wait=lsw)
        P.step("scalar", lambda e: e.activation(
            sml[:, 4:5], sml[:, 8:9], AF.Relu, scale=-1.0 / D, bias=sml[:, 1:2]))
        P.step("scalar", lambda e: e.activation(
            sml[:, 7:8], sml[:, 4:5], AF.Ln, bias=ceps[:, 0:1]))
        P.step("scalar", lambda e: e.activation(
            sml[:, 5:6], sml[:, 7:8], AF.Exp, scale=-0.5, bias=ceps[:, 3:4]))
        P.step("scalar", lambda e: e.mul(sml[:, 2:3], sml[:, 0:1], 1.0 / D))
        P.step("scalar", lambda e: e.mul(sml[:, 9:10], sml[:, 7:8], 0.5))
        P.step("scalar", lambda e: e.activation(
            sml[:, 10:11], sml[:, 4:5], AF.Ln, scale=1.0 + RMS_EPS,
            bias=ceps[:, 2:3]))
        rrc = 11 if t % 2 == 0 else 3
        P.step("scalar", lambda e, rrc=rrc: e.activation(
            sml[:, rrc:rrc + 1], sml[:, 10:11], AF.Exp, scale=-0.5,
            bias=sml[:, 9:10]))
        ev[("lnexp", t)] = C["sE"]

    def emit_xnbf(t):
        bi = t % 2
        P.step("vector", lambda e, bi=bi: e.tensor_scalar(
            xn_bf[bi], x_buf[bi], sml[:, 2:3], sml[:, 5:6],
            ALU.subtract, ALU.mult),
            wait=[("sE", ev[("lnexp", t)])], inc=[("vE", 1)])
        ev[("xnbf", t)] = C["vE"]

    def emit_xpose(tn):
        bj = tn % 2
        xbanks = [PS7_BF, PS6_BF]
        for g2 in range(2):
            xw = [("vE", ev[("xnbf", tn)])]
            if ("xpfree", g2) in ev:
                xw.append(("vE", ev[("xpfree", g2)]))
            if g2 == 1 and ("lgfe", tn - 1) in ev:
                xw.append(("vE", ev[("lgfe", tn - 1)]))

            def xpose(e, g2=g2, bj=bj):
                last = None
                for k in range(8):
                    c = g2 * 8 + k
                    last = e.transpose(
                        xbanks[g2][:, k * 128:(k + 1) * 128],
                        xn_bf[bj][:, c * 128:(c + 1) * 128], id_bf)
                return last
            P.step("tensor", xpose, wait=xw, inc=[("tE", 1)])
            ev[("xpg", tn, g2)] = C["tE"]
            xew = [("tE", ev[("xpg", tn, g2)])]
            if g2 == 0 and tn >= 2:
                xew += [("tE", ev[("mms", 0, tn - 2)]),
                        ev[("xnts", tn - 2)]]
            P.step("vector", lambda e, g2=g2, bj=bj: e.tensor_copy(
                xnt_b[bj][:, g2 * 1024:(g2 + 1) * 1024],
                xbanks[g2][:, 0:1024]),
                wait=xew, inc=[("vE", 1)])
            ev[("xpfree", g2)] = C["vE"]
        ev[("xnt", tn)] = C["vE"]
        ev[("xnts", tn)] = P.dma(
            "sync",
            xnt_s[:, :, tn * TT:(tn + 1) * TT].rearrange("c p e -> p c e"),
            xnt_b[bj].rearrange("p (c e) -> p c e", c=KC),
            wait=[("vE", ev[("xnt", tn)])])

    for h in range(2):
        if h == 1:
            w_ready[1] = load_w_half(1)
            for e_i in range(3, NE):
                convert_expert(e_i)
            conv_done["wo"] = P.dma(
                "gpsimd", wobf, wo_d.rearrange("(c p) d -> c p d", p=128))

        if h == 0:
            emit_ln(0)
            emit_xnbf(0)
            emit_xpose(0)
        for t in range(NT):
            bi = t % 2
            if h == 1:
                if t >= 2:
                    w = [("tE", ev[("mms", 1, t - 2)])]
                else:
                    w = [("tE", ev[("mms", 0, NT - 1)]),
                         ev[("xnts", NT - 2 + t)]]
                ev[("xntl", t)] = P.dma(
                    "sync", xnt_b[bi].rearrange("p (c e) -> p c e", c=KC),
                    xnt_s[:, :, t * TT:(t + 1) * TT].rearrange("c p e -> p c e"),
                    wait=w)

            mmw = list(w_ready[h])
            if (h, t) == (0, 0):
                mmw += b_rows
            if h == 0:
                mmw.append(("vE", ev[("xnt", t)]))
            else:
                mmw.append(ev[("xntl", t)])
            if (h, t) != (0, 0):
                ph, pt = (h, t - 1) if t > 0 else (0, NT - 1)
                mmw += [("vE", ev[("elu_sub", ph, pt)]),
                        ("vE", ev[("qt", ph, pt)])]
                if ph == 0:
                    mmw.append(("vE", ev[("lgfe", pt)]))

            def wave_a(e, h=h, bi=bi):
                for ps_t, br in ((q_ps, bqk_r), (v_ps, bv_r)):
                    for s2 in range(2):
                        e.matmul(ps_t[:, s2 * 512:(s2 + 1) * 512], ones_bf,
                                 br[:, h * HALF + s2 * 512: h * HALF + (s2 + 1) * 512],
                                 start=True, stop=False)
                last = None
                for c in range(KC):
                    lhs = xnt_b[bi][:, c * 128:(c + 1) * 128]
                    for ps_t, wsb in ((q_ps, wq_sb), (v_ps, wv_sb)):
                        for s2 in range(2):
                            last = e.matmul(
                                ps_t[:, s2 * 512:(s2 + 1) * 512], lhs,
                                wsb[:, c * HALF + s2 * 512: c * HALF + (s2 + 1) * 512],
                                start=False, stop=(c == KC - 1))
                    if h == 0:
                        last = e.matmul(log_ps, lhs, rwnT[:, c * 8:(c + 1) * 8],
                                        start=(c == 0), stop=(c == KC - 1))
                return last
            P.step("tensor", wave_a, wait=mmw, inc=[("tE", 1)])
            ev[("mmsA", h, t)] = C["tE"]

            if h == 0 and t + 1 < NT:
                emit_ln(t + 1)
            mbw = []
            if (h, t) != (0, 0):
                ph, pt = (h, t - 1) if t > 0 else (0, NT - 1)
                mbw.append(("vE", ev[("kv_evac", ph, pt)]))

            def wave_b(e, h=h, bi=bi):
                for s2 in range(2):
                    e.matmul(u_ps[:, s2 * 512:(s2 + 1) * 512], ones_bf,
                             bu_r[:, h * HALF + s2 * 512: h * HALF + (s2 + 1) * 512],
                             start=True, stop=False)
                last = None
                for c in range(KC):
                    lhs = xnt_b[bi][:, c * 128:(c + 1) * 128]
                    for s2 in range(2):
                        last = e.matmul(
                            u_ps[:, s2 * 512:(s2 + 1) * 512], lhs,
                            wu_sb[:, c * HALF + s2 * 512: c * HALF + (s2 + 1) * 512],
                            start=False, stop=(c == KC - 1))
                return last
            P.step("tensor", wave_b, wait=mbw, inc=[("tE", 1)])
            ev[("mms", h, t)] = C["tE"]

            P.step("vector", lambda e: e.tensor_scalar_min(el_m, q_ps, 0.0),
                   wait=[("tE", ev[("mmsA", h, t)])], inc=[("vE", 1)])
            ev[("elu_m", h, t)] = C["vE"]
            P.step("vector", lambda e: e.tensor_sub(el_r, q_ps, el_m),
                   inc=[("vE", 1)])
            ev[("elu_sub", h, t)] = C["vE"]
            P.step("scalar", lambda e: e.activation(el_e, el_m, AF.Exp),
                   wait=[("vE", ev[("elu_m", h, t)])], inc=[("sE", 1)])
            ev[("elu_e", h, t)] = C["sE"]
            P.step("vector", lambda e, bi=bi: e.scalar_tensor_tensor(
                q_bf[bi], el_e, -1.0, el_r, ALU.add, ALU.add),
                wait=[("sE", ev[("elu_e", h, t)])], inc=[("vE", 1)])
            ev[("elu", h, t)] = C["vE"]

            silw = [("tE", ev[("mmsA", h, t)])]
            if (h, t) != (0, 0):
                ph, pt = (h, t - 1) if t > 0 else (0, NT - 1)
                silw.append(("tE", ev[("kv", ph, pt)]))
            P.step("scalar", lambda e: e.activation(v_bf, v_ps, AF.Silu), wait=silw)
            ev[("silu_v", h, t)] = C["sE"]
            suw = [("tE", ev[("mms", h, t)])]
            if t >= 2:
                suw.append(ev[("us", h, t - 2)])
            elif h == 1:
                suw.append(ev[("us", 0, NT - 2 + t)])
            P.step("scalar", lambda e, bi=bi: e.activation(u_bf[bi], u_ps, AF.Silu),
                   wait=suw)
            ev[("silu", h, t)] = C["sE"]
            ev[("us", h, t)] = P.dma("sync", u_s[t, h], u_bf[bi],
                                     wait=[("sE", ev[("silu", h, t)])])

            if h == 0:
                rrc = 11 if t % 2 == 0 else 3
                P.step("vector", lambda e, rrc=rrc: e.tensor_scalar_mul(
                    lgf, log_ps, sml[:, rrc:rrc + 1]),
                    wait=[("tE", ev[("mmsA", 0, t)]), ("sE", ev[("lnexp", t)])],
                    inc=[("vE", 1)])
                ev[("lgfe", t)] = C["vE"]

            if h == 0 and t + 1 < NT:
                emit_xnbf(t + 1)
                emit_xpose(t + 1)
            def kv_mms(e, bi=bi):
                last = None
                for i in range(8):
                    last = e.matmul(u_ps[:, i * 128:(i + 1) * 128],
                                    q_bf[bi][:, i * 128:(i + 1) * 128],
                                    v_bf[:, i * 128:(i + 1) * 128],
                                    start=True, stop=True)
                return last
            P.step("tensor", kv_mms,
                   wait=[("vE", ev[("elu", h, t)]), ("sE", ev[("silu", h, t)])],
                   inc=[("tE", 1)])
            ev[("kv", h, t)] = C["tE"]
            if t == 0:
                P.step("vector", lambda e, h=h: e.tensor_copy(
                    kv_acc[:, h * HALF:(h + 1) * HALF], u_ps),
                    wait=[("tE", ev[("kv", h, t)])], inc=[("vE", 1)])
            else:
                P.step("vector", lambda e, h=h: e.tensor_add(
                    kv_acc[:, h * HALF:(h + 1) * HALF], u_ps,
                    kv_acc[:, h * HALF:(h + 1) * HALF]),
                    wait=[("tE", ev[("kv", h, t)])], inc=[("vE", 1)])
            ev[("kv_evac", h, t)] = C["vE"]

            qxw = [("vE", ev[("elu", h, t)]), ("sE", ev[("silu_v", h, t)])]
            if ("qtprev",) in ev:
                qxw.append(("vE", ev[("qtprev",)]))

            def qpose(e, bi=bi):
                last = None
                for k in range(8):
                    last = e.transpose(
                        QPB[0][:, k * 128:(k + 1) * 128],
                        q_bf[bi][:, k * 128:(k + 1) * 128], id_bf)
                return last
            P.step("tensor", qpose, wait=qxw, inc=[("tE", 1)])
            ev[("qpg", h, t)] = C["tE"]
            qsw = [("tE", ev[("qpg", h, t)])]
            if t >= 2:
                qsw.append(ev[("qts", h, t - 2)])
            elif h == 1:
                qsw.append(ev[("qts", 0, NT - 2 + t)])
            P.step("vector", lambda e, bi=bi: e.tensor_copy(
                qt_b[bi], QPB[0][:, 0:1024]),
                wait=qsw, inc=[("vE", 1)])
            ev[("qtprev",)] = C["vE"]
            ev[("qt", h, t)] = C["vE"]
            ev[("qts", h, t)] = P.dma("sync", qt_s[t, h], qt_b[bi],
                                      wait=[("vE", ev[("qt", h, t)])])

            if h == 0:
                P.step("vector", lambda e: e.reduce_max(
                    sml[:, 12:13], lgf, axis=AX.X))
                P.step("vector", lambda e: e.reduce_max(
                    sml[:, 13:14], lgf, axis=AX.X, negate=True))
                P.step("vector", lambda e: e.tensor_single_scalar(
                    cwt, lgf, sml[:, 12:13], ALU.is_ge))
                P.step("vector", lambda e: e.scalar_tensor_tensor(
                    cwt, cwt, 1e30, lgf, ALU.mult, ALU.subtract))
                P.step("vector", lambda e: e.tensor_reduce(
                    sml[:, 14:15], cwt, axis=AX.X, op=ALU.min, negate=True))
                ev[("cwv1", t)] = C["vE"]

                P.step("scalar", lambda e: e.activation(
                    cwt, lgf, AF.Exp, bias=sml[:, 13:14]),
                    wait=[("vE", ev[("cwv1", t)])])
                P.step("scalar", lambda e: e.activation(
                    sml[:, 15:16], sml[:, 14:15], AF.Exp, bias=sml[:, 13:14]))
                ev[("cws", t)] = C["sE"]

                P.step("vector", lambda e: e.tensor_scalar_add(
                    sml[:, 15:16], sml[:, 15:16], 1.0),
                    wait=[("sE", ev[("cws", t)])])
                P.step("vector", lambda e: e.reciprocal(
                    sml[:, 15:16], sml[:, 15:16]))
                P.step("vector", lambda e: e.tensor_scalar_mul(
                    sml[:, 15:16], sml[:, 15:16], MOE_SCALE))
                P.step("vector", lambda e: e.tensor_single_scalar(
                    lgf, lgf, sml[:, 14:15], ALU.is_ge))
                P.step("vector", lambda e: e.tensor_mul(cwt, cwt, lgf))
                P.step("vector", lambda e, t=t: e.tensor_scalar_mul(
                    cw_res[:, t * 8:(t + 1) * 8], cwt, sml[:, 15:16]))
                ev[("cw", t)] = C["vE"]

    P.step("vector", lambda e: e.tensor_copy(kv_bf, kv_acc),
           wait=[("vE", ev[("kv_evac", 1, NT - 1)])], inc=[("vE", 1)])

    P.barrier()
    marks["p1"] = P.mark()
    arena.base = ARENA_BASE0
    arena.reset()

    xng = arena.alloc("xng", [128, KC * GT], BF16)
    y_g = arena.alloc("yg", [128, JT * D], F32)
    gate_b = [arena.alloc(f"gateb{i}", [128, KC * 256], BF16) for i in range(2)]
    up_b = [arena.alloc(f"upb{i}", [128, KC * 256], BF16) for i in range(2)]
    down_b = [arena.alloc(f"downb{i}", [128, FC * D], BF16) for i in range(2)]
    at_b = arena.alloc("atb", [128, FC * GT], BF16)
    sg_b = arena.alloc("sgb", [128, GT], F32)

    gps = [PS[:, 0:512], PS[:, 512:1024]]
    ups = [PS[:, 1024:1536], PS[:, 1536:2048]]
    yp2 = [PS[:, 2048:3072], PS[:, 3072:4096]]

    def load_gu_quarter(g, e_i, qq, wait):
        qbi = qq % 2
        qw = list(conv_done[e_i]) + list(wait)
        ev[("gql", g, e_i, qq)] = P.dma(
            "sync", gate_b[qbi].rearrange("p (c f) -> p c f", c=KC),
            gbf[e_i][:, :, qq * 256:(qq + 1) * 256].rearrange("c p f -> p c f"),
            wait=qw)
        ev[("uql", g, e_i, qq)] = P.dma(
            "sync", up_b[qbi].rearrange("p (c f) -> p c f", c=KC),
            ubf[e_i][:, :, qq * 256:(qq + 1) * 256].rearrange("c p f -> p c f"),
            wait=qw)

    load_gu_quarter(0, 0, 0, [])
    load_gu_quarter(0, 0, 1, [])

    for g in range(NG):
        xw = [("tE", ev[("ffn_mm", g - 1)])] if g > 0 else []
        ev[("xng", g)] = P.dma(
            "sync", xng.rearrange("p (c e) -> p c e", c=KC),
            xnt_s[:, :, g * GT:(g + 1) * GT].rearrange("c p e -> p c e"), wait=xw)

        for e_i in range(NE):
            dbi = e_i % 2
            dw_w = list(conv_done[e_i])
            gi = g * NE + e_i
            if gi >= 2:
                pg, pe = divmod(gi - 2, NE)
                dw_w.append(("tE", ev[("down_mm", pg, pe)]))
            ev[("downl", g, e_i)] = P.dma(
                "sync", down_b[dbi].rearrange("p (c d) -> p c d", c=FC),
                dbf[e_i].rearrange("c p d -> p c d"), wait=dw_w)

            for fc in range(FC):
                qq = fc // 2
                qbi = qq % 2
                fo = (fc % 2) * 128
                pp = fc % 2
                fi = gi * FC + fc
                mmw = [ev[("gql", g, e_i, qq)], ev[("uql", g, e_i, qq)],
                       ev[("xng", g)]]
                if fi >= 2:
                    pg2, pe2f = divmod(fi - 2, FC * NE)
                    pe2, pfc = divmod(pe2f, FC)
                    mmw += [("sE", ev[("silu2", pg2, pe2, pfc)]),
                            ("vE", ev[("at", pg2, pe2, pfc)])]

                def gu_mms(e, qbi=qbi, fo=fo, pp=pp):
                    last = None
                    for c in range(KC):
                        last = e.matmul(gps[pp],
                                        gate_b[qbi][:, c * 256 + fo: c * 256 + fo + 128],
                                        xng[:, c * GT:(c + 1) * GT],
                                        start=(c == 0), stop=(c == KC - 1))
                    for c in range(KC):
                        last = e.matmul(ups[pp],
                                        up_b[qbi][:, c * 256 + fo: c * 256 + fo + 128],
                                        xng[:, c * GT:(c + 1) * GT],
                                        start=(c == 0), stop=(c == KC - 1))
                    return last
                P.step("tensor", gu_mms, wait=mmw, inc=[("tE", 1)])
                ev[("guq_mm", g, e_i, qq)] = C["tE"]
                ev[("gu_mm", g, e_i, fc)] = C["tE"]

                s2w = [("tE", ev[("gu_mm", g, e_i, fc)])]
                if fi >= 1:
                    pg2, pe2f = divmod(fi - 1, FC * NE)
                    pe2, pfc = divmod(pe2f, FC)
                    s2w.append(("vE", ev[("at", pg2, pe2, pfc)]))
                P.step("scalar", lambda e, pp=pp: e.activation(sg_b, gps[pp], AF.Silu),
                       wait=s2w, inc=[("sE", 1)])
                ev[("silu2", g, e_i, fc)] = C["sE"]
                atw = [("sE", ev[("silu2", g, e_i, fc)])]
                if e_i == 0 and fc == 0 and g > 0:
                    atw.append(ev[("ys", g - 1)])
                P.step("vector", lambda e, fc=fc, pp=pp: e.tensor_mul(
                    at_b[:, fc * GT:(fc + 1) * GT], sg_b, ups[pp]),
                    wait=atw, inc=[("vE", 1)])
                ev[("at", g, e_i, fc)] = C["vE"]

                if fc in (1, 3):
                    load_gu_quarter(g, e_i, fc // 2 + 2,
                                    [("tE", ev[("guq_mm", g, e_i, fc // 2)])])
                elif fc in (5, 7):
                    ng_, ne_ = (g, e_i + 1) if e_i + 1 < NE else (g + 1, 0)
                    if ng_ < NG:
                        load_gu_quarter(ng_, ne_, fc // 2 - 2,
                                        [("tE", ev[("guq_mm", g, e_i, fc // 2)])])

            for j in range(JT):
                for dh in range(2):
                    jdh = j * 2 + dh
                    pp = jdh % 2
                    dmw = [ev[("downl", g, e_i)],
                           ("vE", ev[("at", g, e_i, FC - 1)])]
                    prev = gi * 8 + jdh - 2
                    if prev >= 0:
                        dmw.append(("vE", ev[("yup", prev)]))

                    def down_mms(e, j=j, dh=dh, pp=pp, dbi=dbi):
                        last = None
                        for fcc in range(FC):
                            lhs = at_b[:, fcc * GT + j * 128: fcc * GT + (j + 1) * 128]
                            for q in range(2):
                                dc = dh * 2 + q
                                last = e.matmul(
                                    yp2[pp][:, q * 512:(q + 1) * 512], lhs,
                                    down_b[dbi][:, fcc * D + dc * 512:
                                                fcc * D + (dc + 1) * 512],
                                    start=(fcc == 0), stop=(fcc == FC - 1))
                        return last
                    P.step("tensor", down_mms, wait=dmw, inc=[("tE", 1)])
                    ev[("down_mm", g, e_i)] = C["tE"]
                    t_dmm = C["tE"]

                    cw_col = cw_res[:, (g * JT + j) * 8 + e_i:
                                    (g * JT + j) * 8 + e_i + 1]
                    for q in range(2):
                        dc = dh * 2 + q
                        ysl = y_g[:, j * D + dc * 512: j * D + (dc + 1) * 512]
                        yw = [("tE", t_dmm)]
                        if e_i == 0 and g > 0 and jdh == 0 and q == 0:
                            yw.append(ev[("ys", g - 1)])
                        if e_i == 0:
                            P.step("vector",
                                   lambda e, pp=pp, q=q, ysl=ysl, cw_col=cw_col:
                                   e.tensor_scalar_mul(
                                       ysl, yp2[pp][:, q * 512:(q + 1) * 512],
                                       cw_col),
                                   wait=yw, inc=[("vE", 1)])
                        else:
                            P.step("vector",
                                   lambda e, pp=pp, q=q, ysl=ysl, cw_col=cw_col:
                                   e.scalar_tensor_tensor(
                                       ysl, yp2[pp][:, q * 512:(q + 1) * 512],
                                       cw_col, ysl, ALU.mult, ALU.add),
                                   wait=yw, inc=[("vE", 1)])
                    ev[("yup", gi * 8 + jdh)] = C["vE"]
        ev[("ffn_mm", g)] = C["tE"]

        ev[("ys", g)] = P.dma(
            "sync",
            y_s[g * GT:(g + 1) * GT, :].rearrange("(j p) d -> p j d", p=128),
            y_g.rearrange("p (j d) -> p j d", j=JT),
            wait=[("vE", ev[("yup", (g * NE + NE - 1) * 8 + 7)])])

    P.barrier()
    marks["p2"] = P.mark()
    arena.reset()

    wo_sb = arena.alloc("wosb", [128, KC * D], BF16)
    qt_t = [arena.alloc(f"qtt{i}", [128, D], BF16) for i in range(2)]
    u_t = [arena.alloc(f"ut{i}", [128, D], BF16) for i in range(2)]
    x_t = arena.alloc("xt0", [128, D], F32)
    y_t = arena.alloc("yt0", [128, D], F32)
    o_t = arena.alloc("ot0", [128, D], F32)
    n_sb = arena.alloc("nsb", [128, D], F32)
    t1_sb = arena.alloc("t1sb", [128, D], F32)
    t4_sb = arena.alloc("t4sb", [128, D], F32)
    ug_sb = arena.alloc("ugsb", [128, D], F32)
    ub_sb = arena.alloc("ubsb", [128, D], F32)
    z_bf = arena.alloc("zbf", [128, D], BF16)
    zt_sb = [arena.alloc(f"ztsb{i}", [128, D], BF16) for i in range(2)]
    sg_c = [arena.alloc(f"sgc{i}", [128, 512], F32) for i in range(2)]
    sm3 = arena.alloc("sm3", [128, 8], F32)

    att_ps = PS[:, 0:2048]
    sps = [PS[:, 2048:2560], PS[:, 2560:3072]]
    ZPB = [PS6_BF, PS7_BF]

    ev[("wol",)] = P.dma("sync", wo_sb.rearrange("p (c d) -> p c d", c=KC),
                         wobf.rearrange("c p d -> p c d"),
                         wait=[conv_done["wo"]])

    def emit_s_combine(t):
        zi = t % 2
        sgw = [ev[("yl", t)]]
        if t > 0:
            sgw.append(("vE", ev[("t4", (t - 1) * 4 + 3)]))
        for dc in range(4):
            pp = dc % 2
            smw = [("vE", ev[("zte", t)]), ev[("wol",)]]
            prev_dc = t * 4 + dc - 2
            if prev_dc >= 0:
                smw.append(("vE", ev[("t4", prev_dc)]))

            def s_mms(e, dc=dc, pp=pp, zi=zi):
                e.matmul(sps[pp], ones_bf, bo_r[:, dc * 512:(dc + 1) * 512],
                         start=True, stop=False)
                last = None
                for c in range(KC):
                    last = e.matmul(sps[pp], zt_sb[zi][:, c * 128:(c + 1) * 128],
                                    wo_sb[:, c * D + dc * 512: c * D + (dc + 1) * 512],
                                    start=False, stop=(c == KC - 1))
                return last
            P.step("tensor", s_mms, wait=smw, inc=[("tE", 1)])
            ev[("s_mm", t, dc)] = C["tE"]

            if dc == 0:
                scw = list(sgw)
            elif dc == 1:
                scw = [("vE", ev[("t4", (t - 1) * 4 + 3)])] if t > 0 else []
            else:
                scw = [("vE", ev[("t4", t * 4 + dc - 2)])]
            P.step("scalar", lambda e, dc=dc: e.activation(
                sg_c[dc % 2], y_t[:, dc * 512:(dc + 1) * 512], AF.Silu),
                wait=scw)
            ev[("sgc", t, dc)] = C["sE"]

            t4w = [("tE", ev[("s_mm", t, dc)]), ("sE", ev[("sgc", t, dc)])]
            if prev_dc >= 0:
                t4w.append(("pE", ev[("comb_c", prev_dc)]))
            P.step("vector", lambda e, dc=dc, pp=pp: e.tensor_mul(
                t4_sb[:, dc * 512:(dc + 1) * 512], sps[pp], sg_c[dc % 2]),
                wait=t4w, inc=[("vE", 1)])
            ev[("t4", t * 4 + dc)] = C["vE"]

            cbw = [("vE", ev[("t4", t * 4 + dc)]), ev[("xl", t)]]
            if t >= 1 and dc == 0:
                cbw.append(ev[("outs", t - 1)])
            P.step("gpsimd", lambda e, dc=dc: e.tensor_add(
                o_t[:, dc * 512:(dc + 1) * 512],
                t4_sb[:, dc * 512:(dc + 1) * 512],
                x_t[:, dc * 512:(dc + 1) * 512]),
                wait=cbw, inc=[("pE", 1)])
            ev[("comb_c", t * 4 + dc)] = C["pE"]
        ev[("comb", t)] = C["pE"]
        ev[("outs", t)] = P.dma("sync", out_d[t * TT:(t + 1) * TT, :], o_t,
                                wait=[("pE", ev[("comb", t)])])
        if t + 1 < NT:
            ev[("xl", t + 1)] = P.dma(
                "sync", x_t, x_d[(t + 1) * TT:(t + 2) * TT, :],
                wait=[("pE", ev[("comb", t)])])
            ev[("yl", t + 1)] = P.dma(
                "sync", y_t, y_s[(t + 1) * TT:(t + 2) * TT, :],
                wait=[("sE", ev[("sgc", t, 3)]), ev[("ys", (t + 1) // JT)]])

    ev[("xl", 0)] = P.dma("sync", x_t, x_d[0:TT, :])
    ev[("yl", 0)] = P.dma("sync", y_t, y_s[0:TT, :], wait=[ev[("ys", 0)]])

    for t in range(NT):
        bi = t % 2
        lw = [("tE", ev[("att", t - 2)])] if t >= 2 else []
        ev[("qtl0", t)] = P.dma("sync", qt_t[bi][:, 0:HALF], qt_s[t, 0], wait=lw)
        ev[("qtl", t)] = P.dma("sync", qt_t[bi][:, HALF:D], qt_s[t, 1], wait=lw)
        uw = [("pE", ev[("ug", t - 2)])] if t >= 2 else []
        ev[("ul0", t)] = P.dma("sync", u_t[bi][:, 0:HALF], u_s[t, 0], wait=uw)
        ev[("ul", t)] = P.dma("sync", u_t[bi][:, HALF:D], u_s[t, 1], wait=uw)

        aw = [ev[("qtl0", t)], ev[("qtl", t)]]
        if t > 0:
            aw.append(("vE", ev[("nev", t - 1)]))

        def att_mms(e, bi=bi):
            last = None
            for hh in range(NH):
                last = e.matmul(att_ps[:, hh * 128:(hh + 1) * 128],
                                qt_t[bi][:, hh * 128:(hh + 1) * 128],
                                kv_bf[:, hh * 128:(hh + 1) * 128],
                                start=True, stop=True)
            return last
        P.step("tensor", att_mms, wait=aw, inc=[("tE", 1)])
        ev[("att", t)] = C["tE"]
        marks[f"p3_att{t}"] = P.mark()

        ugw = [ev[("ul0", t)], ev[("ul", t)]]
        if t > 0:
            ugw.append(("vE", ev[("tmul", t - 1)]))
        P.step("gpsimd", lambda e, bi=bi: e.tensor_mul(ug_sb, u_t[bi], g_rep),
               wait=ugw)
        zbw = [("vE", ev[("z", t - 1)])] if t > 0 else []
        P.step("gpsimd", lambda e, bi=bi: e.tensor_mul(ub_sb, u_t[bi], b_rep),
               wait=zbw, inc=[("pE", 1)])
        ev[("ug", t)] = C["pE"]

        if t >= 1:
            emit_s_combine(t - 1)
            marks[f"p3_comb{t-1}"] = P.mark()
        sqw = [("tE", ev[("att", t)])]
        if t > 0:
            sqw.append(("vE", ev[("tmul", t - 1)]))
        P.step("scalar", lambda e: e.activation(
            n_sb, att_ps, AF.Identity, accum_out=sm3[:, 0:1]), wait=sqw)
        P.step("scalar", lambda e: e.activation(
            n_sb, att_ps, AF.Square, accum_out=sm3[:, 1:2]))
        ev[("asq", t)] = C["sE"]

        P.step("scalar", lambda e: e.activation(sm3[:, 7:8], sm3[:, 0:1], AF.Square))
        P.step("scalar", lambda e: e.activation(
            sm3[:, 4:5], sm3[:, 7:8], AF.Relu, scale=-1.0 / D, bias=sm3[:, 1:2]))
        P.step("scalar", lambda e: e.activation(
            sm3[:, 3:4], sm3[:, 4:5], AF.Ln, bias=ceps[:, 0:1]))
        P.step("scalar", lambda e: e.activation(
            sm3[:, 5:6], sm3[:, 3:4], AF.Exp, scale=-0.5, bias=ceps[:, 3:4]))
        P.step("scalar", lambda e: e.mul(sm3[:, 2:3], sm3[:, 0:1], 1.0 / D))
        ev[("l3e", t)] = C["sE"]
        marks[f"p3_nev{t}"] = P.mark()

        P.step("vector", lambda e: e.tensor_scalar(
            n_sb, att_ps, sm3[:, 2:3], sm3[:, 5:6], ALU.subtract, ALU.mult),
            wait=[("sE", ev[("l3e", t)])], inc=[("vE", 1)])
        ev[("nev", t)] = C["vE"]

        P.step("vector", lambda e: e.tensor_mul(t1_sb, n_sb, ug_sb),
               wait=[("pE", ev[("ug", t)])], inc=[("vE", 1)])
        ev[("tmul", t)] = C["vE"]
        zw = []
        if t > 0:
            zw.append(("tE", ev[("zpg", t - 1, 3)]))
        P.step("vector", lambda e: e.tensor_add(z_bf, t1_sb, ub_sb),
               wait=zw, inc=[("vE", 1)])
        ev[("z", t)] = C["vE"]
        marks[f"p3_t1b{t}"] = P.mark()

        zi = t % 2
        for g4 in range(4):
            zpw = [("vE", ev[("z", t)])]
            if ("zpe", g4 % 2) in ev:
                zpw.append(("vE", ev[("zpe", g4 % 2)]))

            def zpose(e, g4=g4):
                last = None
                for k in range(4):
                    c = g4 * 4 + k
                    last = e.transpose(
                        ZPB[g4 % 2][:, k * 128:(k + 1) * 128],
                        z_bf[:, c * 128:(c + 1) * 128], id_bf)
                return last
            P.step("tensor", zpose, wait=zpw, inc=[("tE", 1)])
            ev[("zpg", t, g4)] = C["tE"]
            zew = [("tE", ev[("zpg", t, g4)])]
            if g4 < 2 and t >= 2:
                zew.append(("tE", ev[("s_mm", t - 2, 3)]))
            P.step("vector", lambda e, g4=g4, zi=zi: e.tensor_copy(
                zt_sb[zi][:, g4 * 512:(g4 + 1) * 512],
                ZPB[g4 % 2][:, 0:512]),
                wait=zew, inc=[("vE", 1)])
            ev[("zpe", g4 % 2)] = C["vE"]
        ev[("zte", t)] = C["vE"]
        marks[f"p3_zte{t}"] = P.mark()

    emit_s_combine(NT - 1)

    P.step("sync", lambda e: e.nop(), wait=P.all_dma_events())

    with nc.Block() as block:
        P.emit(block, upto=None if upto is None else marks[upto])
    return nc


def kernel(L0_attn_x, L0_attn_Wqk, L0_attn_bqk, L0_attn_Wv, L0_attn_bv,
           L0_attn_Wu, L0_attn_bu, L0_attn_Wo, L0_attn_bo, L0_attn_ln_g,
           L0_attn_ln_b, L0_ffn_router_w, L0_ffn_norm_w, L0_ffn_gate_w,
           L0_ffn_up_w, L0_ffn_down_w):
    nc = build()
    shared = {
        "wqk": np.ascontiguousarray(L0_attn_Wqk),
        "bqk": np.ascontiguousarray(L0_attn_bqk),
        "wv": np.ascontiguousarray(L0_attn_Wv),
        "bv": np.ascontiguousarray(L0_attn_bv),
        "wu": np.ascontiguousarray(L0_attn_Wu),
        "bu": np.ascontiguousarray(L0_attn_bu),
        "wo": np.ascontiguousarray(L0_attn_Wo),
        "bo": np.ascontiguousarray(L0_attn_bo),
        "lng": np.ascontiguousarray(L0_attn_ln_g),
        "lnb": np.ascontiguousarray(L0_attn_ln_b),
        "rw": np.ascontiguousarray(L0_ffn_router_w),
        "nw": np.ascontiguousarray(L0_ffn_norm_w),
        "gw": np.ascontiguousarray(L0_ffn_gate_w),
        "uw": np.ascontiguousarray(L0_ffn_up_w),
        "dw": np.ascontiguousarray(L0_ffn_down_w),
    }
    in_maps = [dict(shared, x=np.ascontiguousarray(L0_attn_x[i]))
               for i in range(NCORES)]
    res = run_bass_kernel_spmd(nc, in_maps, core_ids=list(range(NCORES)))
    return np.stack([r["out"] for r in res.results], axis=0)


# version 11
# speedup vs baseline: 1.2245x; 1.0543x over previous
import math
import numpy as np
import concourse.bass as bass
import concourse.mybir as mybir
from concourse.bass_utils import run_bass_kernel_spmd

F32 = mybir.dt.float32
BF16 = mybir.dt.bfloat16
AF = mybir.ActivationFunctionType
ALU = mybir.AluOpType
AX = mybir.AxisListType

NCORES = 8
D = 2048
SEQ = 4096
TT = 128
NT = SEQ // TT
KC = 16
HALF = 1024
NH = 16
NE = 8
FE = 1024
FC = 8
GT = 512
NG = SEQ // GT
JT = GT // TT

LN_EPS = 1e-5
RMS_EPS = 1e-6
MOE_SCALE = 4.0

DEBUG_SERIAL = False
HALF_LN_D = 0.5 * math.log(float(2048))

SB_RIGHT_END = 229344
ARENA_BYTES = 178 * 1024 + 256
ARENA_RESERVE = 16 * 1024


class Planner:
    ENGINES = ("sync", "gpsimd", "vector", "scalar", "tensor")

    def __init__(self, nc):
        self.nc = nc
        self.items = {e: [] for e in self.ENGINES}
        self.sems = {}
        self.cnt = {}

    def sem(self, name):
        self.sems[name] = self.nc.alloc_semaphore(name)
        self.cnt[name] = 0

    SERIAL_SEM = {"vector": "vE", "scalar": "sE", "gpsimd": "pE"}

    def step(self, engine, fn, wait=(), inc=(), _dma=False):
        wait = list(wait)
        inc = list(inc)
        own = self.SERIAL_SEM.get(engine)
        if own is not None and not _dma:
            wait.append((own, self.cnt[own]))
            if not any(s == own for s, _ in inc):
                inc.append((own, 1))
        wait = [(s, v) for (s, v) in wait if v > 0]
        if DEBUG_SERIAL:
            wait = [(s, self.cnt[s]) for s in self.cnt if self.cnt[s] > 0]
        self.items[engine].append((fn, tuple(wait), tuple(inc)))
        for s, a in inc:
            self.cnt[s] += a

    DMA_LANES = {"sync": 8, "scalar": 4, "gpsimd": 6}

    def init_lanes(self):
        self.lanes = {}
        self.lane_rr = {}
        for eng, n in self.DMA_LANES.items():
            pfx = {"sync": "dS", "scalar": "dA", "gpsimd": "dG"}[eng]
            self.lanes[eng] = [f"{pfx}{i}" for i in range(n)]
            for s in self.lanes[eng]:
                self.sem(s)
            self.lane_rr[eng] = 0

    def dma(self, engine, out, in_, wait=(), **kw):
        lane = self.lanes[engine][self.lane_rr[engine] % len(self.lanes[engine])]
        self.lane_rr[engine] += 1
        wait = list(wait) + [(lane, self.cnt[lane])]
        self.step(engine, lambda e: e.dma_start(out=out, in_=in_, **kw),
                  wait=wait, inc=[(lane, 16)], _dma=True)
        return (lane, self.cnt[lane])

    def all_lanes(self, engine):
        return [(s, self.cnt[s]) for s in self.lanes[engine]]

    def all_dma_events(self):
        out = []
        for eng in self.lanes:
            out += self.all_lanes(eng)
        return out

    def barrier(self):
        eng_sem = {"sync": "bS", "gpsimd": "bG", "vector": "bV",
                   "scalar": "bA", "tensor": "bT"}
        for e in self.ENGINES:
            w = self.all_lanes(e) if (e in self.lanes and e != "gpsimd") else []
            if e in self.SERIAL_SEM:
                w.append((self.SERIAL_SEM[e], self.cnt[self.SERIAL_SEM[e]]))
            self.step(e, lambda eng: eng.nop(), wait=w, inc=[(eng_sem[e], 1)],
                      _dma=True)
        targets = [(s, self.cnt[s]) for s in eng_sem.values()]
        for e in self.ENGINES:
            self.step(e, lambda eng: eng.nop(), wait=targets, _dma=True)

    def mark(self):
        return {e: len(self.items[e]) for e in self.ENGINES}

    def emit(self, block, upto=None):
        def make(engine):
            def run(eng):
                last = {}
                items = self.items[engine]
                if upto is not None:
                    items = items[:upto[engine]]
                for fn, waits, incs in items:
                    for s, v in waits:
                        if last.get(s, 0) >= v:
                            continue
                        eng.wait_ge(self.sems[s], v)
                        last[s] = v
                    inst = fn(eng)
                    for s, a in incs:
                        inst = inst.then_inc(self.sems[s], a)
            return run
        block.sync(make("sync"))
        block.gpsimd(make("gpsimd"))
        block.vector(make("vector"))
        block.scalar(make("scalar"))
        block.tensor(make("tensor"))


class Arena:
    def __init__(self, nc, base, size):
        self.nc, self.base, self.off, self.n = nc, base, base, 0
        self.limit = base + size

    def alloc(self, name, shape, dtype):
        esz = 2 if dtype == BF16 else 4
        nbytes = shape[1] * esz
        off = (self.off + 127) // 128 * 128
        assert off + nbytes <= self.limit, (
            f"arena overflow: {name} needs {nbytes}B at +{off - self.base}")
        self.n += 1
        t = self.nc.alloc_sbuf_tensor_at(f"{name}_{self.n}", shape, dtype, offset=off)
        self.off = off + nbytes
        return t.ap()

    def reset(self):
        self.off = self.base


def build(upto=None, dbg=False):
    nc = bass.Bass("TRN2", target_bir_lowering=False, debug=False)

    x_d = nc.dram_tensor("x", [SEQ, D], F32, kind="ExternalInput").ap()
    wqk_d = nc.dram_tensor("wqk", [D, D], F32, kind="ExternalInput").ap()
    bqk_d = nc.dram_tensor("bqk", [D], F32, kind="ExternalInput").ap()
    wv_d = nc.dram_tensor("wv", [D, D], F32, kind="ExternalInput").ap()
    bv_d = nc.dram_tensor("bv", [D], F32, kind="ExternalInput").ap()
    wu_d = nc.dram_tensor("wu", [D, D], F32, kind="ExternalInput").ap()
    bu_d = nc.dram_tensor("bu", [D], F32, kind="ExternalInput").ap()
    wo_d = nc.dram_tensor("wo", [D, D], F32, kind="ExternalInput").ap()
    bo_d = nc.dram_tensor("bo", [D], F32, kind="ExternalInput").ap()
    lng_d = nc.dram_tensor("lng", [D], F32, kind="ExternalInput").ap()
    lnb_d = nc.dram_tensor("lnb", [D], F32, kind="ExternalInput").ap()
    rw_d = nc.dram_tensor("rw", [NE, D], F32, kind="ExternalInput").ap()
    nw_d = nc.dram_tensor("nw", [D], F32, kind="ExternalInput").ap()
    gw_d = nc.dram_tensor("gw", [NE, D, FE], F32, kind="ExternalInput").ap()
    uw_d = nc.dram_tensor("uw", [NE, D, FE], F32, kind="ExternalInput").ap()
    dw_d = nc.dram_tensor("dw", [NE, FE, D], F32, kind="ExternalInput").ap()
    out_d = nc.dram_tensor("out", [SEQ, D], F32, kind="ExternalOutput").ap()

    knd = "ExternalOutput" if dbg else "Internal"
    xnt_s = nc.dram_tensor("xnt_s", [KC, 128, SEQ], BF16, kind=knd).ap()
    wh1_s = nc.dram_tensor("wh1_s", [3, KC, 128, HALF], BF16, kind="Internal").ap()
    qt_s = nc.dram_tensor("qt_s", [NT, 2, 128, HALF], BF16, kind=knd).ap()
    u_s = nc.dram_tensor("u_s", [NT, 2, 128, HALF], BF16, kind=knd).ap()
    y_s = nc.dram_tensor("y_s", [SEQ, D], F32, kind=knd).ap()
    if dbg:
        dbg_cw = nc.dram_tensor("dbg_cw", [128, NT * 8], F32, kind="ExternalOutput").ap()
        dbg_kv = nc.dram_tensor("dbg_kv", [128, D], BF16, kind="ExternalOutput").ap()
        dbg_gr = nc.dram_tensor("dbg_gr", [128, 2 * D], F32, kind="ExternalOutput").ap()
    gbf = nc.dram_tensor("gbf", [NE, KC, 128, FE], BF16, kind="Internal").ap()
    ubf = nc.dram_tensor("ubf", [NE, KC, 128, FE], BF16, kind="Internal").ap()
    dbf = nc.dram_tensor("dbf", [NE, FC, 128, D], BF16, kind="Internal").ap()
    wobf = nc.dram_tensor("wobf", [KC, 128, D], BF16, kind="Internal").ap()

    def sb(name, shape, dtype):
        return nc.alloc_sbuf_tensor(name, shape, dtype).ap()

    id_bf = sb("id_bf", [128, 128], BF16)
    id_f32 = sb("id_f32", [128, 128], F32)
    iot = sb("iot", [128, 128], F32)
    ones_bf = sb("ones_bf", [1, 128], BF16)
    ones_f32 = sb("ones_f32", [1, 128], F32)
    bqk_r = sb("bqk_r", [1, D], BF16)
    bv_r = sb("bv_r", [1, D], BF16)
    bu_r = sb("bu_r", [1, D], BF16)
    bo_r = sb("bo_r", [1, D], BF16)
    nwT = sb("nwT", [128, 16], F32)
    g_rep = nc.alloc_sbuf_tensor_at(
        "g_rep", [128, D], F32, offset=SB_RIGHT_END - ARENA_RESERVE).ap()
    b_rep = nc.alloc_sbuf_tensor_at(
        "b_rep", [128, D], F32, offset=SB_RIGHT_END - ARENA_RESERVE + D * 4).ap()
    rwnT = sb("rwnT", [128, KC * 8], BF16)
    cw_res = sb("cw_res", [128, NT * 8], F32)
    ceps = sb("ceps", [128, 4], F32)
    kv_bf = sb("kv_bf", [128, D], BF16)
    lgf = sb("lgf", [128, 8], F32)
    sml = sb("sml", [128, 16], F32)
    cwt = sb("cwt", [128, 8], F32)

    PS = nc.alloc_psum_tensor("PS", [128, 4096], F32).ap()
    PS7_BF = PS[:, 3584:4096].bitcast(BF16)
    PS6_BF = PS[:, 3072:3584].bitcast(BF16)

    arena = Arena(nc, SB_RIGHT_END - ARENA_BYTES, ARENA_BYTES - ARENA_RESERVE)
    wq_sb = arena.alloc("wq", [128, KC * HALF], BF16)
    wv_sb = arena.alloc("wvs", [128, KC * HALF], BF16)
    wu_sb = arena.alloc("wus", [128, KC * HALF], BF16)

    P = Planner(nc)
    for s in ("tE", "vE", "sE", "pE", "bS", "bG", "bV", "bA", "bT"):
        P.sem(s)
    P.init_lanes()
    C = P.cnt
    ev = {}

    W_END = arena.off
    lng_row = arena.alloc("lng_row", [1, D], F32)
    lnb_row = arena.alloc("lnb_row", [1, D], F32)
    rw_sb = arena.alloc("rw_sb", [8, D], F32)
    nw16 = arena.alloc("nw16", [16, 128], F32)
    w_ready = {0: [
        P.dma("gpsimd", wsb.rearrange("p (c n) -> p c n", c=KC),
              wd[:, 0:HALF].rearrange("(c p) n -> p c n", p=128))
        for wd, wsb in ((wqk_d, wq_sb), (wv_d, wv_sb), (wu_d, wu_sb))]}
    conv_done = {}
    conv_done["wh1"] = [
        P.dma("gpsimd", wh1_s[i],
              wd[:, HALF:D].rearrange("(c p) n -> c p n", p=128))
        for i, wd in enumerate((wqk_d, wv_d, wu_d))]

    d_setup = [
        P.dma("sync", lng_row, lng_d.rearrange("(o d) -> o d", o=1)),
        P.dma("sync", lnb_row, lnb_d.rearrange("(o d) -> o d", o=1)),
        P.dma("sync", rw_sb, rw_d),
        P.dma("sync", nw16, nw_d.rearrange("(c p) -> c p", p=128)),
    ]
    b_rows = [
        P.dma("gpsimd", bqk_r, bqk_d.rearrange("(o d) -> o d", o=1)),
        P.dma("gpsimd", bv_r, bv_d.rearrange("(o d) -> o d", o=1)),
        P.dma("gpsimd", bu_r, bu_d.rearrange("(o d) -> o d", o=1)),
        P.dma("gpsimd", bo_r, bo_d.rearrange("(o d) -> o d", o=1)),
    ]
    P.step("gpsimd", lambda e: e.memset(ceps[:, 0:1], D * LN_EPS))
    P.step("gpsimd", lambda e: e.memset(ceps[:, 1:2], RMS_EPS))
    P.step("gpsimd", lambda e: e.memset(ceps[:, 2:3], D * LN_EPS * RMS_EPS))
    P.step("gpsimd", lambda e: e.memset(ceps[:, 3:4], HALF_LN_D))
    P.step("gpsimd", lambda e: e.memset(ones_bf, 1.0))
    P.step("gpsimd", lambda e: e.memset(ones_f32, 1.0))
    P.step("gpsimd", lambda e: e.iota(iot, [[1, 128]], channel_multiplier=-1,
                                      allow_small_or_imprecise_dtypes=True),
           inc=[("pE", 1)])
    pe_iota = C["pE"]
    P.step("vector", lambda e: e.tensor_single_scalar(id_bf, iot, 0.0, ALU.is_equal),
           wait=[("pE", pe_iota)])
    P.step("vector", lambda e: e.tensor_single_scalar(id_f32, iot, 0.0, ALU.is_equal),
           inc=[("vE", 1)])

    for ci in range(4):
        P.step("tensor",
               lambda e, ci=ci: (
                   e.matmul(PS[:, 0:512], ones_f32,
                            lng_row[:, ci * 512:(ci + 1) * 512], start=True, stop=True),
                   e.matmul(PS[:, 512:1024], ones_f32,
                            lnb_row[:, ci * 512:(ci + 1) * 512], start=True, stop=True))[-1],
               wait=d_setup + [("vE", C["vE"])], inc=[("tE", 1)])
        t_rep = C["tE"]
        P.step("vector", lambda e, ci=ci: e.tensor_copy(
            g_rep[:, ci * 512:(ci + 1) * 512], PS[:, 0:512]), wait=[("tE", t_rep)])
        P.step("vector", lambda e, ci=ci: e.tensor_copy(
            b_rep[:, ci * 512:(ci + 1) * 512], PS[:, 512:1024]),
            wait=[("tE", t_rep)], inc=[("vE", 1)])

    P.step("tensor", lambda e: e.transpose(PS[:, 1024:1040], nw16, id_f32[0:16, 0:16]),
           wait=d_setup + [("vE", C["vE"])], inc=[("tE", 1)])
    t_nw = C["tE"]
    P.step("vector", lambda e: e.tensor_copy(nwT, PS[:, 1024:1040]),
           wait=[("tE", t_nw)], inc=[("vE", 1)])
    v_nwT = C["vE"]
    for c in range(KC):
        P.step("tensor", lambda e, c=c: e.transpose(
            PS[:, 1536 + (c % 2) * 512: 1544 + (c % 2) * 512],
            rw_sb[0:8, c * 128:(c + 1) * 128], id_f32[0:8, 0:8]),
            wait=[("vE", C["vE"])], inc=[("tE", 1)])
        t_rw = C["tE"]
        P.step("vector", lambda e, c=c: e.tensor_scalar_mul(
            rwnT[:, c * 8:(c + 1) * 8],
            PS[:, 1536 + (c % 2) * 512: 1544 + (c % 2) * 512], nwT[:, c:c + 1]),
            wait=[("tE", t_rw), ("vE", v_nwT)], inc=[("vE", 1)])

    P.barrier()
    marks = {"setup": P.mark()}
    ARENA_BASE0 = arena.base
    arena.base = arena.off = ((W_END + 127) // 128 * 128)


    def convert_expert(e, wait=()):
        e1 = P.dma("gpsimd", gbf[e], gw_d[e].rearrange("(c p) f -> c p f", p=128),
                   wait=wait)
        e2 = P.dma("gpsimd", ubf[e], uw_d[e].rearrange("(c p) f -> c p f", p=128))
        e3 = P.dma("gpsimd", dbf[e], dw_d[e].rearrange("(c p) d -> c p d", p=128))
        conv_done[e] = [e1, e2, e3]

    x_buf = [arena.alloc(f"x{i}", [128, D], F32) for i in range(2)]
    xn_bf = [arena.alloc(f"xnbf{i}", [128, D], BF16) for i in range(2)]
    xnt_b = [arena.alloc(f"xnt{i}", [128, D], BF16) for i in range(2)]
    q_bf = [arena.alloc(f"qbf{i}", [128, HALF], BF16) for i in range(2)]
    v_bf = arena.alloc("vbf", [128, HALF], BF16)
    u_bf = [arena.alloc(f"ubfb{i}", [128, HALF], BF16) for i in range(2)]
    qt_b = [arena.alloc(f"qtb{i}", [128, HALF], BF16) for i in range(2)]
    el_m = arena.alloc("elm", [128, HALF], F32)
    el_e = arena.alloc("ele", [128, HALF], F32)
    el_r = arena.alloc("elr", [128, HALF], F32)
    kv_acc = arena.alloc("kvacc", [128, D], F32)

    q_ps = PS[:, 0:1024]
    v_ps = PS[:, 1024:2048]
    u_ps = PS[:, 2048:3072]
    log_ps = PS[:, 3072:3080]
    QPB = [PS[:, 1024:1536].bitcast(BF16), PS[:, 1536:2048].bitcast(BF16)]

    def load_w_half(h):
        evs = []
        if True:
            w = [("tE", ev[("mms", 0, NT - 1)])] + list(conv_done["wh1"])
            for i, wsb in enumerate((wq_sb, wv_sb, wu_sb)):
                evs.append(P.dma(
                    "sync", wsb.rearrange("p (c n) -> p c n", c=KC),
                    wh1_s[i].rearrange("c p n -> p c n"), wait=w))
        return evs


    for _e in range(3):
        convert_expert(_e)

    def emit_ln(t):
        bi = t % 2
        w = []
        if t >= 2:
            w = [("vE", ev[("xnbf", t - 2)]), ("sE", ev[("xsq", t - 2)])]
        ev[("x", t)] = P.dma("sync", x_buf[bi],
                             x_d[t * TT:(t + 1) * TT, :], wait=w)
        idw = [ev[("x", t)]]
        if t >= 2:
            idw.append(("tE", ev[("xpg", t - 2, 1)]))
        P.step("scalar", lambda e, bi=bi: e.activation(
            xn_bf[bi], x_buf[bi], AF.Identity, accum_out=sml[:, 0:1]),
            wait=idw)
        P.step("scalar", lambda e, bi=bi: e.activation(
            xn_bf[bi], x_buf[bi], AF.Square, accum_out=sml[:, 1:2]))
        ev[("xsq", t)] = C["sE"]

        lsw = [("sE", ev[("xsq", t)])]
        if ("lgfe", t - 2) in ev:
            lsw.append(("vE", ev[("lgfe", t - 2)]))
        if ("xnbf", t - 1) in ev:
            lsw.append(("vE", ev[("xnbf", t - 1)]))
        P.step("scalar", lambda e: e.activation(sml[:, 8:9], sml[:, 0:1], AF.Square),
               wait=lsw)
        P.step("scalar", lambda e: e.activation(
            sml[:, 4:5], sml[:, 8:9], AF.Relu, scale=-1.0 / D, bias=sml[:, 1:2]))
        P.step("scalar", lambda e: e.activation(
            sml[:, 7:8], sml[:, 4:5], AF.Ln, bias=ceps[:, 0:1]))
        P.step("scalar", lambda e: e.activation(
            sml[:, 5:6], sml[:, 7:8], AF.Exp, scale=-0.5, bias=ceps[:, 3:4]))
        P.step("scalar", lambda e: e.mul(sml[:, 2:3], sml[:, 0:1], 1.0 / D))
        P.step("scalar", lambda e: e.mul(sml[:, 9:10], sml[:, 7:8], 0.5))
        P.step("scalar", lambda e: e.activation(
            sml[:, 10:11], sml[:, 4:5], AF.Ln, scale=1.0 + RMS_EPS,
            bias=ceps[:, 2:3]))
        rrc = 11 if t % 2 == 0 else 3
        P.step("scalar", lambda e, rrc=rrc: e.activation(
            sml[:, rrc:rrc + 1], sml[:, 10:11], AF.Exp, scale=-0.5,
            bias=sml[:, 9:10]))
        ev[("lnexp", t)] = C["sE"]

    def emit_xnbf(t):
        bi = t % 2
        P.step("vector", lambda e, bi=bi: e.tensor_scalar(
            xn_bf[bi], x_buf[bi], sml[:, 2:3], sml[:, 5:6],
            ALU.subtract, ALU.mult),
            wait=[("sE", ev[("lnexp", t)])], inc=[("vE", 1)])
        ev[("xnbf", t)] = C["vE"]

    def emit_xpose(tn):
        bj = tn % 2
        xbanks = [PS7_BF, PS6_BF]
        for g2 in range(2):
            xw = [("vE", ev[("xnbf", tn)])]
            if ("xpfree", g2) in ev:
                xw.append(("vE", ev[("xpfree", g2)]))
            if g2 == 1 and ("lgfe", tn - 1) in ev:
                xw.append(("vE", ev[("lgfe", tn - 1)]))

            def xpose(e, g2=g2, bj=bj):
                last = None
                for k in range(8):
                    c = g2 * 8 + k
                    last = e.transpose(
                        xbanks[g2][:, k * 128:(k + 1) * 128],
                        xn_bf[bj][:, c * 128:(c + 1) * 128], id_bf)
                return last
            P.step("tensor", xpose, wait=xw, inc=[("tE", 1)])
            ev[("xpg", tn, g2)] = C["tE"]
            xew = [("tE", ev[("xpg", tn, g2)])]
            if g2 == 0 and tn >= 2:
                xew += [("tE", ev[("mms", 0, tn - 2)]),
                        ev[("xnts", tn - 2)]]
            P.step("vector", lambda e, g2=g2, bj=bj: e.tensor_copy(
                xnt_b[bj][:, g2 * 1024:(g2 + 1) * 1024],
                xbanks[g2][:, 0:1024]),
                wait=xew, inc=[("vE", 1)])
            ev[("xpfree", g2)] = C["vE"]
        ev[("xnt", tn)] = C["vE"]
        ev[("xnts", tn)] = P.dma(
            "sync",
            xnt_s[:, :, tn * TT:(tn + 1) * TT].rearrange("c p e -> p c e"),
            xnt_b[bj].rearrange("p (c e) -> p c e", c=KC),
            wait=[("vE", ev[("xnt", tn)])])

    for h in range(2):
        if h == 1:
            w_ready[1] = load_w_half(1)
            for e_i in range(3, NE):
                convert_expert(e_i)
            conv_done["wo"] = P.dma(
                "gpsimd", wobf, wo_d.rearrange("(c p) d -> c p d", p=128))

        if h == 0:
            emit_ln(0)
            emit_xnbf(0)
            emit_xpose(0)
        for t in range(NT):
            bi = t % 2
            if h == 1:
                if t >= 2:
                    w = [("tE", ev[("mms", 1, t - 2)])]
                else:
                    w = [("tE", ev[("mms", 0, NT - 1)]),
                         ev[("xnts", NT - 2 + t)]]
                ev[("xntl", t)] = P.dma(
                    "sync", xnt_b[bi].rearrange("p (c e) -> p c e", c=KC),
                    xnt_s[:, :, t * TT:(t + 1) * TT].rearrange("c p e -> p c e"),
                    wait=w)

            mmw = list(w_ready[h])
            if (h, t) == (0, 0):
                mmw += b_rows
            if h == 0:
                mmw.append(("vE", ev[("xnt", t)]))
            else:
                mmw.append(ev[("xntl", t)])
            if (h, t) != (0, 0):
                ph, pt = (h, t - 1) if t > 0 else (0, NT - 1)
                mmw += [("vE", ev[("elu_sub", ph, pt)]),
                        ("vE", ev[("qt", ph, pt)])]
                if ph == 0:
                    mmw.append(("vE", ev[("lgfe", pt)]))

            def wave_a(e, h=h, bi=bi):
                for ps_t, br in ((q_ps, bqk_r), (v_ps, bv_r)):
                    for s2 in range(2):
                        e.matmul(ps_t[:, s2 * 512:(s2 + 1) * 512], ones_bf,
                                 br[:, h * HALF + s2 * 512: h * HALF + (s2 + 1) * 512],
                                 start=True, stop=False)
                last = None
                for c in range(KC):
                    lhs = xnt_b[bi][:, c * 128:(c + 1) * 128]
                    for ps_t, wsb in ((q_ps, wq_sb), (v_ps, wv_sb)):
                        for s2 in range(2):
                            last = e.matmul(
                                ps_t[:, s2 * 512:(s2 + 1) * 512], lhs,
                                wsb[:, c * HALF + s2 * 512: c * HALF + (s2 + 1) * 512],
                                start=False, stop=(c == KC - 1))
                    if h == 0:
                        last = e.matmul(log_ps, lhs, rwnT[:, c * 8:(c + 1) * 8],
                                        start=(c == 0), stop=(c == KC - 1))
                return last
            P.step("tensor", wave_a, wait=mmw, inc=[("tE", 1)])
            ev[("mmsA", h, t)] = C["tE"]

            if h == 0 and t + 1 < NT:
                emit_ln(t + 1)
            mbw = []
            if (h, t) != (0, 0):
                ph, pt = (h, t - 1) if t > 0 else (0, NT - 1)
                mbw.append(("vE", ev[("kv_evac", ph, pt)]))

            def wave_b(e, h=h, bi=bi):
                for s2 in range(2):
                    e.matmul(u_ps[:, s2 * 512:(s2 + 1) * 512], ones_bf,
                             bu_r[:, h * HALF + s2 * 512: h * HALF + (s2 + 1) * 512],
                             start=True, stop=False)
                last = None
                for c in range(KC):
                    lhs = xnt_b[bi][:, c * 128:(c + 1) * 128]
                    for s2 in range(2):
                        last = e.matmul(
                            u_ps[:, s2 * 512:(s2 + 1) * 512], lhs,
                            wu_sb[:, c * HALF + s2 * 512: c * HALF + (s2 + 1) * 512],
                            start=False, stop=(c == KC - 1))
                return last
            P.step("tensor", wave_b, wait=mbw, inc=[("tE", 1)])
            ev[("mms", h, t)] = C["tE"]

            P.step("vector", lambda e: e.tensor_scalar_min(el_m, q_ps, 0.0),
                   wait=[("tE", ev[("mmsA", h, t)])], inc=[("vE", 1)])
            ev[("elu_m", h, t)] = C["vE"]
            P.step("vector", lambda e: e.tensor_sub(el_r, q_ps, el_m),
                   inc=[("vE", 1)])
            ev[("elu_sub", h, t)] = C["vE"]
            P.step("scalar", lambda e: e.activation(el_e, el_m, AF.Exp),
                   wait=[("vE", ev[("elu_m", h, t)])], inc=[("sE", 1)])
            ev[("elu_e", h, t)] = C["sE"]
            silw = [("tE", ev[("mmsA", h, t)])]
            if (h, t) != (0, 0):
                ph, pt = (h, t - 1) if t > 0 else (0, NT - 1)
                silw.append(("tE", ev[("kv", ph, pt)]))
            P.step("scalar", lambda e: e.activation(v_bf, v_ps, AF.Silu), wait=silw)
            ev[("silu_v", h, t)] = C["sE"]
            suw = [("tE", ev[("mms", h, t)])]
            if t >= 2:
                suw.append(ev[("us", h, t - 2)])
            elif h == 1:
                suw.append(ev[("us", 0, NT - 2 + t)])
            P.step("scalar", lambda e, bi=bi: e.activation(u_bf[bi], u_ps, AF.Silu),
                   wait=suw)
            ev[("silu", h, t)] = C["sE"]
            ev[("us", h, t)] = P.dma("sync", u_s[t, h], u_bf[bi],
                                     wait=[("sE", ev[("silu", h, t)])])

            if h == 0:
                rrc = 11 if t % 2 == 0 else 3
                P.step("vector", lambda e, rrc=rrc: e.tensor_scalar_mul(
                    lgf, log_ps, sml[:, rrc:rrc + 1]),
                    wait=[("tE", ev[("mmsA", 0, t)]), ("sE", ev[("lnexp", t)])],
                    inc=[("vE", 1)])
                ev[("lgfe", t)] = C["vE"]

            if h == 0 and t + 1 < NT:
                emit_xnbf(t + 1)
                emit_xpose(t + 1)
            P.step("vector", lambda e, bi=bi: e.scalar_tensor_tensor(
                q_bf[bi], el_e, -1.0, el_r, ALU.add, ALU.add),
                wait=[("sE", ev[("elu_e", h, t)])], inc=[("vE", 1)])
            ev[("elu", h, t)] = C["vE"]

            qxw = [("vE", ev[("elu", h, t)]), ("sE", ev[("silu_v", h, t)])]
            if ("qtprev",) in ev:
                qxw.append(("vE", ev[("qtprev",)]))

            def qpose(e, bi=bi):
                last = None
                for k in range(8):
                    last = e.transpose(
                        QPB[0][:, k * 128:(k + 1) * 128],
                        q_bf[bi][:, k * 128:(k + 1) * 128], id_bf)
                return last
            P.step("tensor", qpose, wait=qxw, inc=[("tE", 1)])
            ev[("qpg", h, t)] = C["tE"]
            qsw = [("tE", ev[("qpg", h, t)])]
            if t >= 2:
                qsw.append(ev[("qts", h, t - 2)])
            elif h == 1:
                qsw.append(ev[("qts", 0, NT - 2 + t)])
            P.step("vector", lambda e, bi=bi: e.tensor_copy(
                qt_b[bi], QPB[0][:, 0:1024]),
                wait=qsw, inc=[("vE", 1)])
            ev[("qtprev",)] = C["vE"]
            ev[("qt", h, t)] = C["vE"]
            ev[("qts", h, t)] = P.dma("sync", qt_s[t, h], qt_b[bi],
                                      wait=[("vE", ev[("qt", h, t)])])

            def kv_mms(e, bi=bi):
                last = None
                for i in range(8):
                    last = e.matmul(u_ps[:, i * 128:(i + 1) * 128],
                                    q_bf[bi][:, i * 128:(i + 1) * 128],
                                    v_bf[:, i * 128:(i + 1) * 128],
                                    start=True, stop=True)
                return last
            P.step("tensor", kv_mms,
                   wait=[("vE", ev[("elu", h, t)]), ("sE", ev[("silu", h, t)])],
                   inc=[("tE", 1)])
            ev[("kv", h, t)] = C["tE"]
            if t == 0:
                P.step("vector", lambda e, h=h: e.tensor_copy(
                    kv_acc[:, h * HALF:(h + 1) * HALF], u_ps),
                    wait=[("tE", ev[("kv", h, t)])], inc=[("vE", 1)])
            else:
                P.step("vector", lambda e, h=h: e.tensor_add(
                    kv_acc[:, h * HALF:(h + 1) * HALF], u_ps,
                    kv_acc[:, h * HALF:(h + 1) * HALF]),
                    wait=[("tE", ev[("kv", h, t)])], inc=[("vE", 1)])
            ev[("kv_evac", h, t)] = C["vE"]

            if h == 0:
                P.step("vector", lambda e: e.reduce_max(
                    sml[:, 12:13], lgf, axis=AX.X))
                P.step("vector", lambda e: e.reduce_max(
                    sml[:, 13:14], lgf, axis=AX.X, negate=True))
                P.step("vector", lambda e: e.tensor_single_scalar(
                    cwt, lgf, sml[:, 12:13], ALU.is_ge))
                P.step("vector", lambda e: e.scalar_tensor_tensor(
                    cwt, cwt, 1e30, lgf, ALU.mult, ALU.subtract))
                P.step("vector", lambda e: e.tensor_reduce(
                    sml[:, 14:15], cwt, axis=AX.X, op=ALU.min, negate=True))
                ev[("cwv1", t)] = C["vE"]

                P.step("scalar", lambda e: e.activation(
                    cwt, lgf, AF.Exp, bias=sml[:, 13:14]),
                    wait=[("vE", ev[("cwv1", t)])])
                P.step("scalar", lambda e: e.activation(
                    sml[:, 15:16], sml[:, 14:15], AF.Exp, bias=sml[:, 13:14]))
                ev[("cws", t)] = C["sE"]

                P.step("vector", lambda e: e.tensor_scalar_add(
                    sml[:, 15:16], sml[:, 15:16], 1.0),
                    wait=[("sE", ev[("cws", t)])])
                P.step("vector", lambda e: e.reciprocal(
                    sml[:, 15:16], sml[:, 15:16]))
                P.step("vector", lambda e: e.tensor_scalar_mul(
                    sml[:, 15:16], sml[:, 15:16], MOE_SCALE))
                P.step("vector", lambda e: e.tensor_single_scalar(
                    lgf, lgf, sml[:, 14:15], ALU.is_ge))
                P.step("vector", lambda e: e.tensor_mul(cwt, cwt, lgf))
                P.step("vector", lambda e, t=t: e.tensor_scalar_mul(
                    cw_res[:, t * 8:(t + 1) * 8], cwt, sml[:, 15:16]))
                ev[("cw", t)] = C["vE"]

    P.step("vector", lambda e: e.tensor_copy(kv_bf, kv_acc),
           wait=[("vE", ev[("kv_evac", 1, NT - 1)])], inc=[("vE", 1)])

    P.barrier()
    marks["p1"] = P.mark()
    arena.base = ARENA_BASE0
    arena.reset()

    xng = arena.alloc("xng", [128, KC * GT], BF16)
    y_g = arena.alloc("yg", [128, JT * D], F32)
    gate_b = [arena.alloc(f"gateb{i}", [128, KC * 256], BF16) for i in range(2)]
    up_b = [arena.alloc(f"upb{i}", [128, KC * 256], BF16) for i in range(2)]
    down_b = [arena.alloc(f"downb{i}", [128, FC * D], BF16) for i in range(2)]
    at_b = arena.alloc("atb", [128, FC * GT], BF16)
    sg_b = arena.alloc("sgb", [128, GT], F32)

    gps = [PS[:, 0:512], PS[:, 512:1024]]
    ups = [PS[:, 1024:1536], PS[:, 1536:2048]]
    yp2 = [PS[:, 2048:3072], PS[:, 3072:4096]]

    def load_gu_quarter(g, e_i, qq, wait):
        qbi = qq % 2
        qw = list(conv_done[e_i]) + list(wait)
        ev[("gql", g, e_i, qq)] = P.dma(
            "sync", gate_b[qbi].rearrange("p (c f) -> p c f", c=KC),
            gbf[e_i][:, :, qq * 256:(qq + 1) * 256].rearrange("c p f -> p c f"),
            wait=qw)
        ev[("uql", g, e_i, qq)] = P.dma(
            "sync", up_b[qbi].rearrange("p (c f) -> p c f", c=KC),
            ubf[e_i][:, :, qq * 256:(qq + 1) * 256].rearrange("c p f -> p c f"),
            wait=qw)

    load_gu_quarter(0, 0, 0, [])
    load_gu_quarter(0, 0, 1, [])

    for g in range(NG):
        xw = [("tE", ev[("ffn_mm", g - 1)])] if g > 0 else []
        ev[("xng", g)] = P.dma(
            "sync", xng.rearrange("p (c e) -> p c e", c=KC),
            xnt_s[:, :, g * GT:(g + 1) * GT].rearrange("c p e -> p c e"), wait=xw)

        for e_i in range(NE):
            dbi = e_i % 2
            dw_w = list(conv_done[e_i])
            gi = g * NE + e_i
            if gi >= 2:
                pg, pe = divmod(gi - 2, NE)
                dw_w.append(("tE", ev[("down_mm", pg, pe)]))
            ev[("downl", g, e_i)] = P.dma(
                "sync", down_b[dbi].rearrange("p (c d) -> p c d", c=FC),
                dbf[e_i].rearrange("c p d -> p c d"), wait=dw_w)

            for fc in range(FC):
                qq = fc // 2
                qbi = qq % 2
                fo = (fc % 2) * 128
                pp = fc % 2
                fi = gi * FC + fc
                mmw = [ev[("gql", g, e_i, qq)], ev[("uql", g, e_i, qq)],
                       ev[("xng", g)]]
                if fi >= 2:
                    pg2, pe2f = divmod(fi - 2, FC * NE)
                    pe2, pfc = divmod(pe2f, FC)
                    mmw += [("sE", ev[("silu2", pg2, pe2, pfc)]),
                            ("vE", ev[("at", pg2, pe2, pfc)])]

                def gu_mms(e, qbi=qbi, fo=fo, pp=pp):
                    last = None
                    for c in range(KC):
                        last = e.matmul(gps[pp],
                                        gate_b[qbi][:, c * 256 + fo: c * 256 + fo + 128],
                                        xng[:, c * GT:(c + 1) * GT],
                                        start=(c == 0), stop=(c == KC - 1))
                    for c in range(KC):
                        last = e.matmul(ups[pp],
                                        up_b[qbi][:, c * 256 + fo: c * 256 + fo + 128],
                                        xng[:, c * GT:(c + 1) * GT],
                                        start=(c == 0), stop=(c == KC - 1))
                    return last
                P.step("tensor", gu_mms, wait=mmw, inc=[("tE", 1)])
                ev[("guq_mm", g, e_i, qq)] = C["tE"]
                ev[("gu_mm", g, e_i, fc)] = C["tE"]

                s2w = [("tE", ev[("gu_mm", g, e_i, fc)])]
                if fi >= 1:
                    pg2, pe2f = divmod(fi - 1, FC * NE)
                    pe2, pfc = divmod(pe2f, FC)
                    s2w.append(("vE", ev[("at", pg2, pe2, pfc)]))
                P.step("scalar", lambda e, pp=pp: e.activation(sg_b, gps[pp], AF.Silu),
                       wait=s2w, inc=[("sE", 1)])
                ev[("silu2", g, e_i, fc)] = C["sE"]
                atw = [("sE", ev[("silu2", g, e_i, fc)])]
                if e_i == 0 and fc == 0 and g > 0:
                    atw.append(ev[("ys", g - 1)])
                P.step("vector", lambda e, fc=fc, pp=pp: e.tensor_mul(
                    at_b[:, fc * GT:(fc + 1) * GT], sg_b, ups[pp]),
                    wait=atw, inc=[("vE", 1)])
                ev[("at", g, e_i, fc)] = C["vE"]

                if fc in (1, 3):
                    load_gu_quarter(g, e_i, fc // 2 + 2,
                                    [("tE", ev[("guq_mm", g, e_i, fc // 2)])])
                elif fc in (5, 7):
                    ng_, ne_ = (g, e_i + 1) if e_i + 1 < NE else (g + 1, 0)
                    if ng_ < NG:
                        load_gu_quarter(ng_, ne_, fc // 2 - 2,
                                        [("tE", ev[("guq_mm", g, e_i, fc // 2)])])

            for j in range(JT):
                for dh in range(2):
                    jdh = j * 2 + dh
                    pp = jdh % 2
                    dmw = [ev[("downl", g, e_i)],
                           ("vE", ev[("at", g, e_i, FC - 1)])]
                    prev = gi * 8 + jdh - 2
                    if prev >= 0:
                        dmw.append(("vE", ev[("yup", prev)]))

                    def down_mms(e, j=j, dh=dh, pp=pp, dbi=dbi):
                        last = None
                        for fcc in range(FC):
                            lhs = at_b[:, fcc * GT + j * 128: fcc * GT + (j + 1) * 128]
                            for q in range(2):
                                dc = dh * 2 + q
                                last = e.matmul(
                                    yp2[pp][:, q * 512:(q + 1) * 512], lhs,
                                    down_b[dbi][:, fcc * D + dc * 512:
                                                fcc * D + (dc + 1) * 512],
                                    start=(fcc == 0), stop=(fcc == FC - 1))
                        return last
                    P.step("tensor", down_mms, wait=dmw, inc=[("tE", 1)])
                    ev[("down_mm", g, e_i)] = C["tE"]
                    t_dmm = C["tE"]

                    cw_col = cw_res[:, (g * JT + j) * 8 + e_i:
                                    (g * JT + j) * 8 + e_i + 1]
                    for q in range(2):
                        dc = dh * 2 + q
                        ysl = y_g[:, j * D + dc * 512: j * D + (dc + 1) * 512]
                        yw = [("tE", t_dmm)]
                        if e_i == 0 and g > 0 and jdh == 0 and q == 0:
                            yw.append(ev[("ys", g - 1)])
                        if e_i == 0:
                            P.step("vector",
                                   lambda e, pp=pp, q=q, ysl=ysl, cw_col=cw_col:
                                   e.tensor_scalar_mul(
                                       ysl, yp2[pp][:, q * 512:(q + 1) * 512],
                                       cw_col),
                                   wait=yw, inc=[("vE", 1)])
                        else:
                            P.step("vector",
                                   lambda e, pp=pp, q=q, ysl=ysl, cw_col=cw_col:
                                   e.scalar_tensor_tensor(
                                       ysl, yp2[pp][:, q * 512:(q + 1) * 512],
                                       cw_col, ysl, ALU.mult, ALU.add),
                                   wait=yw, inc=[("vE", 1)])
                    ev[("yup", gi * 8 + jdh)] = C["vE"]
        ev[("ffn_mm", g)] = C["tE"]

        ev[("ys", g)] = P.dma(
            "sync",
            y_s[g * GT:(g + 1) * GT, :].rearrange("(j p) d -> p j d", p=128),
            y_g.rearrange("p (j d) -> p j d", j=JT),
            wait=[("vE", ev[("yup", (g * NE + NE - 1) * 8 + 7)])])

    P.barrier()
    marks["p2"] = P.mark()
    arena.reset()

    wo_sb = arena.alloc("wosb", [128, KC * D], BF16)
    qt_t = [arena.alloc(f"qtt{i}", [128, D], BF16) for i in range(2)]
    u_t = [arena.alloc(f"ut{i}", [128, D], BF16) for i in range(2)]
    x_t = arena.alloc("xt0", [128, D], F32)
    y_t = arena.alloc("yt0", [128, D], F32)
    o_t = arena.alloc("ot0", [128, D], F32)
    n_sb = arena.alloc("nsb", [128, D], F32)
    t1_sb = arena.alloc("t1sb", [128, D], F32)
    t4_sb = arena.alloc("t4sb", [128, D], F32)
    ug_sb = arena.alloc("ugsb", [128, D], F32)
    ub_sb = arena.alloc("ubsb", [128, D], F32)
    z_bf = arena.alloc("zbf", [128, D], BF16)
    zt_sb = [arena.alloc(f"ztsb{i}", [128, D], BF16) for i in range(2)]
    sg_c = [arena.alloc(f"sgc{i}", [128, 512], F32) for i in range(2)]
    sm3 = arena.alloc("sm3", [128, 8], F32)

    att_ps = PS[:, 0:2048]
    sps = [PS[:, 2048:2560], PS[:, 2560:3072]]
    ZPB = [PS6_BF, PS7_BF]

    ev[("wol",)] = P.dma("sync", wo_sb.rearrange("p (c d) -> p c d", c=KC),
                         wobf.rearrange("c p d -> p c d"),
                         wait=[conv_done["wo"]])

    def emit_s_combine(t):
        zi = t % 2
        sgw = [ev[("yl", t)]]
        if t > 0:
            sgw.append(("vE", ev[("t4", (t - 1) * 4 + 3)]))
        for dc in range(4):
            pp = dc % 2
            smw = [("vE", ev[("zte", t)]), ev[("wol",)]]
            prev_dc = t * 4 + dc - 2
            if prev_dc >= 0:
                smw.append(("vE", ev[("t4", prev_dc)]))

            def s_mms(e, dc=dc, pp=pp, zi=zi):
                e.matmul(sps[pp], ones_bf, bo_r[:, dc * 512:(dc + 1) * 512],
                         start=True, stop=False)
                last = None
                for c in range(KC):
                    last = e.matmul(sps[pp], zt_sb[zi][:, c * 128:(c + 1) * 128],
                                    wo_sb[:, c * D + dc * 512: c * D + (dc + 1) * 512],
                                    start=False, stop=(c == KC - 1))
                return last
            P.step("tensor", s_mms, wait=smw, inc=[("tE", 1)])
            ev[("s_mm", t, dc)] = C["tE"]

            if dc == 0:
                scw = list(sgw)
            elif dc == 1:
                scw = [("vE", ev[("t4", (t - 1) * 4 + 3)])] if t > 0 else []
            else:
                scw = [("vE", ev[("t4", t * 4 + dc - 2)])]
            P.step("scalar", lambda e, dc=dc: e.activation(
                sg_c[dc % 2], y_t[:, dc * 512:(dc + 1) * 512], AF.Silu),
                wait=scw)
            ev[("sgc", t, dc)] = C["sE"]

            t4w = [("tE", ev[("s_mm", t, dc)]), ("sE", ev[("sgc", t, dc)])]
            if prev_dc >= 0:
                t4w.append(("pE", ev[("comb_c", prev_dc)]))
            P.step("vector", lambda e, dc=dc, pp=pp: e.tensor_mul(
                t4_sb[:, dc * 512:(dc + 1) * 512], sps[pp], sg_c[dc % 2]),
                wait=t4w, inc=[("vE", 1)])
            ev[("t4", t * 4 + dc)] = C["vE"]

            cbw = [("vE", ev[("t4", t * 4 + dc)]), ev[("xl", t)]]
            if t >= 1 and dc == 0:
                cbw.append(ev[("outs", t - 1)])
            P.step("gpsimd", lambda e, dc=dc: e.tensor_add(
                o_t[:, dc * 512:(dc + 1) * 512],
                t4_sb[:, dc * 512:(dc + 1) * 512],
                x_t[:, dc * 512:(dc + 1) * 512]),
                wait=cbw, inc=[("pE", 1)])
            ev[("comb_c", t * 4 + dc)] = C["pE"]
        ev[("comb", t)] = C["pE"]
        ev[("outs", t)] = P.dma("sync", out_d[t * TT:(t + 1) * TT, :], o_t,
                                wait=[("pE", ev[("comb", t)])])
        if t + 1 < NT:
            ev[("xl", t + 1)] = P.dma(
                "sync", x_t, x_d[(t + 1) * TT:(t + 2) * TT, :],
                wait=[("pE", ev[("comb", t)])])
            ev[("yl", t + 1)] = P.dma(
                "sync", y_t, y_s[(t + 1) * TT:(t + 2) * TT, :],
                wait=[("sE", ev[("sgc", t, 3)]), ev[("ys", (t + 1) // JT)]])

    ev[("xl", 0)] = P.dma("sync", x_t, x_d[0:TT, :])
    ev[("yl", 0)] = P.dma("sync", y_t, y_s[0:TT, :], wait=[ev[("ys", 0)]])

    for t in range(NT):
        bi = t % 2
        lw = [("tE", ev[("att", t - 2)])] if t >= 2 else []
        ev[("qtl0", t)] = P.dma("sync", qt_t[bi][:, 0:HALF], qt_s[t, 0], wait=lw)
        ev[("qtl", t)] = P.dma("sync", qt_t[bi][:, HALF:D], qt_s[t, 1], wait=lw)
        uw = [("pE", ev[("ug", t - 2)])] if t >= 2 else []
        ev[("ul0", t)] = P.dma("sync", u_t[bi][:, 0:HALF], u_s[t, 0], wait=uw)
        ev[("ul", t)] = P.dma("sync", u_t[bi][:, HALF:D], u_s[t, 1], wait=uw)

        aw = [ev[("qtl0", t)], ev[("qtl", t)]]
        if t > 0:
            aw.append(("vE", ev[("nev", t - 1)]))

        def att_mms(e, bi=bi):
            last = None
            for hh in range(NH):
                last = e.matmul(att_ps[:, hh * 128:(hh + 1) * 128],
                                qt_t[bi][:, hh * 128:(hh + 1) * 128],
                                kv_bf[:, hh * 128:(hh + 1) * 128],
                                start=True, stop=True)
            return last
        P.step("tensor", att_mms, wait=aw, inc=[("tE", 1)])
        ev[("att", t)] = C["tE"]
        marks[f"p3_att{t}"] = P.mark()

        ugw = [ev[("ul0", t)], ev[("ul", t)]]
        if t > 0:
            ugw.append(("vE", ev[("tmul", t - 1)]))
        P.step("gpsimd", lambda e, bi=bi: e.tensor_mul(ug_sb, u_t[bi], g_rep),
               wait=ugw)
        zbw = [("vE", ev[("z", t - 1)])] if t > 0 else []
        P.step("gpsimd", lambda e, bi=bi: e.tensor_mul(ub_sb, u_t[bi], b_rep),
               wait=zbw, inc=[("pE", 1)])
        ev[("ug", t)] = C["pE"]

        sqw = [("tE", ev[("att", t)])]
        if t > 0:
            sqw.append(("vE", ev[("tmul", t - 1)]))
        P.step("scalar", lambda e: e.activation(
            n_sb, att_ps, AF.Identity, accum_out=sm3[:, 0:1]), wait=sqw)
        P.step("scalar", lambda e: e.activation(
            n_sb, att_ps, AF.Square, accum_out=sm3[:, 1:2]))
        ev[("asq", t)] = C["sE"]

        P.step("scalar", lambda e: e.activation(sm3[:, 7:8], sm3[:, 0:1], AF.Square))
        P.step("scalar", lambda e: e.activation(
            sm3[:, 4:5], sm3[:, 7:8], AF.Relu, scale=-1.0 / D, bias=sm3[:, 1:2]))
        P.step("scalar", lambda e: e.activation(
            sm3[:, 3:4], sm3[:, 4:5], AF.Ln, bias=ceps[:, 0:1]))
        P.step("scalar", lambda e: e.activation(
            sm3[:, 5:6], sm3[:, 3:4], AF.Exp, scale=-0.5, bias=ceps[:, 3:4]))
        P.step("scalar", lambda e: e.mul(sm3[:, 2:3], sm3[:, 0:1], 1.0 / D))
        ev[("l3e", t)] = C["sE"]
        marks[f"p3_nev{t}"] = P.mark()

        P.step("vector", lambda e: e.tensor_scalar(
            n_sb, att_ps, sm3[:, 2:3], sm3[:, 5:6], ALU.subtract, ALU.mult),
            wait=[("sE", ev[("l3e", t)])], inc=[("vE", 1)])
        ev[("nev", t)] = C["vE"]

        if t >= 1:
            emit_s_combine(t - 1)
            marks[f"p3_comb{t-1}"] = P.mark()
        P.step("vector", lambda e: e.tensor_mul(t1_sb, n_sb, ug_sb),
               wait=[("pE", ev[("ug", t)])], inc=[("vE", 1)])
        ev[("tmul", t)] = C["vE"]
        zw = []
        if t > 0:
            zw.append(("tE", ev[("zpg", t - 1, 3)]))
        P.step("vector", lambda e: e.tensor_add(z_bf, t1_sb, ub_sb),
               wait=zw, inc=[("vE", 1)])
        ev[("z", t)] = C["vE"]
        marks[f"p3_t1b{t}"] = P.mark()

        zi = t % 2
        for g4 in range(4):
            zpw = [("vE", ev[("z", t)])]
            if ("zpe", g4 % 2) in ev:
                zpw.append(("vE", ev[("zpe", g4 % 2)]))

            def zpose(e, g4=g4):
                last = None
                for k in range(4):
                    c = g4 * 4 + k
                    last = e.transpose(
                        ZPB[g4 % 2][:, k * 128:(k + 1) * 128],
                        z_bf[:, c * 128:(c + 1) * 128], id_bf)
                return last
            P.step("tensor", zpose, wait=zpw, inc=[("tE", 1)])
            ev[("zpg", t, g4)] = C["tE"]
            zew = [("tE", ev[("zpg", t, g4)])]
            if g4 < 2 and t >= 2:
                zew.append(("tE", ev[("s_mm", t - 2, 3)]))
            P.step("vector", lambda e, g4=g4, zi=zi: e.tensor_copy(
                zt_sb[zi][:, g4 * 512:(g4 + 1) * 512],
                ZPB[g4 % 2][:, 0:512]),
                wait=zew, inc=[("vE", 1)])
            ev[("zpe", g4 % 2)] = C["vE"]
        ev[("zte", t)] = C["vE"]
        marks[f"p3_zte{t}"] = P.mark()

    emit_s_combine(NT - 1)

    P.step("sync", lambda e: e.nop(), wait=P.all_dma_events())

    with nc.Block() as block:
        P.emit(block, upto=None if upto is None else marks[upto])
    return nc


def kernel(L0_attn_x, L0_attn_Wqk, L0_attn_bqk, L0_attn_Wv, L0_attn_bv,
           L0_attn_Wu, L0_attn_bu, L0_attn_Wo, L0_attn_bo, L0_attn_ln_g,
           L0_attn_ln_b, L0_ffn_router_w, L0_ffn_norm_w, L0_ffn_gate_w,
           L0_ffn_up_w, L0_ffn_down_w):
    nc = build()
    shared = {
        "wqk": np.ascontiguousarray(L0_attn_Wqk),
        "bqk": np.ascontiguousarray(L0_attn_bqk),
        "wv": np.ascontiguousarray(L0_attn_Wv),
        "bv": np.ascontiguousarray(L0_attn_bv),
        "wu": np.ascontiguousarray(L0_attn_Wu),
        "bu": np.ascontiguousarray(L0_attn_bu),
        "wo": np.ascontiguousarray(L0_attn_Wo),
        "bo": np.ascontiguousarray(L0_attn_bo),
        "lng": np.ascontiguousarray(L0_attn_ln_g),
        "lnb": np.ascontiguousarray(L0_attn_ln_b),
        "rw": np.ascontiguousarray(L0_ffn_router_w),
        "nw": np.ascontiguousarray(L0_ffn_norm_w),
        "gw": np.ascontiguousarray(L0_ffn_gate_w),
        "uw": np.ascontiguousarray(L0_ffn_up_w),
        "dw": np.ascontiguousarray(L0_ffn_down_w),
    }
    in_maps = [dict(shared, x=np.ascontiguousarray(L0_attn_x[i]))
               for i in range(NCORES)]
    res = run_bass_kernel_spmd(nc, in_maps, core_ids=list(range(NCORES)))
    return np.stack([r["out"] for r in res.results], axis=0)
